# version 06
# baseline (speedup 1.0000x reference)
import numpy as np
from contextlib import ExitStack

import concourse.bass as bass
import concourse.mybir as mybir
from concourse.bass_utils import run_bass_kernel_spmd

F32 = mybir.dt.float32
BF16 = mybir.dt.bfloat16
AF = mybir.ActivationFunctionType
ALU = mybir.AluOpType
AX = mybir.AxisListType

NCORES = 8
S = 8192
D = 2048
TLOC = 1024
NT = 8
NEG = -1e30
SCL_A = 0.08838834764831845
SCL_B = 0.17677669529663687
EPS_LN = 1e-5
EPS_RMS = 1e-8
LAM_INIT = 0.2
TAU = 0.05
NEXP = 32
INTER = 768

BUILD_SUBLAYERS = 4

ENGS = ("sync", "vector", "scalar", "tensor", "gpsimd")



class _Prod:
    __slots__ = ("name", "sem", "cnt", "idx", "step", "engine", "is_cc")

    def __init__(self, name, sem, idx, step, engine=None):
        self.name = name
        self.sem = sem
        self.cnt = 0
        self.idx = idx
        self.step = step
        self.engine = engine
        self.is_cc = False


class Buf:
    __slots__ = ("name", "w", "readers", "dmap")

    def __init__(self, name=""):
        self.name = name
        self.w = None
        self.readers = {}
        self.dmap = None


class MK:
    def __init__(self, nc, es):
        self.nc = nc
        self.es = es
        self.seq = {e: [] for e in ENGS}
        self.prods = []
        self.eng = {}
        for e in ENGS:
            sem = es.enter_context(nc.semaphore(f"sem_{e}"))
            p = _Prod(e, sem, len(self.prods), 1, engine=e)
            self.prods.append(p)
            self.eng[e] = p
        self.waited = {e: {} for e in ENGS}
        self._nq = 0

    class _Queue:
        __slots__ = ("engine",)
        def __init__(self, engine):
            self.engine = engine

    def new_q(self, name, engine):
        return MK._Queue(engine)

    def _waits_for(self, ename, reads, writes):
        need = {}

        def add(p, c):
            if c > need.get(p, 0):
                need[p] = c

        for b in reads:
            if b.w is not None:
                add(*b.w)
        for b in writes:
            if b.w is not None:
                add(*b.w)
            for p, c in b.readers.items():
                add(p, c)
        w = self.waited[ename]
        out = []
        for p in sorted(need, key=lambda p: p.idx):
            c = need[p]
            assert c <= p.cnt, f"wait beyond emitted count on {p.name}: {c} > {p.cnt}"
            if w.get(p.idx, 0) < c:
                out.append((p.sem, c))
                w[p.idx] = c
        return out

    def _mark(self, prod, cnt, reads, writes):
        for b in reads:
            b.readers[prod] = cnt
        for b in writes:
            b.w = (prod, cnt)
            b.readers = {}

    def emit(self, ename, build, reads=(), writes=(), inc=True):
        for sem, c in self._waits_for(ename, reads, writes):
            self.seq[ename].append(("wait", sem, c))
        p = self.eng[ename]
        if not inc:
            self.seq[ename].append(("inst", build, None, 0))
            return p.cnt
        p.cnt += 1
        self.seq[ename].append(("inst", build, p.sem, 1))
        self._mark(p, p.cnt, reads, writes)
        return p.cnt

    def _buf_dma_prod(self, buf, is_cc=False):
        if buf.dmap is None:
            sem = self.es.enter_context(self.nc.semaphore(f"qb_{buf.name}"))
            buf.dmap = _Prod(f"qb_{buf.name}", sem, len(self.prods), 16)
            buf.dmap.is_cc = is_cc
            self.prods.append(buf.dmap)
        return buf.dmap

    def dma(self, q, out_ap, in_ap, reads=(), writes=(), **kw):
        assert len(writes) == 1, "dma needs exactly one target buf"
        p = self._buf_dma_prod(writes[0])
        for sem, c in self._waits_for(q.engine, reads, writes):
            self.seq[q.engine].append(("wait", sem, c))
        p.cnt += 16
        self.seq[q.engine].append(
            ("inst", lambda E, o=out_ap, i=in_ap, kw=kw: E.dma_start(out=o, in_=i, **kw),
             p.sem, 16))
        self._mark(p, p.cnt, reads, writes)
        return p.cnt

    def collective(self, q, kind, op, ins, outs, reads=(), writes=()):
        rg = [list(range(NCORES))]
        for i, (iap, oap) in enumerate(zip(ins, outs)):
            wr = [writes[i]]
            p = self._buf_dma_prod(writes[i], is_cc=True)
            p.step = 1
            for sem, c in self._waits_for(q.engine, reads, wr):
                self.seq[q.engine].append(("wait", sem, c))
            p.cnt += 1
            self.seq[q.engine].append(
                ("inst", lambda E, iap=iap, oap=oap: E.collective_compute(
                    kind, op, replica_groups=rg, ins=[iap], outs=[oap]),
                 p.sem, 1))
            self._mark(p, p.cnt, reads, wr)

    def barrier(self):
        snap = [(p, p.cnt) for p in self.prods if not p.is_cc]
        for e in ENGS:
            w = self.waited[e]
            for p, c in snap:
                if p is self.eng[e]:
                    continue
                if c > 0 and w.get(p.idx, 0) < c:
                    self.seq[e].append(("wait", p.sem, c))
                    w[p.idx] = c

    def finalize(self, out_buf):
        w = self.waited["sync"]
        p = out_buf.dmap
        assert p is not None and p.cnt > 0, "no output stores recorded"
        if w.get(p.idx, 0) < p.cnt:
            self.seq["sync"].append(("wait", p.sem, p.cnt))
            w[p.idx] = p.cnt

    def replay(self):
        with self.nc.Block() as blk:
            for ename in ENGS:
                items = self.seq[ename]

                def run(E, items=items):
                    for it in items:
                        if it[0] == "wait":
                            E.wait_ge(it[1], it[2])
                        else:
                            _, build, sem, inc = it
                            ins = build(E)
                            if sem is not None:
                                ins.then_inc(sem, inc)

                getattr(blk, ename)(run)



def _host_consts(core):
    import ml_dtypes
    pos = (np.arange(NT) * NCORES + core) * 128
    pos = (pos[:, None] + np.arange(128)[None, :]).reshape(-1).astype(np.float64)
    theta = 1.0 / (10000.0 ** (np.arange(0, 256, 2, dtype=np.float64) / 256.0))
    ang = pos[:, None] * theta[None, :]
    cos2 = np.repeat(np.cos(ang), 2, axis=1).astype(np.float32)
    sin2 = np.repeat(np.sin(ang), 2, axis=1).astype(np.float32)
    p = np.arange(128)[:, None]
    ck = np.arange(1024)[None, :]
    c = ck // 128
    kk = ck % 128
    keep = (c - core) * 128 + kk <= p
    tailmask = np.where(keep, 0.0, NEG).astype(np.float32)
    return cos2, sin2, tailmask


def _host_shared_consts():
    import ml_dtypes
    ident_bf_u16 = np.eye(128, dtype=ml_dtypes.bfloat16).view(np.uint16)
    ident_f32 = np.eye(128, dtype=np.float32)
    ones_row = np.ones((1, 128), dtype=np.float32)
    return ident_bf_u16, ident_f32, ones_row


def _shard_idx(core):
    return ((np.arange(NT)[:, None] * NCORES + core) * 128
            + np.arange(128)[None, :]).reshape(-1)



def build_program(n_sub=4):
    nc = bass.Bass("TRN2", target_bir_lowering=False, debug=False)
    es = ExitStack()
    mk = MK(nc, es)

    T = {}

    def din(name, shape, dt=F32):
        T[name] = nc.dram_tensor(name, list(shape), dt, kind="ExternalInput").ap()
        return T[name]

    def dint(name, shape, dt, addr_space="Local"):
        T[name] = nc.dram_tensor(name, list(shape), dt, kind="Internal",
                                 addr_space=addr_space).ap()
        return T[name]

    din("x", (TLOC, D))
    din("aWq", (D, 4096)); din("aWk", (D, 1024)); din("aWv", (D, 512)); din("aWo", (D, D))
    din("lq1", (16, 128)); din("lq2", (16, 128)); din("lk1", (16, 128)); din("lk2", (16, 128))
    din("rms_w_tile", (128, 128))
    din("bWq", (D, D)); din("bWk", (D, D)); din("bWv", (D, D)); din("bWo", (D, D))
    din("bbq", (16, 128)); din("bbk", (16, 128)); din("bbv", (16, 128)); din("bbo", (1, D))
    din("gw", (NEXP, D))
    din("gup_my", (4, 2 * INTER, D))
    din("dwn_my", (4, D, INTER))
    din("alpha", (1, 1))
    din("cos2", (TLOC, 256)); din("sin2", (TLOC, 256))
    din("tailmask", (128, 1024))
    din("ident_bf_u16", (128, 128), mybir.dt.uint16); din("ident_f32", (128, 128))
    din("ones_row", (1, 128))
    out_d = nc.dram_tensor("out", [TLOC, D], F32, kind="ExternalOutput").ap()
    T["out"] = out_d

    dint("R1", (TLOC, D), F32)
    dint("R2", (TLOC, D), F32)
    dint("R3", (TLOC, D), F32)
    dint("ka_loc", (1024, TLOC), BF16)
    dint("va_loc", (TLOC, 512), BF16)
    dint("ka_all", (1024 * NCORES, TLOC), BF16, addr_space="Shared")
    dint("va_all", (TLOC * NCORES, 512), BF16, addr_space="Shared")
    dint("kb_loc", (D, TLOC), BF16)
    dint("vb_loc", (TLOC, D), BF16)
    dint("kb_all", (D * NCORES, TLOC), BF16, addr_space="Shared")
    dint("vb_all", (TLOC * NCORES, D), BF16, addr_space="Shared")
    dint("gupT_loc", (4 * D, 2 * INTER), BF16)
    dint("dwnT_loc", (4 * INTER, D), BF16)
    dint("gupT_all", (4 * D * NCORES, 2 * INTER), BF16, addr_space="Shared")
    dint("dwnT_all", (4 * INTER * NCORES, D), BF16, addr_space="Shared")

    q_const = mk.new_q("const", "sync")
    q_cc = mk.new_q("cc", "gpsimd")

    sb = {}

    def alloc(name, shape, dt, scope=es):
        t = scope.enter_context(nc.sbuf_tensor(name, list(shape), dt))
        sb[name] = t
        return t

    c_lnT = alloc("c_lnT", (128, 16 * 1024), BF16)
    ident_bf_u16 = alloc("s_ident_bf", (128, 128), mybir.dt.uint16)
    ident_f32 = alloc("s_ident_f32", (128, 128), F32)
    ones_row = alloc("s_ones_row", (1, 128), F32)

    b = {}

    def B(name):
        if name not in b:
            b[name] = Buf(name)
        return b[name]

    mk.dma(q_const, ident_bf_u16[:], T["ident_bf_u16"], writes=[B("ident_bf")])
    mk.dma(q_const, ident_f32[:], T["ident_f32"], writes=[B("ident_f32")])
    mk.dma(q_const, ones_row[:], T["ones_row"], writes=[B("ones_row")])

    ident_bf_view = ident_bf_u16[:].bitcast(BF16)
    ctx = dict(nc=nc, mk=mk, es=es, T=T, sb=sb, B=B, alloc=alloc,
               q_const=q_const, q_cc=q_cc,
               c_lnT=c_lnT, ident_bf=ident_bf_view, ident_f32=ident_f32,
               ones_row=ones_row)

    for k in range(16):
        B(f"lnT{k}")

    _build_phases(ctx, n_sub)

    mk.replay()
    es.close()
    return nc


def _emit_ln(ctx, src_ap, src_buf, tag, with_gating=False, scope=None):
    nc, mk, B = ctx["nc"], ctx["mk"], ctx["B"]
    es = scope if scope is not None else ctx["es"]
    alloc = ctx["alloc"]
    c_lnT = ctx["c_lnT"]
    q_r = mk.new_q(f"lnr_{tag}", "sync")
    rt = [alloc(f"ln_rt{i}_{tag}", (128, D), F32, es) for i in range(2)]
    dump = alloc(f"ln_dump_{tag}", (128, D), F32, es)
    lnbf = alloc(f"ln_bf_{tag}", (128, D), BF16, es)
    stats = alloc(f"ln_stats_{tag}", (128, 8), F32, es)
    p2 = es.enter_context(nc.psum_tensor(f"ln_p2_{tag}", [128, 1024], F32))
    ptA = es.enter_context(nc.psum_tensor(f"ln_ptA_{tag}", [128, 1024], BF16))
    ptB = es.enter_context(nc.psum_tensor(f"ln_ptB_{tag}", [128, 1024], BF16))
    if with_gating:
        gat = _gating_prep(ctx, es, tag)

    for tt in range(NT):
        sl = tt % 2
        brt = B(f"ln_rt{sl}")
        mk.dma(q_r, rt[sl][:], src_ap[tt * 128:(tt + 1) * 128, :],
               reads=[src_buf], writes=[brt])
        bst = B("ln_stats")
        mk.emit("vector", lambda E, i=rt[sl][:], o=stats[:, 0:1]:
                E.reduce_sum(o, i, axis=AX.X), reads=[brt], writes=[bst])
        mk.emit("vector", lambda E, i=stats[:, 0:1], o=stats[:, 1:2]:
                E.tensor_scalar_mul(o, i, -1.0 / D), reads=[bst], writes=[bst])
        bdump = B("ln_dump")
        mk.emit("scalar", lambda E, i=rt[sl][:], o=dump[:], m=stats[:, 1:2], a=stats[:, 2:3]:
                E.activation(o, i, AF.Square, bias=m, scale=1.0, accum_out=a),
                reads=[brt, bst], writes=[bdump, bst])
        mk.emit("vector", lambda E, i=stats[:, 2:3], o=stats[:, 5:6]:
                E.tensor_scalar(o, i, 1.0 / D, EPS_LN, ALU.mult, ALU.add),
                reads=[bst], writes=[bst])
        mk.emit("scalar", lambda E, i=stats[:, 5:6], o=stats[:, 3:4]:
                E.activation(o, i, AF.Ln, bias=0.0, scale=1.0),
                reads=[bst], writes=[bst])
        mk.emit("scalar", lambda E, i=stats[:, 3:4], o=stats[:, 4:5]:
                E.activation(o, i, AF.Exp, bias=0.0, scale=-0.5),
                reads=[bst], writes=[bst])
        blnbf = B("ln_bf")
        mk.emit("vector", lambda E, i=rt[sl][:], o=lnbf[:], m=stats[:, 1:2], r=stats[:, 4:5]:
                E.tensor_scalar(o, i, m, r, ALU.add, ALU.mult),
                reads=[brt, bst], writes=[blnbf])
        for k in range(16):
            pt = ptA if k % 2 == 0 else ptB
            bpt = B(f"ln_pt{k % 2}")
            mk.emit("tensor", lambda E, o=pt[:, 0:128], i=lnbf[:, k * 128:(k + 1) * 128],
                    idn=ctx["ident_bf"]:
                    E.transpose(o, i, idn),
                    reads=[blnbf, B("ident_bf")], writes=[bpt])
            mk.emit("vector", lambda E, o=c_lnT[:, k * 1024 + tt * 128: k * 1024 + (tt + 1) * 128],
                    i=pt[:, 0:128]:
                    E.tensor_copy(o, i),
                    reads=[bpt], writes=[B(f"lnT{k}")])
        if with_gating:
            _gating_tile(ctx, gat, tt, rt[sl], brt, stats, bst, p2, tag)
    return gat if with_gating else None


def _gating_prep(ctx, es, tag):
    nc, mk, B, alloc = ctx["nc"], ctx["mk"], ctx["B"], ctx["alloc"]
    g = {}
    g["gwT"] = alloc(f"gwT_{tag}", (128, 16 * 32), F32, es)
    g["comb"] = ctx["sb"][f"comb_{tag}"][:] if f"comb_{tag}" in ctx["sb"] else None
    assert g["comb"] is not None, f"comb_{tag} must be preallocated by caller"
    g["ln32"] = alloc(f"ln32_{tag}", (128, D), F32, es)
    g["lnT32"] = alloc(f"lnT32_{tag}", (128, D), F32, es)
    g["gsc"] = alloc(f"gsc_{tag}", (128, 80), F32, es)
    g["alpha_b"] = alloc(f"alpha_b_{tag}", (128, 1), F32, es)
    g["gwsb"] = alloc(f"gwsb_{tag}", (32, D), F32, es)
    q_g = mk.new_q(f"gw_{tag}", "sync")
    g["q"] = q_g
    p4 = es.enter_context(nc.psum_tensor(f"g_p4_{tag}", [128, 512], F32))
    g["p4"] = p4
    mk.dma(q_g, g["gwsb"][:], ctx["T"]["gw"], writes=[B("gwsb")])
    for k in range(16):
        mk.emit("tensor", lambda E, o=p4[:, 0:32], i=g["gwsb"][:, k * 128:(k + 1) * 128],
                idn=ctx["ident_f32"][0:32, 0:32]:
                E.transpose(o, i, idn),
                reads=[B("gwsb"), B("ident_f32")], writes=[B("g_p4")])
        mk.emit("vector", lambda E, o=g["gwT"][:, k * 32:(k + 1) * 32], i=p4[:, 0:32]:
                E.tensor_copy(o, i),
                reads=[B("g_p4")], writes=[B("gwT")])
    al = alloc(f"alpha_sb_{tag}", (1, 1), F32, es)
    mk.dma(q_g, al[:], ctx["T"]["alpha"], writes=[B("alpha_sb")])
    mk.emit("tensor", lambda E, o=p4[:, 0:1], l=ctx["ones_row"][:], r=al[:]:
            E.matmul(o, l, r, start=True, stop=True),
            reads=[B("alpha_sb"), B("ones_row")], writes=[B("g_p4")])
    mk.emit("vector", lambda E, o=g["alpha_b"][:], i=p4[:, 0:1]:
            E.tensor_copy(o, i),
            reads=[B("g_p4")], writes=[B("alpha_b")])
    g["tag"] = tag
    return g


def _gating_tile(ctx, g, tt, rt, brt, stats, bst, p2, tag):
    nc, mk, B = ctx["nc"], ctx["mk"], ctx["B"]
    b32 = B("ln32")
    mk.emit("vector", lambda E, i=rt[:], o=g["ln32"][:], m=stats[:, 1:2], r=stats[:, 4:5]:
            E.tensor_scalar(o, i, m, r, ALU.add, ALU.mult),
            reads=[brt, bst], writes=[b32])
    bT32 = B("lnT32")
    for k in range(16):
        half = (k % 2) * 512
        bp = B(f"g_p2{k % 2}")
        mk.emit("tensor", lambda E, o=p2[:, half:half + 128], i=g["ln32"][:, k * 128:(k + 1) * 128],
                idn=ctx["ident_f32"][:]:
                E.transpose(o, i, idn),
                reads=[b32, B("ident_f32")], writes=[bp])
        mk.emit("vector", lambda E, o=g["lnT32"][:, k * 128:(k + 1) * 128], i=p2[:, half:half + 128]:
                E.tensor_copy(o, i),
                reads=[bp], writes=[bT32])
    p4 = g["p4"]
    bpl = B("g_logit")
    for k in range(16):
        mk.emit("tensor", lambda E, o=p4[:, 64:96], l=g["lnT32"][:, k * 128:(k + 1) * 128],
                r=g["gwT"][:, k * 32:(k + 1) * 32], s=(k == 0), st=(k == 15):
                E.matmul(o, l, r, start=s, stop=st),
                reads=[bT32, B("gwT")], writes=[bpl], inc=(k == 15))
    gs = ctx["sb"][f"gsc_{g['tag']}"]
    bgs = B("gsc")
    probs = gs[:, 0:32]
    top8 = gs[:, 32:40]
    mk.emit("vector", lambda E, o=gs[:, 40:41], i=p4[:, 64:96]:
            E.reduce_max(o, i, axis=AX.X, negate=True), reads=[bpl], writes=[bgs])
    mk.emit("scalar", lambda E, o=probs, i=p4[:, 64:96], m=gs[:, 40:41], a=gs[:, 41:42]:
            E.activation(o, i, AF.Exp, bias=m, scale=1.0, accum_out=a),
            reads=[bpl, bgs], writes=[bgs])
    mk.emit("vector", lambda E, o=gs[:, 42:43], i=gs[:, 41:42]:
            E.reciprocal(o, i), reads=[bgs], writes=[bgs])
    mk.emit("vector", lambda E, o=probs, i=probs, r=gs[:, 42:43]:
            E.tensor_scalar_mul(o, i, r), reads=[bgs], writes=[bgs])
    mk.emit("vector", lambda E, o=top8, i=probs:
            E.max(o, i), reads=[bgs], writes=[bgs])
    mk.emit("vector", lambda E, o=gs[:, 43:44], i=top8:
            E.reduce_sum(o, i, axis=AX.X), reads=[bgs], writes=[bgs])
    pa = gs[:, 48:80]
    mk.emit("vector", lambda E, o=gs[:, 44:45], i=gs[:, 43:44]:
            E.tensor_scalar_mul(o, i, TAU), reads=[bgs], writes=[bgs])
    mk.emit("vector", lambda E, o=pa, i=probs, r=ctx["sb"][f"alpha_b_{g['tag']}"][:]:
            E.tensor_scalar_mul(o, i, r), reads=[bgs, B("alpha_b")], writes=[bgs])
    mk.emit("vector", lambda E, o=pa, i=pa, r=gs[:, 44:45]:
            E.tensor_single_scalar(o, i, r, ALU.is_ge), reads=[bgs], writes=[bgs])
    mk.emit("vector", lambda E, o=gs[:, 45:46], i=top8[:, 0:1], r=ctx["sb"][f"alpha_b_{g['tag']}"][:]:
            E.tensor_scalar_mul(o, i, r), reads=[bgs, B("alpha_b")], writes=[bgs])
    mk.emit("vector", lambda E, o=gs[:, 45:46], i=gs[:, 45:46], r=gs[:, 44:45]:
            E.tensor_single_scalar(o, i, r, ALU.is_lt), reads=[bgs], writes=[bgs])
    fb = gs[:, 0:32]
    comb_t = g["comb"][:, tt * 32:(tt + 1) * 32]
    bcomb = B("comb")
    mk.emit("vector", lambda E, o=comb_t, i=probs, r=top8[:, 0:1]:
            E.tensor_single_scalar(o, i, r, ALU.is_equal), reads=[bgs], writes=[bcomb])
    mk.emit("vector", lambda E, o=comb_t, i=comb_t, r=gs[:, 45:46]:
            E.tensor_scalar_mul(o, i, r), reads=[bcomb, bgs], writes=[bcomb])
    mk.emit("vector", lambda E, o=pa, i0=pa, i1=comb_t:
            E.tensor_add(o, i0, i1), reads=[bgs, bcomb], writes=[bgs])
    mk.emit("vector", lambda E, o=comb_t, i=probs, r=top8[:, 7:8]:
            E.tensor_single_scalar(o, i, r, ALU.is_ge), reads=[bgs], writes=[bcomb])
    mk.emit("vector", lambda E, o=pa, i0=pa, i1=comb_t:
            E.tensor_mul(o, i0, i1), reads=[bgs, bcomb], writes=[bgs])
    mk.emit("vector", lambda E, o=comb_t, i0=probs, i1=pa:
            E.tensor_mul(o, i0, i1), reads=[bgs], writes=[bcomb])
    mk.emit("vector", lambda E, o=gs[:, 46:47], i=comb_t:
            E.reduce_sum(o, i, axis=AX.X), reads=[bcomb], writes=[bgs])
    mk.emit("vector", lambda E, o=gs[:, 46:47], i=gs[:, 46:47]:
            E.tensor_single_scalar(o, i, 1e-12, ALU.max), reads=[bgs], writes=[bgs])
    mk.emit("vector", lambda E, o=gs[:, 47:48], i=gs[:, 46:47]:
            E.reciprocal(o, i), reads=[bgs], writes=[bgs])
    mk.emit("vector", lambda E, o=comb_t, i=comb_t, r=gs[:, 47:48]:
            E.tensor_scalar_mul(o, i, r), reads=[bcomb, bgs], writes=[bcomb])


def _phase_kv_a(ctx, scope):
    nc, mk, B, T, alloc = ctx["nc"], ctx["mk"], ctx["B"], ctx["T"], ctx["alloc"]
    c_lnT = ctx["c_lnT"]
    es = scope
    wk = alloc("kv_wk", (128, 16 * 1024), BF16, es)
    wv = alloc("kv_wv", (128, 16 * 512), BF16, es)
    krt = alloc("kv_krt", (128, 1024), BF16, es)
    tco = alloc("kv_tcos", (128, 256), F32, es)
    tsi = alloc("kv_tsin", (128, 256), F32, es)
    vbf = alloc("kv_vbf", (128, 512), BF16, es)
    kev = alloc("kv_kev", (128, 256), BF16, es)
    p2a = es.enter_context(nc.psum_tensor("kv_p2a", [128, 1024], F32))
    p2b = es.enter_context(nc.psum_tensor("kv_p2b", [128, 1024], F32))
    p1 = es.enter_context(nc.psum_tensor("kv_p1", [128, 512], F32))
    ptA = es.enter_context(nc.psum_tensor("kv_ptA", [128, 1024], BF16))
    ptB = es.enter_context(nc.psum_tensor("kv_ptB", [128, 1024], BF16))
    ctx["kv_ptA"], ctx["kv_ptB"] = ptA, ptB
    q_w = mk.new_q("kvw", "gpsimd")
    q_st = mk.new_q("kvst", "scalar")

    for k in range(16):
        mk.dma(q_w, wk[:, k * 1024:(k + 1) * 1024], T["aWk"][k * 128:(k + 1) * 128, :],
               writes=[B("kv_wk")])
        mk.dma(q_w, wv[:, k * 512:(k + 1) * 512], T["aWv"][k * 128:(k + 1) * 128, :],
               writes=[B("kv_wv")])
    for tt in range(NT):
        p2 = p2a if tt % 2 == 0 else p2b
        bp2 = B(f"kv_p2{tt % 2}")
        for k in range(16):
            for hl in range(2):
                mk.emit("tensor", lambda E, o=p2[:, hl * 512:(hl + 1) * 512],
                        l=c_lnT[:, k * 1024 + tt * 128: k * 1024 + (tt + 1) * 128],
                        r=wk[:, k * 1024 + hl * 512: k * 1024 + (hl + 1) * 512],
                        s=(k == 0), st=(k == 15):
                        E.matmul(o, l, r, start=s, stop=st),
                        reads=[B(f"lnT{k}"), B("kv_wk")], writes=[bp2], inc=(k == 15))
        bkrt = B("kv_krt")
        for kh in range(4):
            c0 = kh * 256
            mk.emit("vector", lambda E, o=tco[:], i0=p2[:, c0:c0 + 256],
                    i1=ctx["cos2_sb"][:, tt * 256:(tt + 1) * 256]:
                    E.tensor_mul(o, i0, i1),
                    reads=[bp2, B("cos2_sb")], writes=[B("kv_tcos")])
            mk.emit("vector", lambda E, o=tsi[:], i0=p2[:, c0:c0 + 256],
                    i1=ctx["sin2_sb"][:, tt * 256:(tt + 1) * 256]:
                    E.tensor_mul(o, i0, i1),
                    reads=[bp2, B("sin2_sb")], writes=[B("kv_tsin")])
            mk.emit("vector", lambda E, o=krt[:, c0:c0 + 256:2], i0=tco[:, 0:256:2],
                    i1=tsi[:, 1:256:2]:
                    E.tensor_sub(o, i0, i1),
                    reads=[B("kv_tcos"), B("kv_tsin")], writes=[bkrt])
            mk.emit("vector", lambda E, o=krt[:, c0 + 1:c0 + 256:2], i0=tsi[:, 0:256:2],
                    i1=tco[:, 1:256:2]:
                    E.tensor_add(o, i0, i1),
                    reads=[B("kv_tcos"), B("kv_tsin")], writes=[bkrt])
        for c in range(8):
            pt = ptA if c % 2 == 0 else ptB
            bpt = B(f"kv_pt{c % 2}")
            mk.emit("tensor", lambda E, o=pt[:, 0:128], i=krt[:, c * 128:(c + 1) * 128],
                    idn=ctx["ident_bf"]:
                    E.transpose(o, i, idn),
                    reads=[bkrt, B("ident_bf")], writes=[bpt])
            bke = B(f"kv_kev{c % 2}")
            mk.emit("vector", lambda E, o=kev[:, (c % 2) * 128:(c % 2) * 128 + 128], i=pt[:, 0:128]:
                    E.tensor_copy(o, i),
                    reads=[bpt], writes=[bke])
            mk.dma(q_st, T["ka_loc"][c * 128:(c + 1) * 128, tt * 128:(tt + 1) * 128],
                   kev[:, (c % 2) * 128:(c % 2) * 128 + 128],
                   reads=[bke], writes=[B("ka_loc")])
        bp1 = B("kv_p1")
        for k in range(16):
            mk.emit("tensor", lambda E, o=p1[:, 0:512],
                    l=c_lnT[:, k * 1024 + tt * 128: k * 1024 + (tt + 1) * 128],
                    r=wv[:, k * 512:(k + 1) * 512], s=(k == 0), st=(k == 15):
                    E.matmul(o, l, r, start=s, stop=st),
                    reads=[B(f"lnT{k}"), B("kv_wv")], writes=[bp1], inc=(k == 15))
        mk.emit("vector", lambda E, o=vbf[:], i=p1[:, 0:512]:
                E.tensor_copy(o, i), reads=[bp1], writes=[B("kv_vbf")])
        mk.dma(q_st, T["va_loc"][tt * 128:(tt + 1) * 128, :], vbf[:],
               reads=[B("kv_vbf")], writes=[B("va_loc")])
    mk.collective(ctx["q_cc"], "AllGather", ALU.bypass,
                  [T["ka_loc"].opt(), T["va_loc"].opt()],
                  [T["ka_all"].opt(), T["va_all"].opt()],
                  reads=[B("ka_loc"), B("va_loc")],
                  writes=[B("ka_all"), B("va_all")])


def _phase_stage0(ctx, scope, core_experts=4):
    nc, mk, B, T, alloc = ctx["nc"], ctx["mk"], ctx["B"], ctx["T"], ctx["alloc"]
    es = scope
    wt = alloc("s0_wt", (128, 2 * 2048), BF16, es)
    wev = alloc("s0_wev", (128, 2 * 128), BF16, es)
    ptA, ptB = ctx["kv_ptA"], ctx["kv_ptB"]
    q_l = mk.new_q("s0l", "gpsimd")
    q_s = mk.new_q("s0s", "scalar")

    def transpose_store(src_view, n_f, dst_rowbase, dst_col, dst_ap, slot, tag):
        for f in range(n_f):
            pt = ptA if f % 2 == 0 else ptB
            bpt = B(f"kv_pt{f % 2}")
            mk.emit("tensor", lambda E, o=pt[:, 0:128], i=src_view[:, f * 128:(f + 1) * 128],
                    idn=ctx["ident_bf"]:
                    E.transpose(o, i, idn),
                    reads=[B(f"s0_wt{slot}"), B("ident_bf")], writes=[bpt])
            bev = B(f"s0_wev{f % 2}")
            mk.emit("vector", lambda E, o=wev[:, (f % 2) * 128:(f % 2) * 128 + 128], i=pt[:, 0:128]:
                    E.tensor_copy(o, i), reads=[bpt], writes=[bev])
            mk.dma(q_s, dst_ap[dst_rowbase + f * 128: dst_rowbase + (f + 1) * 128,
                              dst_col:dst_col + 128],
                   wev[:, (f % 2) * 128:(f % 2) * 128 + 128],
                   reads=[bev], writes=[B(tag)])

    for e4 in range(core_experts):
        for ib in range(12):
            slot = ib % 2
            bwt = B(f"s0_wt{slot}")
            mk.dma(q_l, wt[:, slot * 2048:(slot + 1) * 2048],
                   T["gup_my"][e4, ib * 128:(ib + 1) * 128, :],
                   writes=[bwt])
            transpose_store(wt[:, slot * 2048:(slot + 1) * 2048], 16,
                            e4 * 2048, ib * 128, T["gupT_loc"], slot, "gupT_loc")
        for ob in range(16):
            slot = ob % 2
            bwt = B(f"s0_wt{slot}")
            mk.dma(q_l, wt[:, slot * 2048: slot * 2048 + 768],
                   T["dwn_my"][e4, ob * 128:(ob + 1) * 128, :],
                   writes=[bwt])
            transpose_store(wt[:, slot * 2048: slot * 2048 + 768], 6,
                            e4 * 768, ob * 128, T["dwnT_loc"], slot, "dwnT_loc")
    mk.collective(ctx["q_cc"], "AllGather", ALU.bypass,
                  [T["gupT_loc"].opt(), T["dwnT_loc"].opt()],
                  [T["gupT_all"].opt(), T["dwnT_all"].opt()],
                  reads=[B("gupT_loc"), B("dwnT_loc")],
                  writes=[B("gupT_all"), B("dwnT_all")])


def _phase_att_a(ctx, outer, n_heads=16):
    nc, mk, B, T, alloc = ctx["nc"], ctx["mk"], ctx["B"], ctx["T"], ctx["alloc"]
    es = outer
    cS_t = alloc("aS", (128, 8192), F32, es)
    cS = cS_t[:]
    cE0 = cS_t[:].bitcast(BF16)[:, 0:8192]
    cE_t = alloc("aE", (128, 8192), BF16, es)
    cK_t = alloc("aK", (128, 2 * 8192), BF16, es)
    cV_t = alloc("aV", (128, 8192), BF16, es)
    qfm = alloc("aqfm", (128, 2 * 1024), BF16, es)
    wq = alloc("awq", (128, 2 * 256), BF16, es)
    tmask = alloc("atm", (128, 1024), F32, es)
    zst = alloc("azst", (128, 96), F32, es)
    d0 = alloc("ad0", (128, 512), F32, es)
    t0 = alloc("at0", (128, 512), F32, es)
    pps = alloc("apps", (128, 1024), BF16, es)
    pts = alloc("apts", (128, 1024), BF16, es)
    tco = alloc("atcos", (128, 256), F32, es)
    tsi = alloc("atsin", (128, 256), F32, es)
    qrt = alloc("aqrt", (128, 256), BF16, es)
    ae = alloc("aae", (128, 128), F32, es)
    ae2 = alloc("aae2", (128, 128), F32, es)
    dump = alloc("adump", (128, 128), F32, es)
    aebf = alloc("aaebf", (128, 128), BF16, es)
    pBig = es.enter_context(nc.psum_tensor("a_pbig", [128, 2048], F32))
    ptA = es.enter_context(nc.psum_tensor("a_ptA", [128, 1024], BF16))
    ptB = es.enter_context(nc.psum_tensor("a_ptB", [128, 1024], BF16))
    q_k = mk.new_q("akv", "sync")
    q_wq = mk.new_q("awq", "gpsimd")

    mk.dma(mk.new_q("atm", "sync"), tmask[:], T["tailmask"], writes=[B("atm")])
    bq = [B(f"a_pb{i}") for i in range(4)]
    bptA, bptB = B("a_ptA"), B("a_ptB")
    c_att = ctx["c_att"]

    for g in range(4):
        for a in range(2):
            bK = B(f"aK{a}")
            for r in range(NCORES):
                src = T["ka_all"][r * 1024 + (g * 2 + a) * 128:
                                  r * 1024 + (g * 2 + a) * 128 + 128, :]
                src = src.rearrange("d (j t) -> d j t", t=128)
                dst = cK_t[:, a * 8192:(a + 1) * 8192].rearrange(
                    "p (blk t) -> p blk t", t=128)[:, r::8, :]
                mk.dma(q_k, dst, src, reads=[B("ka_all")], writes=[bK])
        bV = B("aV")
        for r in range(NCORES):
            src = T["va_all"][r * 1024:(r + 1) * 1024, g * 128:(g + 1) * 128]
            src = src.rearrange("(j t) d -> t j d", t=128)
            dst = cV_t[:].rearrange("p (blk d) -> p blk d", d=128)[:, r::8, :]
            mk.dma(q_k, dst, src, reads=[B("va_all")], writes=[bV])

        for h4 in range(4):
            h = g * 4 + h4
            if h >= n_heads:
                continue
            bqfm = B("aqfm")
            for k in range(16):
                slot = k % 2
                bwq = B(f"awq{slot}")
                mk.dma(q_wq, wq[:, slot * 256:(slot + 1) * 256],
                       T["aWq"][k * 128:(k + 1) * 128, h * 256:(h + 1) * 256],
                       writes=[bwq])
                for tt in range(NT):
                    mk.emit("tensor", lambda E, o=pBig[:, tt * 256:(tt + 1) * 256],
                            l=c_lnT_slice(ctx, k, tt),
                            r=wq[:, slot * 256:(slot + 1) * 256],
                            s=(k == 0 and tt % 2 == 0), st=(k == 15 and tt % 2 == 1):
                            E.matmul(o, l, r, start=s, stop=st),
                            reads=[B(f"lnT{k}"), bwq], writes=[bq[tt // 2]],
                            inc=(tt == NT - 1) or (k == 15))
            for tt in range(NT):
                bnk = bq[tt // 2]
                mk.emit("vector", lambda E, o=tco[:], i0=pBig[:, tt * 256:(tt + 1) * 256],
                        i1=ctx["cos2_sb"][:, tt * 256:(tt + 1) * 256]:
                        E.tensor_mul(o, i0, i1),
                        reads=[bnk, B("cos2_sb")], writes=[B("atcos")])
                mk.emit("vector", lambda E, o=tsi[:], i0=pBig[:, tt * 256:(tt + 1) * 256],
                        i1=ctx["sin2_sb"][:, tt * 256:(tt + 1) * 256]:
                        E.tensor_mul(o, i0, i1),
                        reads=[bnk, B("sin2_sb")], writes=[B("atsin")])
                mk.emit("vector", lambda E, o=qrt[:, 0:256:2], i0=tco[:, 0:256:2],
                        i1=tsi[:, 1:256:2]:
                        E.tensor_sub(o, i0, i1),
                        reads=[B("atcos"), B("atsin")], writes=[B("aqrt")])
                mk.emit("vector", lambda E, o=qrt[:, 1:256:2], i0=tsi[:, 0:256:2],
                        i1=tco[:, 1:256:2]:
                        E.tensor_add(o, i0, i1),
                        reads=[B("atcos"), B("atsin")], writes=[B("aqrt")])
                for a in range(2):
                    pt = ptA if a == 0 else ptB
                    bpt = bptA if a == 0 else bptB
                    mk.emit("tensor", lambda E, o=pt[:, 0:128],
                            i=qrt[:, a * 128:(a + 1) * 128], idn=ctx["ident_bf"]:
                            E.transpose(o, i, idn),
                            reads=[B("aqrt"), B("ident_bf")], writes=[bpt])
                    mk.emit("vector", lambda E,
                            o=qfm[:, a * 1024 + tt * 128: a * 1024 + (tt + 1) * 128],
                            i=pt[:, 0:128]:
                            E.tensor_copy(o, i), reads=[bpt], writes=[bqfm])
            for j in range(NT):
                _att_a_unit(ctx, h, j, cS, cE0, cE_t, cK_t, cV_t, qfm, tmask,
                            zst, d0, t0, pps, pts, ae, ae2, dump, aebf,
                            pBig, ptA, ptB, bq, bptA, bptB)


def c_lnT_slice(ctx, k, tt):
    return ctx["c_lnT"][:, k * 1024 + tt * 128: k * 1024 + (tt + 1) * 128]


def _att_a_unit(ctx, h, j, cS, cE0, cE_t, cK_t, cV_t, qfm, tmask, zst, d0, t0,
                pps, pts, ae, ae2, dump, aebf, pBig, ptA, ptB, bq, bptA, bptB):
    nc, mk, B = ctx["nc"], ctx["mk"], ctx["B"]
    nslab = 2 * (j + 1)
    ext = nslab * 512
    bS, bE, bz = B("aS"), B("aE"), B("azst")
    for a in (1, 0):
        for s in range(nslab):
            bank = bq[s % 2]
            mk.emit("tensor", lambda E, o=pBig[:, (s % 2) * 512:(s % 2) * 512 + 512],
                    l=qfm[:, a * 1024 + j * 128: a * 1024 + (j + 1) * 128],
                    r=cK_t[:, a * 8192 + s * 512: a * 8192 + (s + 1) * 512]:
                    E.matmul(o, l, r, start=True, stop=True),
                    reads=[B("aqfm"), B(f"aK{a}")], writes=[bank])
            if s >= nslab - 2:
                tm0 = (s - (nslab - 2)) * 512
                mk.emit("vector", lambda E, o=cS[:, s * 512:(s + 1) * 512],
                        i0=pBig[:, (s % 2) * 512:(s % 2) * 512 + 512],
                        i1=tmask[:, tm0:tm0 + 512]:
                        E.scalar_tensor_tensor(o, i0, SCL_A, i1, ALU.mult, ALU.add),
                        reads=[bank, B("atm")], writes=[bS])
            elif (s + a) % 2 == 1:
                mk.emit("scalar", lambda E, o=cS[:, s * 512:(s + 1) * 512],
                        i=pBig[:, (s % 2) * 512:(s % 2) * 512 + 512]:
                        E.activation(o, i, AF.Copy, bias=0.0, scale=SCL_A),
                        reads=[bank], writes=[bS])
            else:
                mk.emit("vector", lambda E, o=cS[:, s * 512:(s + 1) * 512],
                        i=pBig[:, (s % 2) * 512:(s % 2) * 512 + 512]:
                        E.tensor_scalar_mul(o, i, SCL_A),
                        reads=[bank], writes=[bS])
            mk.emit("vector", lambda E, o=zst[:, 64 + s:65 + s],
                    i=cS[:, s * 512:(s + 1) * 512]:
                    E.reduce_max(o, i, axis=AX.X),
                    reads=[bS], writes=[bz])
        if a == 1:
            mk.emit("vector", lambda E, o=zst[:, 0:1], i=zst[:, 64:64 + nslab]:
                    E.reduce_max(o, i, axis=AX.X, negate=True),
                    reads=[bz], writes=[bz])
            mk.emit("scalar", lambda E, o=cE_t[:, 0:ext], i=cS[:, 0:ext],
                    m=zst[:, 0:1], acc=zst[:, 1:2]:
                    E.activation(o, i, AF.Exp, bias=m, scale=1.0, accum_out=acc),
                    reads=[bS, bz], writes=[bE, bz])
            mk.emit("vector", lambda E, o=zst[:, 2:3], i=zst[:, 1:2]:
                    E.reciprocal(o, i), reads=[bz], writes=[bz])
            mk.emit("vector", lambda E, o=zst[:, 3:4], i=zst[:, 2:3],
                    r=ctx["lamb"][:, h:h + 1]:
                    E.tensor_scalar_mul(o, i, r),
                    reads=[bz, B("lamb")], writes=[bz])
        else:
            mk.emit("vector", lambda E, o=zst[:, 4:5], i=zst[:, 64:64 + nslab]:
                    E.reduce_max(o, i, axis=AX.X, negate=True),
                    reads=[bz], writes=[bz])
            for s in range(nslab):
                mk.emit("scalar", lambda E, o=cE0[:, s * 512:(s + 1) * 512],
                        i=cS[:, s * 512:(s + 1) * 512],
                        m=zst[:, 4:5], acc=zst[:, 16 + s:17 + s]:
                        E.activation(o, i, AF.Exp, bias=m, scale=1.0, accum_out=acc),
                        reads=[bS, bz], writes=[bS, bz])
            mk.emit("vector", lambda E, o=zst[:, 5:6], i=zst[:, 16:16 + nslab]:
                    E.reduce_sum(o, i, axis=AX.X), reads=[bz], writes=[bz])
            mk.emit("vector", lambda E, o=zst[:, 6:7], i=zst[:, 5:6]:
                    E.reciprocal(o, i), reads=[bz], writes=[bz])
    bd0, bt0, bpps, bpts = B("ad0"), B("at0"), B("apps"), B("apts")
    bpv = bq[2]

    def _ptevac_pv(s):
        par = s % 2
        pt = ptA if par == 0 else ptB
        bpt = bptA if par == 0 else bptB
        mk.emit("vector", lambda E, o=pts[:, par * 512:(par + 1) * 512], i=pt[:, 0:512]:
                E.tensor_copy(o, i), reads=[bpt], writes=[bpts])
        for c4 in range(4):
            gblk = s * 4 + c4
            mk.emit("tensor", lambda E, o=pBig[:, 1024:1152],
                    l=cV_t[:, gblk * 128:(gblk + 1) * 128],
                    r=pts[:, par * 512 + c4 * 128: par * 512 + (c4 + 1) * 128],
                    st_=(s == 0 and c4 == 0), sp=(s == nslab - 1 and c4 == 3):
                    E.matmul(o, l, r, start=st_, stop=sp),
                    reads=[B("aV"), bpts], writes=[bpv], inc=(c4 == 3))

    for s in range(nslab):
        par = s % 2
        mk.emit("gpsimd", lambda E, o=t0[:], i=cE_t[:, s * 512:(s + 1) * 512],
                r=zst[:, 3:4]:
                E.tensor_scalar_mul(o, i, r),
                reads=[bE, bz], writes=[bt0])
        mk.emit("vector", lambda E, o=d0[:], i0=cE0[:, s * 512:(s + 1) * 512],
                r=zst[:, 6:7], i1=t0[:]:
                E.scalar_tensor_tensor(o, i0, r, i1, ALU.mult, ALU.subtract),
                reads=[bS, bz, bt0], writes=[bd0])
        if s >= nslab - 2:
            tm0 = (s - (nslab - 2)) * 512
            mk.emit("vector", lambda E, o=d0[:], i0=d0[:], i1=tmask[:, tm0:tm0 + 512]:
                    E.tensor_add(o, i0, i1),
                    reads=[bd0, B("atm")], writes=[bd0])
        mk.emit("scalar", lambda E, o=pps[:, par * 512:(par + 1) * 512], i=d0[:],
                acc=zst[:, 32 + s:33 + s]:
                E.activation(o, i, AF.Exp, bias=0.0, scale=1.0, accum_out=acc),
                reads=[bd0], writes=[bpps, bz])
        pt = ptA if par == 0 else ptB
        bpt = bptA if par == 0 else bptB
        for c4 in range(4):
            mk.emit("tensor", lambda E, o=pt[:, c4 * 128:(c4 + 1) * 128],
                    i=pps[:, par * 512 + c4 * 128: par * 512 + (c4 + 1) * 128],
                    idn=ctx["ident_bf"]:
                    E.transpose(o, i, idn),
                    reads=[bpps, B("ident_bf")], writes=[bpt], inc=(c4 == 3))
        if s > 0:
            _ptevac_pv(s - 1)
    _ptevac_pv(nslab - 1)
    mk.emit("vector", lambda E, o=zst[:, 7:8], i=zst[:, 32:32 + nslab]:
            E.reduce_sum(o, i, axis=AX.X), reads=[bz], writes=[bz])
    mk.emit("vector", lambda E, o=zst[:, 8:9], i=zst[:, 7:8]:
            E.reciprocal(o, i), reads=[bz], writes=[bz])
    bae, bae2, bdump, baebf = B("aae"), B("aae2"), B("adump"), B("aaebf")
    mk.emit("vector", lambda E, o=ae[:], i=pBig[:, 1024:1152]:
            E.tensor_copy(o, i), reads=[bpv], writes=[bae])
    mk.emit("tensor", lambda E, o=pBig[:, 1536:1664], i=ae[:], idn=ctx["ident_f32"][:]:
            E.transpose(o, i, idn),
            reads=[bae, B("ident_f32")], writes=[bq[3]])
    mk.emit("vector", lambda E, o=ae2[:], i=pBig[:, 1536:1664], r=zst[:, 8:9]:
            E.tensor_scalar_mul(o, i, r),
            reads=[bq[3], bz], writes=[bae2])
    mk.emit("scalar", lambda E, o=dump[:], i=ae2[:], acc=zst[:, 48:49]:
            E.activation(o, i, AF.Square, bias=0.0, scale=1.0, accum_out=acc),
            reads=[bae2], writes=[bdump, bz])
    mk.emit("vector", lambda E, o=zst[:, 51:52], i=zst[:, 48:49]:
            E.tensor_scalar(o, i, 1.0 / 128.0, EPS_RMS, ALU.mult, ALU.add),
            reads=[bz], writes=[bz])
    mk.emit("scalar", lambda E, o=zst[:, 49:50], i=zst[:, 51:52]:
            E.activation(o, i, AF.Ln, bias=0.0, scale=1.0),
            reads=[bz], writes=[bz])
    mk.emit("scalar", lambda E, o=zst[:, 50:51], i=zst[:, 49:50]:
            E.activation(o, i, AF.Exp, bias=0.0, scale=-0.5),
            reads=[bz], writes=[bz])
    mk.emit("vector", lambda E, o=ae2[:], i=ae2[:], r=zst[:, 50:51]:
            E.tensor_scalar(o, i, r, 0.8, ALU.mult, ALU.mult),
            reads=[bae2, bz], writes=[bae2])
    mk.emit("vector", lambda E, o=aebf[:], i0=ae2[:], i1=ctx["rmsw_sb"][:]:
            E.tensor_mul(o, i0, i1),
            reads=[bae2, B("rmsw_sb")], writes=[baebf])
    mk.emit("tensor", lambda E, o=ptA[:, 512:640], i=aebf[:], idn=ctx["ident_bf"]:
            E.transpose(o, i, idn),
            reads=[baebf, B("ident_bf")], writes=[bptA])
    mk.emit("vector", lambda E,
            o=ctx["c_att"][:, h * 1024 + j * 128: h * 1024 + (j + 1) * 128],
            i=ptA[:, 512:640]:
            E.tensor_copy(o, i), reads=[bptA], writes=[B(f"att{h}")])


def _phase_wo(ctx, scope, w_ap, res_ap, res_buf, dst_ap, dst_buf, q_store,
              bias_sb=None, bias_buf=None, tag="wo"):
    nc, mk, B, alloc = ctx["nc"], ctx["mk"], ctx["B"], ctx["alloc"]
    es = scope
    wob = alloc(f"{tag}_w", (128, 2 * 512), BF16, es)
    xrb = alloc(f"{tag}_x", (128, 2 * 512), F32, es)
    rob = alloc(f"{tag}_o", (128, 2 * 512), F32, es)
    pw = [es.enter_context(nc.psum_tensor(f"{tag}_p{i}", [128, 512], F32))
          for i in range(8)]
    q_w = mk.new_q(f"{tag}w", "gpsimd")
    q_x = mk.new_q(f"{tag}x", "sync")
    c_att = ctx["c_att"]
    batt = [B(f"att{h}") for h in range(16)]
    for ns in range(4):
        for k in range(16):
            slot = k % 2
            bw = B(f"wo_w{slot}")
            mk.dma(q_w, wob[:, slot * 512:(slot + 1) * 512],
                   w_ap[k * 128:(k + 1) * 128, ns * 512:(ns + 1) * 512],
                   writes=[bw])
            for tt in range(NT):
                mk.emit("tensor", lambda E, o=pw[tt][:, 0:512],
                        l=c_att[:, k * 1024 + tt * 128: k * 1024 + (tt + 1) * 128],
                        r=wob[:, slot * 512:(slot + 1) * 512],
                        s=(k == 0), st=(k == 15):
                        E.matmul(o, l, r, start=s, stop=st),
                        reads=[batt[k], bw], writes=[B(f"wo_p{tt}")],
                        inc=(tt == NT - 1) or (k == 15))
        for tt in range(NT):
            slot = tt % 2
            bx = B(f"wo_x{slot}")
            mk.dma(q_x, xrb[:, slot * 512:(slot + 1) * 512],
                   res_ap[tt * 128:(tt + 1) * 128, ns * 512:(ns + 1) * 512],
                   reads=[res_buf], writes=[bx])
            bo = B(f"wo_o{slot}")
            if bias_sb is not None:
                mk.emit("vector", lambda E, o=rob[:, slot * 512:(slot + 1) * 512],
                        i0=pw[tt][:, 0:512], i1=bias_sb[:, ns * 512:(ns + 1) * 512]:
                        E.tensor_add(o, i0, i1),
                        reads=[B(f"wo_p{tt}"), bias_buf], writes=[bo])
                mk.emit("vector", lambda E, o=rob[:, slot * 512:(slot + 1) * 512],
                        i0=rob[:, slot * 512:(slot + 1) * 512],
                        i1=xrb[:, slot * 512:(slot + 1) * 512]:
                        E.tensor_add(o, i0, i1),
                        reads=[bo, bx], writes=[bo])
            else:
                mk.emit("vector", lambda E, o=rob[:, slot * 512:(slot + 1) * 512],
                        i0=pw[tt][:, 0:512], i1=xrb[:, slot * 512:(slot + 1) * 512]:
                        E.tensor_add(o, i0, i1),
                        reads=[B(f"wo_p{tt}"), bx], writes=[bo])
            mk.dma(q_store, dst_ap[tt * 128:(tt + 1) * 128, ns * 512:(ns + 1) * 512],
                   rob[:, slot * 512:(slot + 1) * 512],
                   reads=[bo], writes=[dst_buf])


def _phase_ffn(ctx, scope, gat, src_ap, src_buf, dst_ap, dst_buf, q_store, tag):
    nc, mk, B, T, alloc = ctx["nc"], ctx["mk"], ctx["B"], ctx["T"], ctx["alloc"]
    es = scope
    acc = alloc(f"{tag}_acc", (128, NT * 2048), F32, es)
    hb = alloc(f"{tag}_h", (128, NT * 768), BF16, es)
    wg = alloc(f"{tag}_wg", (128, 2 * 8192), BF16, es)
    wd = alloc(f"{tag}_wd", (128, 2 * 3072), BF16, es)
    st_ = alloc(f"{tag}_silu", (128, 256), F32, es)
    pg = es.enter_context(nc.psum_tensor(f"{tag}_pg", [128, 1024], F32))
    pd = es.enter_context(nc.psum_tensor(f"{tag}_pd", [128, 1024], F32))
    ptA = es.enter_context(nc.psum_tensor(f"{tag}_ptA", [128, 1024], BF16))
    ptB = es.enter_context(nc.psum_tensor(f"{tag}_ptB", [128, 1024], BF16))
    q_a = mk.new_q(f"{tag}a", "sync")
    q_wg = mk.new_q(f"{tag}wg", "sync")
    q_wd = mk.new_q(f"{tag}wd", "sync")
    bacc = B("ffn_acc")
    bh = B("ffn_h")
    bcomb = B("comb")
    mk.dma(q_a, acc[:].rearrange("p (t d) -> p t d", d=2048),
           src_ap.rearrange("(t p) d -> p t d", p=128),
           reads=[src_buf], writes=[bacc])
    for e in range(NEXP):
        row0 = (e // 4) * 8192 + (e % 4) * 2048
        row0d = (e // 4) * 3072 + (e % 4) * 768
        for jb in range(3):
            slot = (e * 3 + jb) % 2
            bwg = B(f"ffn_wg{slot}")
            for half in range(2):
                src = T["gupT_all"][row0:row0 + 2048,
                                    half * 768 + jb * 256: half * 768 + (jb + 1) * 256]
                dst = wg[:, slot * 8192: (slot + 1) * 8192].rearrange(
                    "p (k c) -> p k c", c=512)[:, :, half * 256:(half + 1) * 256]
                mk.dma(q_wg, dst, src.rearrange("(k p) c -> p k c", p=128),
                       reads=[B("gupT_all")], writes=[bwg])
            for tt in range(NT):
                par = tt % 2
                bpg = B(f"ffn_pg{par}")
                for k in range(16):
                    mk.emit("tensor", lambda E, o=pg[:, par * 512:(par + 1) * 512],
                            l=c_lnT_slice(ctx, k, tt),
                            r=wg[:, slot * 8192 + k * 512: slot * 8192 + (k + 1) * 512],
                            s=(k == 0), st=(k == 15):
                            E.matmul(o, l, r, start=s, stop=st),
                            reads=[B(f"lnT{k}"), bwg], writes=[bpg], inc=(k == 15))
                mk.emit("scalar", lambda E, o=st_[:], i=pg[:, par * 512: par * 512 + 256]:
                        E.activation(o, i, AF.Silu, bias=0.0, scale=1.0),
                        reads=[bpg], writes=[B("ffn_silu")])
                mk.emit("vector", lambda E, o=hb[:, tt * 768 + jb * 256: tt * 768 + (jb + 1) * 256],
                        i0=st_[:], r=gat["comb"][:, tt * 32 + e: tt * 32 + e + 1],
                        i1=pg[:, par * 512 + 256: par * 512 + 512]:
                        E.scalar_tensor_tensor(o, i0, r, i1, ALU.mult, ALU.mult),
                        reads=[B("ffn_silu"), bcomb, bpg], writes=[bh])
        for tt in range(NT):
            for jc in range(6):
                pt = ptA if jc % 2 == 0 else ptB
                bpt = B(f"ffn_pt{jc % 2}")
                mk.emit("tensor", lambda E, o=pt[:, 0:128],
                        i=hb[:, tt * 768 + jc * 128: tt * 768 + (jc + 1) * 128],
                        idn=ctx["ident_bf"]:
                        E.transpose(o, i, idn),
                        reads=[bh, B("ident_bf")], writes=[bpt])
                mk.emit("vector", lambda E,
                        o=hb[:, tt * 768 + jc * 128: tt * 768 + (jc + 1) * 128],
                        i=pt[:, 0:128]:
                        E.tensor_copy(o, i), reads=[bpt], writes=[bh])
        for os in range(4):
            slot = (e * 4 + os) % 2
            bwd = B(f"ffn_wd{slot}")
            src = T["dwnT_all"][row0d:row0d + 768, os * 512:(os + 1) * 512]
            mk.dma(q_wd, wd[:, slot * 3072:(slot + 1) * 3072].rearrange(
                   "p (k c) -> p k c", c=512),
                   src.rearrange("(k p) c -> p k c", p=128),
                   reads=[B("dwnT_all")], writes=[bwd])
            for tt in range(NT):
                par = tt % 2
                bpd = B(f"ffn_pd{par}")
                for jc in range(6):
                    mk.emit("tensor", lambda E, o=pd[:, par * 512:(par + 1) * 512],
                            l=hb[:, tt * 768 + jc * 128: tt * 768 + (jc + 1) * 128],
                            r=wd[:, slot * 3072 + jc * 512: slot * 3072 + (jc + 1) * 512],
                            s=(jc == 0), st=(jc == 5):
                            E.matmul(o, l, r, start=s, stop=st),
                            reads=[bh, bwd], writes=[bpd], inc=(jc == 5))
                mk.emit("vector", lambda E,
                        o=acc[:, tt * 2048 + os * 512: tt * 2048 + (os + 1) * 512],
                        i0=acc[:, tt * 2048 + os * 512: tt * 2048 + (os + 1) * 512],
                        i1=pd[:, par * 512:(par + 1) * 512]:
                        E.tensor_add(o, i0, i1),
                        reads=[bacc, bpd], writes=[bacc])
    for tt in range(NT):
        mk.dma(q_store, dst_ap[tt * 128:(tt + 1) * 128, :],
               acc[:, tt * 2048:(tt + 1) * 2048],
               reads=[bacc], writes=[dst_buf])


def _build_phases(ctx, n_sub):
    nc, mk, B, T, alloc, es = (ctx["nc"], ctx["mk"], ctx["B"], ctx["T"],
                               ctx["alloc"], ctx["es"])
    q_store = mk.new_q("store", "scalar")

    ctx["rmsw_sb"] = alloc("rmsw_sb", (128, 128), F32)
    ctx["lamb"] = alloc("lamb", (128, 16), F32)
    mk.dma(ctx["q_const"], ctx["rmsw_sb"][:], T["rms_w_tile"], writes=[B("rmsw_sb")])
    with ExitStack() as sc:
        _prep_lamb(ctx, sc)
        mk.barrier()

    with ExitStack() as s1:
        ctx["cos2_sb"] = alloc("cos2_sb", (128, 8 * 256), F32, s1)
        ctx["sin2_sb"] = alloc("sin2_sb", (128, 8 * 256), F32, s1)
        ctx["c_att"] = alloc("c_att_a", (128, 16 * 1024), BF16, s1)
        mk.dma(ctx["q_const"], ctx["cos2_sb"][:].rearrange("p (t c) -> p t c", c=256),
               T["cos2"].rearrange("(t p) c -> p t c", p=128), writes=[B("cos2_sb")])
        mk.dma(ctx["q_const"], ctx["sin2_sb"][:].rearrange("p (t c) -> p t c", c=256),
               T["sin2"].rearrange("(t p) c -> p t c", p=128), writes=[B("sin2_sb")])
        with ExitStack() as sc:
            _emit_ln(ctx, T["x"], B("x_dram"), "ln1", scope=sc)
            mk.barrier()
        with ExitStack() as sc:
            _phase_kv_a(ctx, sc)
            _phase_stage0(ctx, sc)
            mk.barrier()
        with ExitStack() as sc:
            _phase_att_a(ctx, sc)
            mk.barrier()
        dst1 = T["out"] if n_sub == 1 else T["R1"]
        dbuf1 = B("out_dram") if n_sub == 1 else B("R1")
        with ExitStack() as sc:
            _phase_wo(ctx, sc, T["aWo"], T["x"], B("x_dram"), dst1, dbuf1,
                      q_store, tag="woa")
            mk.barrier()
    if n_sub == 1:
        mk.finalize(B("out_dram"))
        return

    with ExitStack() as s2:
        comb1 = alloc("comb_ln2", (128, NT * 32), F32, s2)
        with ExitStack() as sc:
            gat = _emit_ln(ctx, T["R1"], B("R1"), "ln2", with_gating=True, scope=sc)
            mk.barrier()
        dst2 = T["out"] if n_sub == 2 else T["R2"]
        dbuf2 = B("out_dram") if n_sub == 2 else B("R2")
        with ExitStack() as sc:
            _phase_ffn(ctx, sc, gat, T["R1"], B("R1"), dst2, dbuf2, q_store, "f1")
            mk.barrier()
    if n_sub == 2:
        mk.finalize(B("out_dram"))
        return

    with ExitStack() as sc:
        _emit_ln(ctx, T["R2"], B("R2"), "ln3", scope=sc)
        mk.barrier()
    with ExitStack() as sc:
        qb_all = alloc("qb_all", (128, 16 * 1024), BF16, sc)
        bo_b = alloc("bo_b", (128, 2048), F32, sc)
        ctx["c_att"] = alloc("c_att_b", (128, 16 * 1024), BF16, sc)
        with ExitStack() as sc2:
            _phase_qkv_b(ctx, sc2, qb_all, bo_b)
            mk.barrier()
        with ExitStack() as sc2:
            _phase_att_b(ctx, sc2, qb_all)
            mk.barrier()
        dst3 = T["out"] if n_sub == 3 else T["R3"]
        dbuf3 = B("out_dram") if n_sub == 3 else B("R3")
        with ExitStack() as sc2:
            _phase_wo(ctx, sc2, T["bWo"], T["R2"], B("R2"), dst3, dbuf3,
                      q_store, bias_sb=bo_b, bias_buf=B("bo_b"), tag="wob")
            mk.barrier()
    if n_sub == 3:
        mk.finalize(B("out_dram"))
        return

    with ExitStack() as s4:
        comb4 = alloc("comb_ln4", (128, NT * 32), F32, s4)
        with ExitStack() as sc:
            gat = _emit_ln(ctx, T["R3"], B("R3"), "ln4", with_gating=True, scope=sc)
            mk.barrier()
        with ExitStack() as sc:
            _phase_ffn(ctx, sc, gat, T["R3"], B("R3"), T["out"], B("out_dram"),
                       q_store, "f2")
            mk.barrier()
    mk.finalize(B("out_dram"))


def _prep_lamb(ctx, scope):
    nc, mk, B, T, alloc = ctx["nc"], ctx["mk"], ctx["B"], ctx["T"], ctx["alloc"]
    lt = alloc("lam_lt", (16, 128), F32, scope)
    ls = alloc("lam_ls", (16, 8), F32, scope)
    lrow = alloc("lam_row", (1, 16), F32, scope)
    q_l = mk.new_q("laml", "sync")
    p1 = scope.enter_context(nc.psum_tensor("lam_p", [128, 512], F32))
    bl = B("lam_lt"); bs = B("lam_ls"); bp = B("lam_p"); br = B("lam_row")
    for i, (a_, b_) in enumerate((("lq1", "lk1"), ("lq2", "lk2"))):
        t_a = alloc(f"lam_in{i}a", (16, 128), F32, scope)
        t_b = alloc(f"lam_in{i}b", (16, 128), F32, scope)
        mk.dma(q_l, t_a[:], T[a_], writes=[B(f"lam_in{i}a")])
        mk.dma(q_l, t_b[:], T[b_], writes=[B(f"lam_in{i}b")])
        mk.emit("vector", lambda E, o=lt[:], i0=t_a[:], i1=t_b[:]:
                E.tensor_mul(o, i0, i1),
                reads=[B(f"lam_in{i}a"), B(f"lam_in{i}b")], writes=[bl])
        mk.emit("vector", lambda E, o=ls[:, i:i + 1], i=lt[:]:
                E.reduce_sum(o, i, axis=AX.X), reads=[bl], writes=[bs])
        mk.emit("scalar", lambda E, o=ls[:, 2 + i:3 + i], i=ls[:, i:i + 1]:
                E.activation(o, i, AF.Exp, bias=0.0, scale=1.0),
                reads=[bs], writes=[bs])
    mk.emit("vector", lambda E, o=ls[:, 4:5], i0=ls[:, 2:3], i1=ls[:, 3:4]:
            E.tensor_sub(o, i0, i1), reads=[bs], writes=[bs])
    mk.emit("vector", lambda E, o=ls[:, 5:6], i=ls[:, 4:5]:
            E.tensor_scalar_add(o, i, LAM_INIT), reads=[bs], writes=[bs])
    mk.emit("tensor", lambda E, o=p1[0:1, 0:16], i=ls[0:16, 5:6],
            idn=ctx["ident_f32"][0:16, 0:16]:
            E.transpose(o, i, idn),
            reads=[bs, B("ident_f32")], writes=[bp])
    mk.emit("vector", lambda E, o=lrow[:], i=p1[0:1, 0:16]:
            E.tensor_copy(o, i), reads=[bp], writes=[br])
    mk.emit("tensor", lambda E, o=p1[:, 32:48], l=ctx["ones_row"][:], r=lrow[:]:
            E.matmul(o, l, r, start=True, stop=True),
            reads=[br, B("ones_row")], writes=[bp])
    mk.emit("vector", lambda E, o=ctx["lamb"][:], i=p1[:, 32:48]:
            E.tensor_copy(o, i), reads=[bp], writes=[B("lamb")])


def _phase_qkv_b(ctx, scope, qb_all, bo_b):
    nc, mk, B, T, alloc = ctx["nc"], ctx["mk"], ctx["B"], ctx["T"], ctx["alloc"]
    es = scope
    bia = alloc("b_bia", (16, 128), F32, es)
    biT = alloc("b_biT", (128, 3 * 16), F32, es)
    wtile = alloc("b_wt", (128, 2 * 384), BF16, es)
    kb_t = alloc("b_kt", (128, 1024), BF16, es)
    vb_t = alloc("b_vt", (128, 1024), BF16, es)
    vev = alloc("b_vev", (128, 256), BF16, es)
    bob_row = alloc("b_borow", (1, 2048), F32, es)
    pA = es.enter_context(nc.psum_tensor("b_pA", [128, 1024], F32))
    pB2 = es.enter_context(nc.psum_tensor("b_pB", [128, 1024], F32))
    pC = es.enter_context(nc.psum_tensor("b_pC", [128, 1024], F32))
    ptA = es.enter_context(nc.psum_tensor("b_ptA", [128, 1024], BF16))
    ptB = es.enter_context(nc.psum_tensor("b_ptB", [128, 1024], BF16))
    q_w = mk.new_q("bqkvw", "gpsimd")
    q_b = mk.new_q("bqkvb", "sync")
    q_st = mk.new_q("bqkvst", "scalar")
    psums = [pA, pB2, pC]
    bps = [B("b_pA"), B("b_pB"), B("b_pC")]
    bbiT = B("b_biT")
    for i, nm in enumerate(("bbq", "bbk", "bbv")):
        mk.dma(q_b, bia[:], T[nm], writes=[B("b_bia")])
        mk.emit("tensor", lambda E, o=pA[0:128, 0:16], i=bia[:], idn=ctx["ident_f32"][0:16, 0:16]:
                E.transpose(o, i, idn),
                reads=[B("b_bia"), B("ident_f32")], writes=[bps[0]])
        mk.emit("vector", lambda E, o=biT[:, i * 16:(i + 1) * 16], i=pA[0:128, 0:16]:
                E.tensor_copy(o, i), reads=[bps[0]], writes=[bbiT])
    mk.emit("vector", lambda E, o=biT[:, 0:16], i=biT[:, 0:16]:
            E.tensor_scalar_mul(o, i, SCL_B), reads=[bbiT], writes=[bbiT])
    mk.dma(q_b, bob_row[:], T["bbo"], writes=[B("b_borow")])
    for ns in range(4):
        mk.emit("tensor", lambda E, o=pC[:, 0:512], l=ctx["ones_row"][:],
                r=bob_row[:, ns * 512:(ns + 1) * 512]:
                E.matmul(o, l, r, start=True, stop=True),
                reads=[B("b_borow"), B("ones_row")], writes=[bps[2]])
        mk.emit("vector", lambda E, o=bo_b[:, ns * 512:(ns + 1) * 512], i=pC[:, 0:512]:
                E.tensor_copy(o, i), reads=[bps[2]], writes=[B("bo_b")])
    groups = [list(range(s, min(s + 3, 16))) for s in range(0, 16, 3)]
    for mat, w_nm, bi, out_kind in (("q", "bWq", 0, "q"), ("k", "bWk", 1, "k"),
                                    ("v", "bWv", 2, "v")):
        for grp in groups:
            for k in range(16):
                slot = k % 2
                bw = B(f"b_wt{slot}")
                cols = [m * 128 for m in grp]
                mk.dma(q_w, wtile[:, slot * 384: slot * 384 + 128 * len(grp)],
                       T[w_nm][k * 128:(k + 1) * 128, grp[0] * 128: grp[0] * 128 + 128 * len(grp)],
                       writes=[bw])
                for mi, m in enumerate(grp):
                    for hl in range(2):
                        mk.emit("tensor", lambda E, o=psums[mi][:, hl * 512:(hl + 1) * 512],
                                l=wtile[:, slot * 384 + mi * 128: slot * 384 + (mi + 1) * 128],
                                r=ctx["c_lnT"][:, k * 1024 + hl * 512: k * 1024 + (hl + 1) * 512],
                                s=(k == 0), st=(k == 15):
                                E.matmul(o, l, r, start=s, stop=st),
                                reads=[bw, B(f"lnT{k}")], writes=[bps[mi]],
                                inc=(mi == len(grp) - 1 and hl == 1) or (k == 15))
            for mi, m in enumerate(grp):
                if out_kind == "q":
                    mk.emit("vector", lambda E, o=qb_all[:, m * 1024:(m + 1) * 1024],
                            i=psums[mi][:, 0:1024], bb=biT[:, 0 * 16 + m: 0 * 16 + m + 1]:
                            E.tensor_scalar(o, i, SCL_B, bb, ALU.mult, ALU.add),
                            reads=[bps[mi], bbiT], writes=[B("qb_all")])
                elif out_kind == "k":
                    mk.emit("vector", lambda E, o=kb_t[:],
                            i=psums[mi][:, 0:1024], bb=biT[:, 16 + m: 16 + m + 1]:
                            E.tensor_scalar(o, i, bb, None, ALU.add),
                            reads=[bps[mi], bbiT], writes=[B("b_kt")])
                    mk.dma(q_st, T["kb_loc"][m * 128:(m + 1) * 128, :], kb_t[:],
                           reads=[B("b_kt")], writes=[B("kb_loc")])
                else:
                    mk.emit("vector", lambda E, o=vb_t[:],
                            i=psums[mi][:, 0:1024], bb=biT[:, 32 + m: 32 + m + 1]:
                            E.tensor_scalar(o, i, bb, None, ALU.add),
                            reads=[bps[mi], bbiT], writes=[B("b_vt")])
                    for c in range(8):
                        pt = ptA if c % 2 == 0 else ptB
                        bpt = B(f"b_pt{c % 2}")
                        mk.emit("tensor", lambda E, o=pt[:, 0:128],
                                i=vb_t[:, c * 128:(c + 1) * 128], idn=ctx["ident_bf"]:
                                E.transpose(o, i, idn),
                                reads=[B("b_vt"), B("ident_bf")], writes=[bpt])
                        bvev = B(f"b_vev{c % 2}")
                        mk.emit("vector", lambda E, o=vev[:, (c % 2) * 128:(c % 2) * 128 + 128],
                                i=pt[:, 0:128]:
                                E.tensor_copy(o, i), reads=[bpt], writes=[bvev])
                        mk.dma(q_st, T["vb_loc"][c * 128:(c + 1) * 128, m * 128:(m + 1) * 128],
                               vev[:, (c % 2) * 128:(c % 2) * 128 + 128],
                               reads=[bvev], writes=[B("vb_loc")])
    mk.collective(ctx["q_cc"], "AllGather", ALU.bypass,
                  [T["kb_loc"].opt(), T["vb_loc"].opt()],
                  [T["kb_all"].opt(), T["vb_all"].opt()],
                  reads=[B("kb_loc"), B("vb_loc")],
                  writes=[B("kb_all"), B("vb_all")])


def _phase_att_b(ctx, scope, qb_all):
    nc, mk, B, T, alloc = ctx["nc"], ctx["mk"], ctx["B"], ctx["T"], ctx["alloc"]
    es = scope
    cS_t = alloc("bS", (128, 8192), F32, es)
    cP_t = alloc("bP", (128, 2 * 8192), BF16, es)
    KB = alloc("bK", (128, 8192), BF16, es)
    VB = alloc("bV", (128, 8192), BF16, es)
    pts = alloc("bpts", (128, 1024), BF16, es)
    zst = alloc("bzst", (128, 40), F32, es)
    ae = alloc("bae", (128, 128), F32, es)
    ae2 = alloc("bae2", (128, 128), F32, es)
    aebf = alloc("baebf", (128, 128), BF16, es)
    psc = es.enter_context(nc.psum_tensor("b_psc", [128, 2048], F32))
    ppv = es.enter_context(nc.psum_tensor("b_ppv", [128, 512], F32))
    pep = es.enter_context(nc.psum_tensor("b_pep", [128, 512], F32))
    ptA = es.enter_context(nc.psum_tensor("b2_ptA", [128, 512], BF16))
    ptB = es.enter_context(nc.psum_tensor("b2_ptB", [128, 512], BF16))
    q_kv = mk.new_q("bkv", "sync")
    bsc = [B(f"b_psc{i}") for i in range(4)]
    bppv, bpep = B("b_ppv"), B("b_pep")
    bptA, bptB = B("b2_ptA"), B("b2_ptB")
    bS, bP, bKb, bVb = B("bS"), B("bP"), B("bK"), B("bV")
    bpts2, bz = B("bpts"), B("bzst")

    for pair in range(16):
        for r in range(NCORES):
            src = T["kb_all"][r * 2048 + pair * 128: r * 2048 + (pair + 1) * 128, :]
            src = src.rearrange("d (j t) -> d j t", t=128)
            dst = KB[:].rearrange("p (blk t) -> p blk t", t=128)[:, r::8, :]
            mk.dma(q_kv, dst, src, reads=[B("kb_all")], writes=[bKb])
            srcv = T["vb_all"][r * 1024:(r + 1) * 1024, pair * 128:(pair + 1) * 128]
            srcv = srcv.rearrange("(j t) d -> t j d", t=128)
            dstv = VB[:].rearrange("p (blk d) -> p blk d", d=128)[:, r::8, :]
            mk.dma(q_kv, dstv, srcv, reads=[B("vb_all")], writes=[bVb])
        for j in range(NT):
            for hh in range(2):
                for s in range(16):
                    bk = bsc[s % 4]
                    mk.emit("tensor", lambda E, o=psc[:, (s % 4) * 512:(s % 4) * 512 + 512],
                            l=qb_all[hh * 64:hh * 64 + 64,
                                     pair * 1024 + j * 128: pair * 1024 + (j + 1) * 128],
                            r=KB[hh * 64:hh * 64 + 64, s * 512:(s + 1) * 512]:
                            E.matmul(o, l, r, start=True, stop=True),
                            reads=[B("qb_all"), bKb], writes=[bk])
                    if s % 2 == 1:
                        eng = "vector" if (s // 2) % 2 == 0 else "scalar"
                        if eng == "vector":
                            mk.emit("vector", lambda E,
                                    o=cS_t[:, (s - 1) * 512:(s + 1) * 512],
                                    i=psc[:, ((s - 1) % 4) * 512:((s - 1) % 4) * 512 + 1024]:
                                    E.tensor_copy(o, i),
                                    reads=[bsc[(s - 1) % 4], bk], writes=[bS])
                        else:
                            mk.emit("scalar", lambda E,
                                    o=cS_t[:, (s - 1) * 512:(s + 1) * 512],
                                    i=psc[:, ((s - 1) % 4) * 512:((s - 1) % 4) * 512 + 1024]:
                                    E.activation(o, i, AF.Copy, bias=0.0, scale=1.0),
                                    reads=[bsc[(s - 1) % 4], bk], writes=[bS])
                        mk.emit("vector", lambda E, o=zst[:, 16 + 8 * hh + (s // 2):
                                                          17 + 8 * hh + (s // 2)],
                                i=cS_t[:, (s - 1) * 512:(s + 1) * 512]:
                                E.reduce_max(o, i, axis=AX.X),
                                reads=[bS], writes=[bz])
                mk.emit("vector", lambda E, o=zst[:, hh:hh + 1],
                        i=zst[:, 16 + 8 * hh:24 + 8 * hh]:
                        E.reduce_max(o, i, axis=AX.X, negate=True),
                        reads=[bz], writes=[bz])
                for ec in range(4):
                    mk.emit("scalar", lambda E,
                            o=cP_t[:, hh * 8192 + ec * 2048: hh * 8192 + (ec + 1) * 2048],
                            i=cS_t[:, ec * 2048:(ec + 1) * 2048], m=zst[:, hh:hh + 1],
                            acc=zst[:, 32 + 4 * hh + ec:33 + 4 * hh + ec]:
                            E.activation(o, i, AF.Exp, bias=m, scale=1.0, accum_out=acc),
                            reads=[bS, bz], writes=[bP, bz])
                mk.emit("vector", lambda E, o=zst[:, 2 + hh:3 + hh],
                        i=zst[:, 32 + 4 * hh:36 + 4 * hh]:
                        E.reduce_sum(o, i, axis=AX.X), reads=[bz], writes=[bz])
                mk.emit("vector", lambda E, o=zst[:, 4 + hh:5 + hh], i=zst[:, 2 + hh:3 + hh]:
                        E.reciprocal(o, i), reads=[bz], writes=[bz])
            def _b_ptevac_pv(hh, s):
                par = s % 2
                pt = ptA if par == 0 else ptB
                bpt = bptA if par == 0 else bptB
                peng = "vector" if par == 0 else "scalar"
                if peng == "vector":
                    mk.emit("vector", lambda E, o=pts[:, par * 512:par * 512 + 512],
                            i=pt[:, 0:512]:
                            E.tensor_copy(o, i), reads=[bpt], writes=[bpts2])
                else:
                    mk.emit("scalar", lambda E, o=pts[:, par * 512:par * 512 + 512],
                            i=pt[:, 0:512]:
                            E.activation(o, i, AF.Copy, bias=0.0, scale=1.0),
                            reads=[bpt], writes=[bpts2])
                for c4 in range(4):
                    gblk = s * 4 + c4
                    mk.emit("tensor", lambda E,
                            o=ppv[hh * 64:hh * 64 + 64, 0:128],
                            l=VB[:, gblk * 128 + hh * 64: gblk * 128 + hh * 64 + 64],
                            r=pts[:, par * 512 + c4 * 128: par * 512 + (c4 + 1) * 128],
                            st_=(s == 0 and c4 == 0),
                            sp=(s == 15 and c4 == 3),
                            tp=(0, hh * 64):
                            E.matmul(o, l, r, start=st_, stop=sp, tile_position=tp),
                            reads=[bVb, bpts2], writes=[bppv], inc=(c4 == 3))

            for hh in range(2):
                for s in range(16):
                    pt = ptA if s % 2 == 0 else ptB
                    bpt = bptA if s % 2 == 0 else bptB
                    for c4 in range(4):
                        mk.emit("tensor", lambda E, o=pt[:, c4 * 128:(c4 + 1) * 128],
                                i=cP_t[:, hh * 8192 + s * 512 + c4 * 128:
                                        hh * 8192 + s * 512 + (c4 + 1) * 128],
                                idn=ctx["ident_bf"]:
                                E.transpose(o, i, idn),
                                reads=[bP, B("ident_bf")], writes=[bpt], inc=(c4 == 3))
                    if s > 0:
                        _b_ptevac_pv(hh, s - 1)
                _b_ptevac_pv(hh, 15)
            mk.emit("vector", lambda E, o=ae[:], i=ppv[:, 0:128]:
                    E.tensor_copy(o, i), reads=[bppv], writes=[B("bae")])
            mk.emit("tensor", lambda E, o=pep[:, 0:128], i=ae[:], idn=ctx["ident_f32"][:]:
                    E.transpose(o, i, idn),
                    reads=[B("bae"), B("ident_f32")], writes=[bpep])
            for hh in range(2):
                mk.emit("vector", lambda E, o=ae2[:, hh * 64:(hh + 1) * 64],
                        i=pep[:, hh * 64:(hh + 1) * 64], r=zst[:, 4 + hh:5 + hh]:
                        E.tensor_scalar_mul(o, i, r),
                        reads=[bpep, bz], writes=[B("bae2")])
            mk.emit("vector", lambda E, o=aebf[:], i=ae2[:]:
                    E.tensor_copy(o, i), reads=[B("bae2")], writes=[B("baebf")])
            mk.emit("tensor", lambda E, o=ptA[:, 0:128], i=aebf[:], idn=ctx["ident_bf"]:
                    E.transpose(o, i, idn),
                    reads=[B("baebf"), B("ident_bf")], writes=[bptA])
            mk.emit("vector", lambda E,
                    o=ctx["c_att"][:, pair * 1024 + j * 128: pair * 1024 + (j + 1) * 128],
                    i=ptA[:, 0:128]:
                    E.tensor_copy(o, i), reads=[bptA], writes=[B(f"att{pair}")])



def kernel(**inputs):
    x = np.asarray(inputs["L0_attn_a_x"]).reshape(S, D)
    ident_bf_u16, ident_f32, ones_row = _host_shared_consts()
    rms_w_tile = np.ascontiguousarray(
        np.broadcast_to(np.asarray(inputs["L0_attn_a_rms_w"]), (128, 128)))
    bbq = np.ascontiguousarray(np.asarray(inputs["L0_attn_b_bq"]).reshape(16, 128))
    bbk = np.ascontiguousarray(np.asarray(inputs["L0_attn_b_bk"]).reshape(16, 128))
    bbv = np.ascontiguousarray(np.asarray(inputs["L0_attn_b_bv"]).reshape(16, 128))
    bbo = np.ascontiguousarray(np.asarray(inputs["L0_attn_b_bo"]).reshape(1, D))
    alpha = np.ascontiguousarray(np.asarray(inputs["L0_ffn_layer_alpha"]).reshape(1, 1))

    shared = {
        "aWq": np.ascontiguousarray(inputs["L0_attn_a_Wq"]),
        "aWk": np.ascontiguousarray(inputs["L0_attn_a_Wk"]),
        "aWv": np.ascontiguousarray(inputs["L0_attn_a_Wv"]),
        "aWo": np.ascontiguousarray(inputs["L0_attn_a_Wo"]),
        "lq1": np.ascontiguousarray(inputs["L0_attn_a_lq1"]),
        "lq2": np.ascontiguousarray(inputs["L0_attn_a_lq2"]),
        "lk1": np.ascontiguousarray(inputs["L0_attn_a_lk1"]),
        "lk2": np.ascontiguousarray(inputs["L0_attn_a_lk2"]),
        "rms_w_tile": rms_w_tile,
        "bWq": np.ascontiguousarray(inputs["L0_attn_b_Wq"]),
        "bWk": np.ascontiguousarray(inputs["L0_attn_b_Wk"]),
        "bWv": np.ascontiguousarray(inputs["L0_attn_b_Wv"]),
        "bWo": np.ascontiguousarray(inputs["L0_attn_b_Wo"]),
        "bbq": bbq, "bbk": bbk, "bbv": bbv, "bbo": bbo,
        "gw": np.ascontiguousarray(inputs["L0_ffn_gate_weight"]),
        "alpha": alpha,
        "ident_bf_u16": ident_bf_u16, "ident_f32": ident_f32, "ones_row": ones_row,
    }

    in_maps = []
    idxs = []
    for r in range(NCORES):
        cos2, sin2, tailmask = _host_consts(r)
        idx = _shard_idx(r)
        idxs.append(idx)
        m = dict(shared)
        m["x"] = np.ascontiguousarray(x[idx])
        m["gup_my"] = np.ascontiguousarray(
            np.asarray(inputs["L0_ffn_gate_up_proj"])[4 * r:4 * r + 4])
        m["dwn_my"] = np.ascontiguousarray(
            np.asarray(inputs["L0_ffn_down_proj"])[4 * r:4 * r + 4])
        m["cos2"] = cos2
        m["sin2"] = sin2
        m["tailmask"] = tailmask
        in_maps.append(m)

    nc = build_program(BUILD_SUBLAYERS)
    res = run_bass_kernel_spmd(nc, in_maps, core_ids=list(range(NCORES)))
    full = np.empty((S, D), dtype=np.float32)
    for r in range(NCORES):
        full[idxs[r]] = res.results[r]["out"]
    return full.reshape(1, S, D)


# version 07
# speedup vs baseline: 1.1868x; 1.1868x over previous
import numpy as np
from contextlib import ExitStack

import concourse.bass as bass
import concourse.mybir as mybir
from concourse.bass_utils import run_bass_kernel_spmd

F32 = mybir.dt.float32
BF16 = mybir.dt.bfloat16
AF = mybir.ActivationFunctionType
ALU = mybir.AluOpType
AX = mybir.AxisListType

NCORES = 8
S = 8192
D = 2048
TLOC = 1024
NT = 8
NEG = -1e30
SCL_A = 0.08838834764831845
SCL_B = 0.17677669529663687
EPS_LN = 1e-5
EPS_RMS = 1e-8
LAM_INIT = 0.2
TAU = 0.05
NEXP = 32
INTER = 768

BUILD_SUBLAYERS = 4

ENGS = ("sync", "vector", "scalar", "tensor", "gpsimd")



class _Prod:
    __slots__ = ("name", "sem", "cnt", "idx", "step", "engine", "is_cc")

    def __init__(self, name, sem, idx, step, engine=None):
        self.name = name
        self.sem = sem
        self.cnt = 0
        self.idx = idx
        self.step = step
        self.engine = engine
        self.is_cc = False


class Buf:
    __slots__ = ("name", "w", "readers", "dmap")

    def __init__(self, name=""):
        self.name = name
        self.w = None
        self.readers = {}
        self.dmap = None


class MK:
    def __init__(self, nc, es):
        self.nc = nc
        self.es = es
        self.seq = {e: [] for e in ENGS}
        self.prods = []
        self.eng = {}
        for e in ENGS:
            sem = es.enter_context(nc.semaphore(f"sem_{e}"))
            p = _Prod(e, sem, len(self.prods), 1, engine=e)
            self.prods.append(p)
            self.eng[e] = p
        self.waited = {e: {} for e in ENGS}
        self._nq = 0

    class _Queue:
        __slots__ = ("engine",)
        def __init__(self, engine):
            self.engine = engine

    def new_q(self, name, engine):
        return MK._Queue(engine)

    def _waits_for(self, ename, reads, writes):
        need = {}

        def add(p, c):
            if c > need.get(p, 0):
                need[p] = c

        for b in reads:
            if b.w is not None:
                add(*b.w)
        for b in writes:
            if b.w is not None:
                add(*b.w)
            for p, c in b.readers.items():
                add(p, c)
        w = self.waited[ename]
        out = []
        for p in sorted(need, key=lambda p: p.idx):
            c = need[p]
            assert c <= p.cnt, f"wait beyond emitted count on {p.name}: {c} > {p.cnt}"
            if w.get(p.idx, 0) < c:
                out.append((p.sem, c))
                w[p.idx] = c
        return out

    def _mark(self, prod, cnt, reads, writes):
        for b in reads:
            b.readers[prod] = cnt
        for b in writes:
            b.w = (prod, cnt)
            b.readers = {}

    def emit(self, ename, build, reads=(), writes=(), inc=True):
        for sem, c in self._waits_for(ename, reads, writes):
            self.seq[ename].append(("wait", sem, c))
        p = self.eng[ename]
        if not inc:
            self.seq[ename].append(("inst", build, None, 0))
            return p.cnt
        p.cnt += 1
        self.seq[ename].append(("inst", build, p.sem, 1))
        self._mark(p, p.cnt, reads, writes)
        return p.cnt

    def _buf_dma_prod(self, buf, is_cc=False):
        if buf.dmap is None:
            sem = self.es.enter_context(self.nc.semaphore(f"qb_{buf.name}"))
            buf.dmap = _Prod(f"qb_{buf.name}", sem, len(self.prods), 16)
            buf.dmap.is_cc = is_cc
            self.prods.append(buf.dmap)
        return buf.dmap

    def dma(self, q, out_ap, in_ap, reads=(), writes=(), **kw):
        assert len(writes) == 1, "dma needs exactly one target buf"
        p = self._buf_dma_prod(writes[0])
        for sem, c in self._waits_for(q.engine, reads, writes):
            self.seq[q.engine].append(("wait", sem, c))
        p.cnt += 16
        self.seq[q.engine].append(
            ("inst", lambda E, o=out_ap, i=in_ap, kw=kw: E.dma_start(out=o, in_=i, **kw),
             p.sem, 16))
        self._mark(p, p.cnt, reads, writes)
        return p.cnt

    def collective(self, q, kind, op, ins, outs, reads=(), writes=()):
        rg = [list(range(NCORES))]
        for i, (iap, oap) in enumerate(zip(ins, outs)):
            wr = [writes[i]]
            p = self._buf_dma_prod(writes[i], is_cc=True)
            p.step = 1
            for sem, c in self._waits_for(q.engine, reads, wr):
                self.seq[q.engine].append(("wait", sem, c))
            p.cnt += 1
            self.seq[q.engine].append(
                ("inst", lambda E, iap=iap, oap=oap: E.collective_compute(
                    kind, op, replica_groups=rg, ins=[iap], outs=[oap]),
                 p.sem, 1))
            self._mark(p, p.cnt, reads, wr)

    def barrier(self):
        snap = [(p, p.cnt) for p in self.prods if not p.is_cc]
        for e in ENGS:
            w = self.waited[e]
            for p, c in snap:
                if p is self.eng[e]:
                    continue
                if c > 0 and w.get(p.idx, 0) < c:
                    self.seq[e].append(("wait", p.sem, c))
                    w[p.idx] = c

    def finalize(self, out_buf):
        w = self.waited["sync"]
        p = out_buf.dmap
        assert p is not None and p.cnt > 0, "no output stores recorded"
        if w.get(p.idx, 0) < p.cnt:
            self.seq["sync"].append(("wait", p.sem, p.cnt))
            w[p.idx] = p.cnt

    def replay(self):
        with self.nc.Block() as blk:
            for ename in ENGS:
                items = self.seq[ename]

                def run(E, items=items):
                    for it in items:
                        if it[0] == "wait":
                            E.wait_ge(it[1], it[2])
                        else:
                            _, build, sem, inc = it
                            ins = build(E)
                            if sem is not None:
                                ins.then_inc(sem, inc)

                getattr(blk, ename)(run)



def _host_consts(core):
    import ml_dtypes
    pos = (np.arange(NT) * NCORES + core) * 128
    pos = (pos[:, None] + np.arange(128)[None, :]).reshape(-1).astype(np.float64)
    theta = 1.0 / (10000.0 ** (np.arange(0, 256, 2, dtype=np.float64) / 256.0))
    ang = pos[:, None] * theta[None, :]
    cos2 = np.repeat(np.cos(ang), 2, axis=1).astype(np.float32)
    sin2 = np.repeat(np.sin(ang), 2, axis=1).astype(np.float32)
    p = np.arange(128)[:, None]
    ck = np.arange(1024)[None, :]
    c = ck // 128
    kk = ck % 128
    keep = (c - core) * 128 + kk <= p
    tailmask = np.where(keep, 0.0, NEG).astype(np.float32)
    return cos2, sin2, tailmask


def _host_shared_consts():
    import ml_dtypes
    ident_bf_u16 = np.eye(128, dtype=ml_dtypes.bfloat16).view(np.uint16)
    ident_f32 = np.eye(128, dtype=np.float32)
    ones_row = np.ones((1, 128), dtype=np.float32)
    return ident_bf_u16, ident_f32, ones_row


def _shard_idx(core):
    return ((np.arange(NT)[:, None] * NCORES + core) * 128
            + np.arange(128)[None, :]).reshape(-1)



def build_program(n_sub=4):
    nc = bass.Bass("TRN2", target_bir_lowering=False, debug=False)
    es = ExitStack()
    mk = MK(nc, es)

    T = {}

    def din(name, shape, dt=F32):
        T[name] = nc.dram_tensor(name, list(shape), dt, kind="ExternalInput").ap()
        return T[name]

    def dint(name, shape, dt, addr_space="Local"):
        T[name] = nc.dram_tensor(name, list(shape), dt, kind="Internal",
                                 addr_space=addr_space).ap()
        return T[name]

    din("x", (TLOC, D))
    din("aWq", (D, 4096)); din("aWk", (D, 1024)); din("aWv", (D, 512)); din("aWo", (D, D))
    din("lq1", (16, 128)); din("lq2", (16, 128)); din("lk1", (16, 128)); din("lk2", (16, 128))
    din("rms_w_tile", (128, 128))
    din("bWq", (D, D)); din("bWk", (D, D)); din("bWv", (D, D)); din("bWo", (D, D))
    din("bbq", (16, 128)); din("bbk", (16, 128)); din("bbv", (16, 128)); din("bbo", (1, D))
    din("gw", (NEXP, D))
    din("gup_my", (4, 2 * INTER, D))
    din("dwn_my", (4, D, INTER))
    din("alpha", (1, 1))
    din("cos2", (TLOC, 256)); din("sin2", (TLOC, 256))
    din("tailmask", (128, 1024))
    din("ident_bf_u16", (128, 128), mybir.dt.uint16); din("ident_f32", (128, 128))
    din("ones_row", (1, 128))
    out_d = nc.dram_tensor("out", [TLOC, D], F32, kind="ExternalOutput").ap()
    T["out"] = out_d

    dint("R1", (TLOC, D), F32)
    dint("R2", (TLOC, D), F32)
    dint("R3", (TLOC, D), F32)
    dint("ka_loc", (1024, TLOC), BF16)
    dint("va_loc", (TLOC, 512), BF16)
    dint("ka_all", (1024 * NCORES, TLOC), BF16, addr_space="Shared")
    dint("va_all", (TLOC * NCORES, 512), BF16, addr_space="Shared")
    dint("kb_loc", (D, TLOC), BF16)
    dint("vb_loc", (TLOC, D), BF16)
    dint("kb_all", (D * NCORES, TLOC), BF16, addr_space="Shared")
    dint("vb_all", (TLOC * NCORES, D), BF16, addr_space="Shared")
    dint("gupT_loc", (4 * D, 2 * INTER), BF16)
    dint("dwnT_loc", (4 * INTER, D), BF16)
    dint("gupT_all", (4 * D * NCORES, 2 * INTER), BF16, addr_space="Shared")
    dint("dwnT_all", (4 * INTER * NCORES, D), BF16, addr_space="Shared")

    q_const = mk.new_q("const", "sync")
    q_cc = mk.new_q("cc", "gpsimd")

    sb = {}

    def alloc(name, shape, dt, scope=es):
        t = scope.enter_context(nc.sbuf_tensor(name, list(shape), dt))
        sb[name] = t
        return t

    c_lnT = alloc("c_lnT", (128, 16 * 1024), BF16)
    ident_bf_u16 = alloc("s_ident_bf", (128, 128), mybir.dt.uint16)
    ident_f32 = alloc("s_ident_f32", (128, 128), F32)
    ones_row = alloc("s_ones_row", (1, 128), F32)

    b = {}

    def B(name):
        if name not in b:
            b[name] = Buf(name)
        return b[name]

    mk.dma(q_const, ident_bf_u16[:], T["ident_bf_u16"], writes=[B("ident_bf")])
    mk.dma(q_const, ident_f32[:], T["ident_f32"], writes=[B("ident_f32")])
    mk.dma(q_const, ones_row[:], T["ones_row"], writes=[B("ones_row")])

    ident_bf_view = ident_bf_u16[:].bitcast(BF16)
    ctx = dict(nc=nc, mk=mk, es=es, T=T, sb=sb, B=B, alloc=alloc,
               q_const=q_const, q_cc=q_cc,
               c_lnT=c_lnT, ident_bf=ident_bf_view, ident_f32=ident_f32,
               ones_row=ones_row)

    for k in range(16):
        B(f"lnT{k}")

    _build_phases(ctx, n_sub)

    mk.replay()
    es.close()
    return nc


def _emit_ln(ctx, src_ap, src_buf, tag, with_gating=False, scope=None):
    nc, mk, B = ctx["nc"], ctx["mk"], ctx["B"]
    es = scope if scope is not None else ctx["es"]
    alloc = ctx["alloc"]
    c_lnT = ctx["c_lnT"]
    q_r = mk.new_q(f"lnr_{tag}", "sync")
    rt = [alloc(f"ln_rt{i}_{tag}", (128, D), F32, es) for i in range(2)]
    dump = alloc(f"ln_dump_{tag}", (128, D), F32, es)
    lnbf = alloc(f"ln_bf_{tag}", (128, D), BF16, es)
    stats = alloc(f"ln_stats_{tag}", (128, 8), F32, es)
    p2 = es.enter_context(nc.psum_tensor(f"ln_p2_{tag}", [128, 1024], F32))
    ptA = es.enter_context(nc.psum_tensor(f"ln_ptA_{tag}", [128, 1024], BF16))
    ptB = es.enter_context(nc.psum_tensor(f"ln_ptB_{tag}", [128, 1024], BF16))
    if with_gating:
        gat = _gating_prep(ctx, es, tag)

    for tt in range(NT):
        sl = tt % 2
        brt = B(f"ln_rt{sl}")
        mk.dma(q_r, rt[sl][:], src_ap[tt * 128:(tt + 1) * 128, :],
               reads=[src_buf], writes=[brt])
        bst = B("ln_stats")
        mk.emit("vector", lambda E, i=rt[sl][:], o=stats[:, 0:1]:
                E.reduce_sum(o, i, axis=AX.X), reads=[brt], writes=[bst])
        mk.emit("vector", lambda E, i=stats[:, 0:1], o=stats[:, 1:2]:
                E.tensor_scalar_mul(o, i, -1.0 / D), reads=[bst], writes=[bst])
        bdump = B("ln_dump")
        mk.emit("scalar", lambda E, i=rt[sl][:], o=dump[:], m=stats[:, 1:2], a=stats[:, 2:3]:
                E.activation(o, i, AF.Square, bias=m, scale=1.0, accum_out=a),
                reads=[brt, bst], writes=[bdump, bst])
        mk.emit("vector", lambda E, i=stats[:, 2:3], o=stats[:, 5:6]:
                E.tensor_scalar(o, i, 1.0 / D, EPS_LN, ALU.mult, ALU.add),
                reads=[bst], writes=[bst])
        mk.emit("scalar", lambda E, i=stats[:, 5:6], o=stats[:, 3:4]:
                E.activation(o, i, AF.Ln, bias=0.0, scale=1.0),
                reads=[bst], writes=[bst])
        mk.emit("scalar", lambda E, i=stats[:, 3:4], o=stats[:, 4:5]:
                E.activation(o, i, AF.Exp, bias=0.0, scale=-0.5),
                reads=[bst], writes=[bst])
        blnbf = B("ln_bf")
        mk.emit("vector", lambda E, i=rt[sl][:], o=lnbf[:], m=stats[:, 1:2], r=stats[:, 4:5]:
                E.tensor_scalar(o, i, m, r, ALU.add, ALU.mult),
                reads=[brt, bst], writes=[blnbf])
        for k in range(16):
            pt = ptA if k % 2 == 0 else ptB
            bpt = B(f"ln_pt{k % 2}")
            mk.emit("tensor", lambda E, o=pt[:, 0:128], i=lnbf[:, k * 128:(k + 1) * 128],
                    idn=ctx["ident_bf"]:
                    E.transpose(o, i, idn),
                    reads=[blnbf, B("ident_bf")], writes=[bpt])
            mk.emit("vector", lambda E, o=c_lnT[:, k * 1024 + tt * 128: k * 1024 + (tt + 1) * 128],
                    i=pt[:, 0:128]:
                    E.tensor_copy(o, i),
                    reads=[bpt], writes=[B(f"lnT{k}")])
        if with_gating:
            _gating_tile(ctx, gat, tt, rt[sl], brt, stats, bst, p2, tag)
    return gat if with_gating else None


def _gating_prep(ctx, es, tag):
    nc, mk, B, alloc = ctx["nc"], ctx["mk"], ctx["B"], ctx["alloc"]
    g = {}
    g["gwT"] = alloc(f"gwT_{tag}", (128, 16 * 32), F32, es)
    g["comb"] = ctx["sb"][f"comb_{tag}"][:] if f"comb_{tag}" in ctx["sb"] else None
    assert g["comb"] is not None, f"comb_{tag} must be preallocated by caller"
    g["ln32"] = alloc(f"ln32_{tag}", (128, D), F32, es)
    g["lnT32"] = alloc(f"lnT32_{tag}", (128, D), F32, es)
    g["gsc"] = alloc(f"gsc_{tag}", (128, 80), F32, es)
    g["alpha_b"] = alloc(f"alpha_b_{tag}", (128, 1), F32, es)
    g["gwsb"] = alloc(f"gwsb_{tag}", (32, D), F32, es)
    q_g = mk.new_q(f"gw_{tag}", "sync")
    g["q"] = q_g
    p4 = es.enter_context(nc.psum_tensor(f"g_p4_{tag}", [128, 512], F32))
    g["p4"] = p4
    mk.dma(q_g, g["gwsb"][:], ctx["T"]["gw"], writes=[B("gwsb")])
    for k in range(16):
        mk.emit("tensor", lambda E, o=p4[:, 0:32], i=g["gwsb"][:, k * 128:(k + 1) * 128],
                idn=ctx["ident_f32"][0:32, 0:32]:
                E.transpose(o, i, idn),
                reads=[B("gwsb"), B("ident_f32")], writes=[B("g_p4")])
        mk.emit("vector", lambda E, o=g["gwT"][:, k * 32:(k + 1) * 32], i=p4[:, 0:32]:
                E.tensor_copy(o, i),
                reads=[B("g_p4")], writes=[B("gwT")])
    al = alloc(f"alpha_sb_{tag}", (1, 1), F32, es)
    mk.dma(q_g, al[:], ctx["T"]["alpha"], writes=[B("alpha_sb")])
    mk.emit("tensor", lambda E, o=p4[:, 0:1], l=ctx["ones_row"][:], r=al[:]:
            E.matmul(o, l, r, start=True, stop=True),
            reads=[B("alpha_sb"), B("ones_row")], writes=[B("g_p4")])
    mk.emit("vector", lambda E, o=g["alpha_b"][:], i=p4[:, 0:1]:
            E.tensor_copy(o, i),
            reads=[B("g_p4")], writes=[B("alpha_b")])
    g["tag"] = tag
    return g


def _gating_tile(ctx, g, tt, rt, brt, stats, bst, p2, tag):
    nc, mk, B = ctx["nc"], ctx["mk"], ctx["B"]
    b32 = B("ln32")
    mk.emit("vector", lambda E, i=rt[:], o=g["ln32"][:], m=stats[:, 1:2], r=stats[:, 4:5]:
            E.tensor_scalar(o, i, m, r, ALU.add, ALU.mult),
            reads=[brt, bst], writes=[b32])
    bT32 = B("lnT32")
    for k in range(16):
        half = (k % 2) * 512
        bp = B(f"g_p2{k % 2}")
        mk.emit("tensor", lambda E, o=p2[:, half:half + 128], i=g["ln32"][:, k * 128:(k + 1) * 128],
                idn=ctx["ident_f32"][:]:
                E.transpose(o, i, idn),
                reads=[b32, B("ident_f32")], writes=[bp])
        mk.emit("vector", lambda E, o=g["lnT32"][:, k * 128:(k + 1) * 128], i=p2[:, half:half + 128]:
                E.tensor_copy(o, i),
                reads=[bp], writes=[bT32])
    p4 = g["p4"]
    bpl = B("g_logit")
    for k in range(16):
        mk.emit("tensor", lambda E, o=p4[:, 64:96], l=g["lnT32"][:, k * 128:(k + 1) * 128],
                r=g["gwT"][:, k * 32:(k + 1) * 32], s=(k == 0), st=(k == 15):
                E.matmul(o, l, r, start=s, stop=st),
                reads=[bT32, B("gwT")], writes=[bpl], inc=(k == 15))
    gs = ctx["sb"][f"gsc_{g['tag']}"]
    bgs = B("gsc")
    probs = gs[:, 0:32]
    top8 = gs[:, 32:40]
    mk.emit("vector", lambda E, o=gs[:, 40:41], i=p4[:, 64:96]:
            E.reduce_max(o, i, axis=AX.X, negate=True), reads=[bpl], writes=[bgs])
    mk.emit("scalar", lambda E, o=probs, i=p4[:, 64:96], m=gs[:, 40:41], a=gs[:, 41:42]:
            E.activation(o, i, AF.Exp, bias=m, scale=1.0, accum_out=a),
            reads=[bpl, bgs], writes=[bgs])
    mk.emit("vector", lambda E, o=gs[:, 42:43], i=gs[:, 41:42]:
            E.reciprocal(o, i), reads=[bgs], writes=[bgs])
    mk.emit("vector", lambda E, o=probs, i=probs, r=gs[:, 42:43]:
            E.tensor_scalar_mul(o, i, r), reads=[bgs], writes=[bgs])
    mk.emit("vector", lambda E, o=top8, i=probs:
            E.max(o, i), reads=[bgs], writes=[bgs])
    mk.emit("vector", lambda E, o=gs[:, 43:44], i=top8:
            E.reduce_sum(o, i, axis=AX.X), reads=[bgs], writes=[bgs])
    pa = gs[:, 48:80]
    mk.emit("vector", lambda E, o=gs[:, 44:45], i=gs[:, 43:44]:
            E.tensor_scalar_mul(o, i, TAU), reads=[bgs], writes=[bgs])
    mk.emit("vector", lambda E, o=pa, i=probs, r=ctx["sb"][f"alpha_b_{g['tag']}"][:]:
            E.tensor_scalar_mul(o, i, r), reads=[bgs, B("alpha_b")], writes=[bgs])
    mk.emit("vector", lambda E, o=pa, i=pa, r=gs[:, 44:45]:
            E.tensor_single_scalar(o, i, r, ALU.is_ge), reads=[bgs], writes=[bgs])
    mk.emit("vector", lambda E, o=gs[:, 45:46], i=top8[:, 0:1], r=ctx["sb"][f"alpha_b_{g['tag']}"][:]:
            E.tensor_scalar_mul(o, i, r), reads=[bgs, B("alpha_b")], writes=[bgs])
    mk.emit("vector", lambda E, o=gs[:, 45:46], i=gs[:, 45:46], r=gs[:, 44:45]:
            E.tensor_single_scalar(o, i, r, ALU.is_lt), reads=[bgs], writes=[bgs])
    fb = gs[:, 0:32]
    comb_t = g["comb"][:, tt * 32:(tt + 1) * 32]
    bcomb = B("comb")
    mk.emit("vector", lambda E, o=comb_t, i=probs, r=top8[:, 0:1]:
            E.tensor_single_scalar(o, i, r, ALU.is_equal), reads=[bgs], writes=[bcomb])
    mk.emit("vector", lambda E, o=comb_t, i=comb_t, r=gs[:, 45:46]:
            E.tensor_scalar_mul(o, i, r), reads=[bcomb, bgs], writes=[bcomb])
    mk.emit("vector", lambda E, o=pa, i0=pa, i1=comb_t:
            E.tensor_add(o, i0, i1), reads=[bgs, bcomb], writes=[bgs])
    mk.emit("vector", lambda E, o=comb_t, i=probs, r=top8[:, 7:8]:
            E.tensor_single_scalar(o, i, r, ALU.is_ge), reads=[bgs], writes=[bcomb])
    mk.emit("vector", lambda E, o=pa, i0=pa, i1=comb_t:
            E.tensor_mul(o, i0, i1), reads=[bgs, bcomb], writes=[bgs])
    mk.emit("vector", lambda E, o=comb_t, i0=probs, i1=pa:
            E.tensor_mul(o, i0, i1), reads=[bgs], writes=[bcomb])
    mk.emit("vector", lambda E, o=gs[:, 46:47], i=comb_t:
            E.reduce_sum(o, i, axis=AX.X), reads=[bcomb], writes=[bgs])
    mk.emit("vector", lambda E, o=gs[:, 46:47], i=gs[:, 46:47]:
            E.tensor_single_scalar(o, i, 1e-12, ALU.max), reads=[bgs], writes=[bgs])
    mk.emit("vector", lambda E, o=gs[:, 47:48], i=gs[:, 46:47]:
            E.reciprocal(o, i), reads=[bgs], writes=[bgs])
    mk.emit("vector", lambda E, o=comb_t, i=comb_t, r=gs[:, 47:48]:
            E.tensor_scalar_mul(o, i, r), reads=[bcomb, bgs], writes=[bcomb])


def _phase_kv_a(ctx, scope):
    nc, mk, B, T, alloc = ctx["nc"], ctx["mk"], ctx["B"], ctx["T"], ctx["alloc"]
    c_lnT = ctx["c_lnT"]
    es = scope
    wk = alloc("kv_wk", (128, 16 * 1024), BF16, es)
    wv = alloc("kv_wv", (128, 16 * 512), BF16, es)
    krt = alloc("kv_krt", (128, 1024), BF16, es)
    tco = alloc("kv_tcos", (128, 256), F32, es)
    tsi = alloc("kv_tsin", (128, 256), F32, es)
    vbf = alloc("kv_vbf", (128, 512), BF16, es)
    kev = alloc("kv_kev", (128, 256), BF16, es)
    p2a = es.enter_context(nc.psum_tensor("kv_p2a", [128, 1024], F32))
    p2b = es.enter_context(nc.psum_tensor("kv_p2b", [128, 1024], F32))
    p1 = es.enter_context(nc.psum_tensor("kv_p1", [128, 512], F32))
    ptA = es.enter_context(nc.psum_tensor("kv_ptA", [128, 1024], BF16))
    ptB = es.enter_context(nc.psum_tensor("kv_ptB", [128, 1024], BF16))
    ctx["kv_ptA"], ctx["kv_ptB"] = ptA, ptB
    q_w = mk.new_q("kvw", "gpsimd")
    q_st = mk.new_q("kvst", "scalar")

    for k in range(16):
        mk.dma(q_w, wk[:, k * 1024:(k + 1) * 1024], T["aWk"][k * 128:(k + 1) * 128, :],
               writes=[B("kv_wk")])
        mk.dma(q_w, wv[:, k * 512:(k + 1) * 512], T["aWv"][k * 128:(k + 1) * 128, :],
               writes=[B("kv_wv")])
    for tt in range(NT):
        p2 = p2a if tt % 2 == 0 else p2b
        bp2 = B(f"kv_p2{tt % 2}")
        for k in range(16):
            for hl in range(2):
                mk.emit("tensor", lambda E, o=p2[:, hl * 512:(hl + 1) * 512],
                        l=c_lnT[:, k * 1024 + tt * 128: k * 1024 + (tt + 1) * 128],
                        r=wk[:, k * 1024 + hl * 512: k * 1024 + (hl + 1) * 512],
                        s=(k == 0), st=(k == 15):
                        E.matmul(o, l, r, start=s, stop=st),
                        reads=[B(f"lnT{k}"), B("kv_wk")], writes=[bp2], inc=(k == 15))
        bkrt = B("kv_krt")
        for kh in range(4):
            c0 = kh * 256
            mk.emit("vector", lambda E, o=tco[:], i0=p2[:, c0:c0 + 256],
                    i1=ctx["cos2_sb"][:, tt * 256:(tt + 1) * 256]:
                    E.tensor_mul(o, i0, i1),
                    reads=[bp2, B("cos2_sb")], writes=[B("kv_tcos")])
            mk.emit("vector", lambda E, o=tsi[:], i0=p2[:, c0:c0 + 256],
                    i1=ctx["sin2_sb"][:, tt * 256:(tt + 1) * 256]:
                    E.tensor_mul(o, i0, i1),
                    reads=[bp2, B("sin2_sb")], writes=[B("kv_tsin")])
            mk.emit("vector", lambda E, o=krt[:, c0:c0 + 256:2], i0=tco[:, 0:256:2],
                    i1=tsi[:, 1:256:2]:
                    E.tensor_sub(o, i0, i1),
                    reads=[B("kv_tcos"), B("kv_tsin")], writes=[bkrt])
            mk.emit("vector", lambda E, o=krt[:, c0 + 1:c0 + 256:2], i0=tsi[:, 0:256:2],
                    i1=tco[:, 1:256:2]:
                    E.tensor_add(o, i0, i1),
                    reads=[B("kv_tcos"), B("kv_tsin")], writes=[bkrt])
        for c in range(8):
            pt = ptA if c % 2 == 0 else ptB
            bpt = B(f"kv_pt{c % 2}")
            mk.emit("tensor", lambda E, o=pt[:, 0:128], i=krt[:, c * 128:(c + 1) * 128],
                    idn=ctx["ident_bf"]:
                    E.transpose(o, i, idn),
                    reads=[bkrt, B("ident_bf")], writes=[bpt])
            bke = B(f"kv_kev{c % 2}")
            mk.emit("vector", lambda E, o=kev[:, (c % 2) * 128:(c % 2) * 128 + 128], i=pt[:, 0:128]:
                    E.tensor_copy(o, i),
                    reads=[bpt], writes=[bke])
            mk.dma(q_st, T["ka_loc"][c * 128:(c + 1) * 128, tt * 128:(tt + 1) * 128],
                   kev[:, (c % 2) * 128:(c % 2) * 128 + 128],
                   reads=[bke], writes=[B("ka_loc")])
        bp1 = B("kv_p1")
        for k in range(16):
            mk.emit("tensor", lambda E, o=p1[:, 0:512],
                    l=c_lnT[:, k * 1024 + tt * 128: k * 1024 + (tt + 1) * 128],
                    r=wv[:, k * 512:(k + 1) * 512], s=(k == 0), st=(k == 15):
                    E.matmul(o, l, r, start=s, stop=st),
                    reads=[B(f"lnT{k}"), B("kv_wv")], writes=[bp1], inc=(k == 15))
        mk.emit("vector", lambda E, o=vbf[:], i=p1[:, 0:512]:
                E.tensor_copy(o, i), reads=[bp1], writes=[B("kv_vbf")])
        mk.dma(q_st, T["va_loc"][tt * 128:(tt + 1) * 128, :], vbf[:],
               reads=[B("kv_vbf")], writes=[B("va_loc")])
    mk.collective(ctx["q_cc"], "AllGather", ALU.bypass,
                  [T["ka_loc"].opt(), T["va_loc"].opt()],
                  [T["ka_all"].opt(), T["va_all"].opt()],
                  reads=[B("ka_loc"), B("va_loc")],
                  writes=[B("ka_all"), B("va_all")])


def _phase_stage0(ctx, scope, core_experts=4):
    nc, mk, B, T, alloc = ctx["nc"], ctx["mk"], ctx["B"], ctx["T"], ctx["alloc"]
    es = scope
    wt = alloc("s0_wt", (128, 2 * 2048), BF16, es)
    wev = alloc("s0_wev", (128, 2 * 128), BF16, es)
    ptA, ptB = ctx["kv_ptA"], ctx["kv_ptB"]
    q_l = mk.new_q("s0l", "gpsimd")
    q_s = mk.new_q("s0s", "scalar")

    def transpose_store(src_view, n_f, dst_rowbase, dst_col, dst_ap, slot, tag):
        for f in range(n_f):
            pt = ptA if f % 2 == 0 else ptB
            bpt = B(f"kv_pt{f % 2}")
            mk.emit("tensor", lambda E, o=pt[:, 0:128], i=src_view[:, f * 128:(f + 1) * 128],
                    idn=ctx["ident_bf"]:
                    E.transpose(o, i, idn),
                    reads=[B(f"s0_wt{slot}"), B("ident_bf")], writes=[bpt])
            bev = B(f"s0_wev{f % 2}")
            mk.emit("vector", lambda E, o=wev[:, (f % 2) * 128:(f % 2) * 128 + 128], i=pt[:, 0:128]:
                    E.tensor_copy(o, i), reads=[bpt], writes=[bev])
            mk.dma(q_s, dst_ap[dst_rowbase + f * 128: dst_rowbase + (f + 1) * 128,
                              dst_col:dst_col + 128],
                   wev[:, (f % 2) * 128:(f % 2) * 128 + 128],
                   reads=[bev], writes=[B(tag)])

    for e4 in range(core_experts):
        for ib in range(12):
            slot = ib % 2
            bwt = B(f"s0_wt{slot}")
            mk.dma(q_l, wt[:, slot * 2048:(slot + 1) * 2048],
                   T["gup_my"][e4, ib * 128:(ib + 1) * 128, :],
                   writes=[bwt])
            transpose_store(wt[:, slot * 2048:(slot + 1) * 2048], 16,
                            e4 * 2048, ib * 128, T["gupT_loc"], slot, "gupT_loc")
        for ob in range(16):
            slot = ob % 2
            bwt = B(f"s0_wt{slot}")
            mk.dma(q_l, wt[:, slot * 2048: slot * 2048 + 768],
                   T["dwn_my"][e4, ob * 128:(ob + 1) * 128, :],
                   writes=[bwt])
            transpose_store(wt[:, slot * 2048: slot * 2048 + 768], 6,
                            e4 * 768, ob * 128, T["dwnT_loc"], slot, "dwnT_loc")
    mk.collective(ctx["q_cc"], "AllGather", ALU.bypass,
                  [T["gupT_loc"].opt(), T["dwnT_loc"].opt()],
                  [T["gupT_all"].opt(), T["dwnT_all"].opt()],
                  reads=[B("gupT_loc"), B("dwnT_loc")],
                  writes=[B("gupT_all"), B("dwnT_all")])


def _phase_att_a(ctx, outer, n_heads=16):
    nc, mk, B, T, alloc = ctx["nc"], ctx["mk"], ctx["B"], ctx["T"], ctx["alloc"]
    es = outer
    cS_t = alloc("aS", (128, 8192), F32, es)
    cS = cS_t[:]
    cE0 = cS_t[:].bitcast(BF16)[:, 0:8192]
    cE_t = alloc("aE", (128, 8192), BF16, es)
    cK_t = alloc("aK", (128, 2 * 8192), BF16, es)
    cV_t = alloc("aV", (128, 8192), BF16, es)
    qfm = alloc("aqfm", (128, 2 * 1024), BF16, es)
    wq = alloc("awq", (128, 2 * 256), BF16, es)
    tmask = alloc("atm", (128, 1024), F32, es)
    zst = alloc("azst", (128, 96), F32, es)
    d0 = alloc("ad0", (128, 512), F32, es)
    t0 = alloc("at0", (128, 512), F32, es)
    pps = alloc("apps", (128, 1024), BF16, es)
    pts = alloc("apts", (128, 1024), BF16, es)
    tco = alloc("atcos", (128, 256), F32, es)
    tsi = alloc("atsin", (128, 256), F32, es)
    qrt = alloc("aqrt", (128, 256), BF16, es)
    ae = alloc("aae", (128, 128), F32, es)
    ae2 = alloc("aae2", (128, 128), F32, es)
    dump = alloc("adump", (128, 128), F32, es)
    aebf = alloc("aaebf", (128, 128), BF16, es)
    pBig = es.enter_context(nc.psum_tensor("a_pbig", [128, 2048], F32))
    ptA = es.enter_context(nc.psum_tensor("a_ptA", [128, 1024], BF16))
    ptB = es.enter_context(nc.psum_tensor("a_ptB", [128, 1024], BF16))
    q_k = mk.new_q("akv", "sync")
    q_wq = mk.new_q("awq", "gpsimd")

    mk.dma(mk.new_q("atm", "sync"), tmask[:], T["tailmask"], writes=[B("atm")])
    bq = [B(f"a_pb{i}") for i in range(4)]
    bptA, bptB = B("a_ptA"), B("a_ptB")
    c_att = ctx["c_att"]

    for g in range(4):
        for a in range(2):
            bK = B(f"aK{a}")
            for r in range(NCORES):
                src = T["ka_all"][r * 1024 + (g * 2 + a) * 128:
                                  r * 1024 + (g * 2 + a) * 128 + 128, :]
                src = src.rearrange("d (j t) -> d j t", t=128)
                dst = cK_t[:, a * 8192:(a + 1) * 8192].rearrange(
                    "p (blk t) -> p blk t", t=128)[:, r::8, :]
                mk.dma(q_k, dst, src, reads=[B("ka_all")], writes=[bK])
        bV = B("aV")
        for r in range(NCORES):
            src = T["va_all"][r * 1024:(r + 1) * 1024, g * 128:(g + 1) * 128]
            src = src.rearrange("(j t) d -> t j d", t=128)
            dst = cV_t[:].rearrange("p (blk d) -> p blk d", d=128)[:, r::8, :]
            mk.dma(q_k, dst, src, reads=[B("va_all")], writes=[bV])

        for h4 in range(4):
            h = g * 4 + h4
            if h >= n_heads:
                continue
            bqfm = B("aqfm")
            for k in range(16):
                slot = k % 2
                bwq = B(f"awq{slot}")
                mk.dma(q_wq, wq[:, slot * 256:(slot + 1) * 256],
                       T["aWq"][k * 128:(k + 1) * 128, h * 256:(h + 1) * 256],
                       writes=[bwq])
                for tt in range(NT):
                    mk.emit("tensor", lambda E, o=pBig[:, tt * 256:(tt + 1) * 256],
                            l=c_lnT_slice(ctx, k, tt),
                            r=wq[:, slot * 256:(slot + 1) * 256],
                            s=(k == 0 and tt % 2 == 0), st=(k == 15 and tt % 2 == 1):
                            E.matmul(o, l, r, start=s, stop=st),
                            reads=[B(f"lnT{k}"), bwq], writes=[bq[tt // 2]],
                            inc=(tt == NT - 1) or (k == 15))
            for tt in range(NT):
                bnk = bq[tt // 2]
                mk.emit("vector", lambda E, o=tco[:], i0=pBig[:, tt * 256:(tt + 1) * 256],
                        i1=ctx["cos2_sb"][:, tt * 256:(tt + 1) * 256]:
                        E.tensor_mul(o, i0, i1),
                        reads=[bnk, B("cos2_sb")], writes=[B("atcos")])
                mk.emit("vector", lambda E, o=tsi[:], i0=pBig[:, tt * 256:(tt + 1) * 256],
                        i1=ctx["sin2_sb"][:, tt * 256:(tt + 1) * 256]:
                        E.tensor_mul(o, i0, i1),
                        reads=[bnk, B("sin2_sb")], writes=[B("atsin")])
                mk.emit("vector", lambda E, o=qrt[:, 0:256:2], i0=tco[:, 0:256:2],
                        i1=tsi[:, 1:256:2]:
                        E.tensor_sub(o, i0, i1),
                        reads=[B("atcos"), B("atsin")], writes=[B("aqrt")])
                mk.emit("vector", lambda E, o=qrt[:, 1:256:2], i0=tsi[:, 0:256:2],
                        i1=tco[:, 1:256:2]:
                        E.tensor_add(o, i0, i1),
                        reads=[B("atcos"), B("atsin")], writes=[B("aqrt")])
                for a in range(2):
                    pt = ptA if a == 0 else ptB
                    bpt = bptA if a == 0 else bptB
                    mk.emit("tensor", lambda E, o=pt[:, 0:128],
                            i=qrt[:, a * 128:(a + 1) * 128], idn=ctx["ident_bf"]:
                            E.transpose(o, i, idn),
                            reads=[B("aqrt"), B("ident_bf")], writes=[bpt])
                    mk.emit("vector", lambda E,
                            o=qfm[:, a * 1024 + tt * 128: a * 1024 + (tt + 1) * 128],
                            i=pt[:, 0:128]:
                            E.tensor_copy(o, i), reads=[bpt], writes=[bqfm])
            for j in range(NT):
                _att_a_unit(ctx, h, j, cS, cE0, cE_t, cK_t, cV_t, qfm, tmask,
                            zst, d0, t0, pps, pts, ae, ae2, dump, aebf,
                            pBig, ptA, ptB, bq, bptA, bptB)


def c_lnT_slice(ctx, k, tt):
    return ctx["c_lnT"][:, k * 1024 + tt * 128: k * 1024 + (tt + 1) * 128]


def _att_a_unit(ctx, h, j, cS, cE0, cE_t, cK_t, cV_t, qfm, tmask, zst, d0, t0,
                pps, pts, ae, ae2, dump, aebf, pBig, ptA, ptB, bq, bptA, bptB):
    nc, mk, B = ctx["nc"], ctx["mk"], ctx["B"]
    nslab = 2 * (j + 1)
    ext = nslab * 512
    bS, bE, bz = B("aS"), B("aE"), B("azst")
    for a in (1, 0):
        for s in range(nslab):
            bank = bq[s % 2]
            mk.emit("tensor", lambda E, o=pBig[:, (s % 2) * 512:(s % 2) * 512 + 512],
                    l=qfm[:, a * 1024 + j * 128: a * 1024 + (j + 1) * 128],
                    r=cK_t[:, a * 8192 + s * 512: a * 8192 + (s + 1) * 512]:
                    E.matmul(o, l, r, start=True, stop=True),
                    reads=[B("aqfm"), B(f"aK{a}")], writes=[bank])
            if s >= nslab - 2:
                tm0 = (s - (nslab - 2)) * 512
                mk.emit("vector", lambda E, o=cS[:, s * 512:(s + 1) * 512],
                        i0=pBig[:, (s % 2) * 512:(s % 2) * 512 + 512],
                        i1=tmask[:, tm0:tm0 + 512]:
                        E.scalar_tensor_tensor(o, i0, SCL_A, i1, ALU.mult, ALU.add),
                        reads=[bank, B("atm")], writes=[bS])
            elif a == 1:
                mk.emit("scalar", lambda E, o=cS[:, s * 512:(s + 1) * 512],
                        i=pBig[:, (s % 2) * 512:(s % 2) * 512 + 512]:
                        E.activation(o, i, AF.Copy, bias=0.0, scale=SCL_A),
                        reads=[bank], writes=[bS])
            else:
                mk.emit("vector", lambda E, o=cS[:, s * 512:(s + 1) * 512],
                        i=pBig[:, (s % 2) * 512:(s % 2) * 512 + 512]:
                        E.tensor_scalar_mul(o, i, SCL_A),
                        reads=[bank], writes=[bS])
            mk.emit("vector", lambda E, o=zst[:, 64 + s:65 + s],
                    i=cS[:, s * 512:(s + 1) * 512]:
                    E.reduce_max(o, i, axis=AX.X),
                    reads=[bS], writes=[bz])
        if a == 1:
            mk.emit("vector", lambda E, o=zst[:, 0:1], i=zst[:, 64:64 + nslab]:
                    E.reduce_max(o, i, axis=AX.X, negate=True),
                    reads=[bz], writes=[bz])
            mk.emit("scalar", lambda E, o=cE_t[:, 0:ext], i=cS[:, 0:ext],
                    m=zst[:, 0:1], acc=zst[:, 1:2]:
                    E.activation(o, i, AF.Exp, bias=m, scale=1.0, accum_out=acc),
                    reads=[bS, bz], writes=[bE, bz])
            mk.emit("vector", lambda E, o=zst[:, 2:3], i=zst[:, 1:2]:
                    E.reciprocal(o, i), reads=[bz], writes=[bz])
            mk.emit("vector", lambda E, o=zst[:, 3:4], i=zst[:, 2:3],
                    r=ctx["lamb"][:, h:h + 1]:
                    E.tensor_scalar_mul(o, i, r),
                    reads=[bz, B("lamb")], writes=[bz])
        else:
            mk.emit("vector", lambda E, o=zst[:, 4:5], i=zst[:, 64:64 + nslab]:
                    E.reduce_max(o, i, axis=AX.X, negate=True),
                    reads=[bz], writes=[bz])
            for s in range(nslab):
                mk.emit("scalar", lambda E, o=cE0[:, s * 512:(s + 1) * 512],
                        i=cS[:, s * 512:(s + 1) * 512],
                        m=zst[:, 4:5], acc=zst[:, 16 + s:17 + s]:
                        E.activation(o, i, AF.Exp, bias=m, scale=1.0, accum_out=acc),
                        reads=[bS, bz], writes=[bS, bz])
            mk.emit("vector", lambda E, o=zst[:, 5:6], i=zst[:, 16:16 + nslab]:
                    E.reduce_sum(o, i, axis=AX.X), reads=[bz], writes=[bz])
            mk.emit("vector", lambda E, o=zst[:, 6:7], i=zst[:, 5:6]:
                    E.reciprocal(o, i), reads=[bz], writes=[bz])
    bd0, bt0, bpps, bpts = B("ad0"), B("at0"), B("apps"), B("apts")
    bpv = bq[2]

    def _ptevac_pv(s):
        par = s % 2
        pt = ptA if par == 0 else ptB
        bpt = bptA if par == 0 else bptB
        mk.emit("vector", lambda E, o=pts[:, par * 512:(par + 1) * 512], i=pt[:, 0:512]:
                E.tensor_copy(o, i), reads=[bpt], writes=[bpts])
        for c4 in range(4):
            gblk = s * 4 + c4
            mk.emit("tensor", lambda E, o=pBig[:, 1024:1152],
                    l=cV_t[:, gblk * 128:(gblk + 1) * 128],
                    r=pts[:, par * 512 + c4 * 128: par * 512 + (c4 + 1) * 128],
                    st_=(s == 0 and c4 == 0), sp=(s == nslab - 1 and c4 == 3):
                    E.matmul(o, l, r, start=st_, stop=sp),
                    reads=[B("aV"), bpts], writes=[bpv], inc=(c4 == 3))

    for s in range(nslab):
        par = s % 2
        mk.emit("vector", lambda E, o=t0[:], i=cE_t[:, s * 512:(s + 1) * 512],
                r=zst[:, 3:4]:
                E.tensor_scalar_mul(o, i, r),
                reads=[bE, bz], writes=[bt0])
        mk.emit("vector", lambda E, o=d0[:], i0=cE0[:, s * 512:(s + 1) * 512],
                r=zst[:, 6:7], i1=t0[:]:
                E.scalar_tensor_tensor(o, i0, r, i1, ALU.mult, ALU.subtract),
                reads=[bS, bz, bt0], writes=[bd0])
        if s >= nslab - 2:
            tm0 = (s - (nslab - 2)) * 512
            mk.emit("vector", lambda E, o=d0[:], i0=d0[:], i1=tmask[:, tm0:tm0 + 512]:
                    E.tensor_add(o, i0, i1),
                    reads=[bd0, B("atm")], writes=[bd0])
        mk.emit("scalar", lambda E, o=pps[:, par * 512:(par + 1) * 512], i=d0[:],
                acc=zst[:, 32 + s:33 + s]:
                E.activation(o, i, AF.Exp, bias=0.0, scale=1.0, accum_out=acc),
                reads=[bd0], writes=[bpps, bz])
        pt = ptA if par == 0 else ptB
        bpt = bptA if par == 0 else bptB
        for c4 in range(4):
            mk.emit("tensor", lambda E, o=pt[:, c4 * 128:(c4 + 1) * 128],
                    i=pps[:, par * 512 + c4 * 128: par * 512 + (c4 + 1) * 128],
                    idn=ctx["ident_bf"]:
                    E.transpose(o, i, idn),
                    reads=[bpps, B("ident_bf")], writes=[bpt], inc=(c4 == 3))
        if s > 0:
            _ptevac_pv(s - 1)
    _ptevac_pv(nslab - 1)
    mk.emit("vector", lambda E, o=zst[:, 7:8], i=zst[:, 32:32 + nslab]:
            E.reduce_sum(o, i, axis=AX.X), reads=[bz], writes=[bz])
    mk.emit("vector", lambda E, o=zst[:, 8:9], i=zst[:, 7:8]:
            E.reciprocal(o, i), reads=[bz], writes=[bz])
    bae, bae2, bdump, baebf = B("aae"), B("aae2"), B("adump"), B("aaebf")
    mk.emit("vector", lambda E, o=ae[:], i=pBig[:, 1024:1152]:
            E.tensor_copy(o, i), reads=[bpv], writes=[bae])
    mk.emit("tensor", lambda E, o=pBig[:, 1536:1664], i=ae[:], idn=ctx["ident_f32"][:]:
            E.transpose(o, i, idn),
            reads=[bae, B("ident_f32")], writes=[bq[3]])
    mk.emit("vector", lambda E, o=ae2[:], i=pBig[:, 1536:1664], r=zst[:, 8:9]:
            E.tensor_scalar_mul(o, i, r),
            reads=[bq[3], bz], writes=[bae2])
    mk.emit("scalar", lambda E, o=dump[:], i=ae2[:], acc=zst[:, 48:49]:
            E.activation(o, i, AF.Square, bias=0.0, scale=1.0, accum_out=acc),
            reads=[bae2], writes=[bdump, bz])
    mk.emit("vector", lambda E, o=zst[:, 51:52], i=zst[:, 48:49]:
            E.tensor_scalar(o, i, 1.0 / 128.0, EPS_RMS, ALU.mult, ALU.add),
            reads=[bz], writes=[bz])
    mk.emit("scalar", lambda E, o=zst[:, 49:50], i=zst[:, 51:52]:
            E.activation(o, i, AF.Ln, bias=0.0, scale=1.0),
            reads=[bz], writes=[bz])
    mk.emit("scalar", lambda E, o=zst[:, 50:51], i=zst[:, 49:50]:
            E.activation(o, i, AF.Exp, bias=0.0, scale=-0.5),
            reads=[bz], writes=[bz])
    mk.emit("vector", lambda E, o=ae2[:], i=ae2[:], r=zst[:, 50:51]:
            E.tensor_scalar(o, i, r, 0.8, ALU.mult, ALU.mult),
            reads=[bae2, bz], writes=[bae2])
    mk.emit("vector", lambda E, o=aebf[:], i0=ae2[:], i1=ctx["rmsw_sb"][:]:
            E.tensor_mul(o, i0, i1),
            reads=[bae2, B("rmsw_sb")], writes=[baebf])
    mk.emit("tensor", lambda E, o=ptA[:, 512:640], i=aebf[:], idn=ctx["ident_bf"]:
            E.transpose(o, i, idn),
            reads=[baebf, B("ident_bf")], writes=[bptA])
    mk.emit("vector", lambda E,
            o=ctx["c_att"][:, h * 1024 + j * 128: h * 1024 + (j + 1) * 128],
            i=ptA[:, 512:640]:
            E.tensor_copy(o, i), reads=[bptA], writes=[B(f"att{h}")])


def _phase_wo(ctx, scope, w_ap, res_ap, res_buf, dst_ap, dst_buf, q_store,
              bias_sb=None, bias_buf=None, tag="wo"):
    nc, mk, B, alloc = ctx["nc"], ctx["mk"], ctx["B"], ctx["alloc"]
    es = scope
    wob = alloc(f"{tag}_w", (128, 2 * 512), BF16, es)
    xrb = alloc(f"{tag}_x", (128, 2 * 512), F32, es)
    rob = alloc(f"{tag}_o", (128, 2 * 512), F32, es)
    pw = [es.enter_context(nc.psum_tensor(f"{tag}_p{i}", [128, 512], F32))
          for i in range(8)]
    q_w = mk.new_q(f"{tag}w", "gpsimd")
    q_x = mk.new_q(f"{tag}x", "sync")
    c_att = ctx["c_att"]
    batt = [B(f"att{h}") for h in range(16)]
    for ns in range(4):
        for k in range(16):
            slot = k % 2
            bw = B(f"wo_w{slot}")
            mk.dma(q_w, wob[:, slot * 512:(slot + 1) * 512],
                   w_ap[k * 128:(k + 1) * 128, ns * 512:(ns + 1) * 512],
                   writes=[bw])
            for tt in range(NT):
                mk.emit("tensor", lambda E, o=pw[tt][:, 0:512],
                        l=c_att[:, k * 1024 + tt * 128: k * 1024 + (tt + 1) * 128],
                        r=wob[:, slot * 512:(slot + 1) * 512],
                        s=(k == 0), st=(k == 15):
                        E.matmul(o, l, r, start=s, stop=st),
                        reads=[batt[k], bw], writes=[B(f"wo_p{tt}")],
                        inc=(tt == NT - 1) or (k == 15))
        for tt in range(NT):
            slot = tt % 2
            bx = B(f"wo_x{slot}")
            mk.dma(q_x, xrb[:, slot * 512:(slot + 1) * 512],
                   res_ap[tt * 128:(tt + 1) * 128, ns * 512:(ns + 1) * 512],
                   reads=[res_buf], writes=[bx])
            bo = B(f"wo_o{slot}")
            if bias_sb is not None:
                mk.emit("vector", lambda E, o=rob[:, slot * 512:(slot + 1) * 512],
                        i0=pw[tt][:, 0:512], i1=bias_sb[:, ns * 512:(ns + 1) * 512]:
                        E.tensor_add(o, i0, i1),
                        reads=[B(f"wo_p{tt}"), bias_buf], writes=[bo])
                mk.emit("vector", lambda E, o=rob[:, slot * 512:(slot + 1) * 512],
                        i0=rob[:, slot * 512:(slot + 1) * 512],
                        i1=xrb[:, slot * 512:(slot + 1) * 512]:
                        E.tensor_add(o, i0, i1),
                        reads=[bo, bx], writes=[bo])
            else:
                mk.emit("vector", lambda E, o=rob[:, slot * 512:(slot + 1) * 512],
                        i0=pw[tt][:, 0:512], i1=xrb[:, slot * 512:(slot + 1) * 512]:
                        E.tensor_add(o, i0, i1),
                        reads=[B(f"wo_p{tt}"), bx], writes=[bo])
            mk.dma(q_store, dst_ap[tt * 128:(tt + 1) * 128, ns * 512:(ns + 1) * 512],
                   rob[:, slot * 512:(slot + 1) * 512],
                   reads=[bo], writes=[dst_buf])


def _phase_ffn(ctx, scope, gat, src_ap, src_buf, dst_ap, dst_buf, q_store, tag):
    nc, mk, B, T, alloc = ctx["nc"], ctx["mk"], ctx["B"], ctx["T"], ctx["alloc"]
    es = scope
    acc = alloc(f"{tag}_acc", (128, NT * 2048), F32, es)
    hb = alloc(f"{tag}_h", (128, NT * 768), BF16, es)
    wg = alloc(f"{tag}_wg", (128, 2 * 8192), BF16, es)
    wd = alloc(f"{tag}_wd", (128, 2 * 3072), BF16, es)
    st_ = alloc(f"{tag}_silu", (128, 256), F32, es)
    pg = es.enter_context(nc.psum_tensor(f"{tag}_pg", [128, 1024], F32))
    pd = es.enter_context(nc.psum_tensor(f"{tag}_pd", [128, 1024], F32))
    ptA = es.enter_context(nc.psum_tensor(f"{tag}_ptA", [128, 1024], BF16))
    ptB = es.enter_context(nc.psum_tensor(f"{tag}_ptB", [128, 1024], BF16))
    q_a = mk.new_q(f"{tag}a", "sync")
    q_wg = mk.new_q(f"{tag}wg", "sync")
    q_wd = mk.new_q(f"{tag}wd", "sync")
    bacc = B("ffn_acc")
    bh = B("ffn_h")
    bcomb = B("comb")
    mk.dma(q_a, acc[:].rearrange("p (t d) -> p t d", d=2048),
           src_ap.rearrange("(t p) d -> p t d", p=128),
           reads=[src_buf], writes=[bacc])
    for e in range(NEXP):
        row0 = (e // 4) * 8192 + (e % 4) * 2048
        row0d = (e // 4) * 3072 + (e % 4) * 768
        for jb in range(3):
            slot = (e * 3 + jb) % 2
            bwg = B(f"ffn_wg{slot}")
            for half in range(2):
                src = T["gupT_all"][row0:row0 + 2048,
                                    half * 768 + jb * 256: half * 768 + (jb + 1) * 256]
                dst = wg[:, slot * 8192: (slot + 1) * 8192].rearrange(
                    "p (k c) -> p k c", c=512)[:, :, half * 256:(half + 1) * 256]
                mk.dma(q_wg, dst, src.rearrange("(k p) c -> p k c", p=128),
                       reads=[B("gupT_all")], writes=[bwg])
            for tt in range(NT):
                par = tt % 2
                bpg = B(f"ffn_pg{par}")
                for k in range(16):
                    mk.emit("tensor", lambda E, o=pg[:, par * 512:(par + 1) * 512],
                            l=c_lnT_slice(ctx, k, tt),
                            r=wg[:, slot * 8192 + k * 512: slot * 8192 + (k + 1) * 512],
                            s=(k == 0), st=(k == 15):
                            E.matmul(o, l, r, start=s, stop=st),
                            reads=[B(f"lnT{k}"), bwg], writes=[bpg], inc=(k == 15))
                mk.emit("scalar", lambda E, o=st_[:], i=pg[:, par * 512: par * 512 + 256]:
                        E.activation(o, i, AF.Silu, bias=0.0, scale=1.0),
                        reads=[bpg], writes=[B("ffn_silu")])
                mk.emit("vector", lambda E, o=hb[:, tt * 768 + jb * 256: tt * 768 + (jb + 1) * 256],
                        i0=st_[:], r=gat["comb"][:, tt * 32 + e: tt * 32 + e + 1],
                        i1=pg[:, par * 512 + 256: par * 512 + 512]:
                        E.scalar_tensor_tensor(o, i0, r, i1, ALU.mult, ALU.mult),
                        reads=[B("ffn_silu"), bcomb, bpg], writes=[bh])
        for tt in range(NT):
            for jc in range(6):
                pt = ptA if jc % 2 == 0 else ptB
                bpt = B(f"ffn_pt{jc % 2}")
                mk.emit("tensor", lambda E, o=pt[:, 0:128],
                        i=hb[:, tt * 768 + jc * 128: tt * 768 + (jc + 1) * 128],
                        idn=ctx["ident_bf"]:
                        E.transpose(o, i, idn),
                        reads=[bh, B("ident_bf")], writes=[bpt])
                mk.emit("vector", lambda E,
                        o=hb[:, tt * 768 + jc * 128: tt * 768 + (jc + 1) * 128],
                        i=pt[:, 0:128]:
                        E.tensor_copy(o, i), reads=[bpt], writes=[bh])
        for os in range(4):
            slot = (e * 4 + os) % 2
            bwd = B(f"ffn_wd{slot}")
            src = T["dwnT_all"][row0d:row0d + 768, os * 512:(os + 1) * 512]
            mk.dma(q_wd, wd[:, slot * 3072:(slot + 1) * 3072].rearrange(
                   "p (k c) -> p k c", c=512),
                   src.rearrange("(k p) c -> p k c", p=128),
                   reads=[B("dwnT_all")], writes=[bwd])
            for tt in range(NT):
                par = tt % 2
                bpd = B(f"ffn_pd{par}")
                for jc in range(6):
                    mk.emit("tensor", lambda E, o=pd[:, par * 512:(par + 1) * 512],
                            l=hb[:, tt * 768 + jc * 128: tt * 768 + (jc + 1) * 128],
                            r=wd[:, slot * 3072 + jc * 512: slot * 3072 + (jc + 1) * 512],
                            s=(jc == 0), st=(jc == 5):
                            E.matmul(o, l, r, start=s, stop=st),
                            reads=[bh, bwd], writes=[bpd], inc=(jc == 5))
                mk.emit("vector", lambda E,
                        o=acc[:, tt * 2048 + os * 512: tt * 2048 + (os + 1) * 512],
                        i0=acc[:, tt * 2048 + os * 512: tt * 2048 + (os + 1) * 512],
                        i1=pd[:, par * 512:(par + 1) * 512]:
                        E.tensor_add(o, i0, i1),
                        reads=[bacc, bpd], writes=[bacc])
    for tt in range(NT):
        mk.dma(q_store, dst_ap[tt * 128:(tt + 1) * 128, :],
               acc[:, tt * 2048:(tt + 1) * 2048],
               reads=[bacc], writes=[dst_buf])


def _build_phases(ctx, n_sub):
    nc, mk, B, T, alloc, es = (ctx["nc"], ctx["mk"], ctx["B"], ctx["T"],
                               ctx["alloc"], ctx["es"])
    q_store = mk.new_q("store", "scalar")

    ctx["rmsw_sb"] = alloc("rmsw_sb", (128, 128), F32)
    ctx["lamb"] = alloc("lamb", (128, 16), F32)
    mk.dma(ctx["q_const"], ctx["rmsw_sb"][:], T["rms_w_tile"], writes=[B("rmsw_sb")])
    with ExitStack() as sc:
        _prep_lamb(ctx, sc)
        mk.barrier()

    with ExitStack() as s1:
        ctx["cos2_sb"] = alloc("cos2_sb", (128, 8 * 256), F32, s1)
        ctx["sin2_sb"] = alloc("sin2_sb", (128, 8 * 256), F32, s1)
        ctx["c_att"] = alloc("c_att_a", (128, 16 * 1024), BF16, s1)
        mk.dma(ctx["q_const"], ctx["cos2_sb"][:].rearrange("p (t c) -> p t c", c=256),
               T["cos2"].rearrange("(t p) c -> p t c", p=128), writes=[B("cos2_sb")])
        mk.dma(ctx["q_const"], ctx["sin2_sb"][:].rearrange("p (t c) -> p t c", c=256),
               T["sin2"].rearrange("(t p) c -> p t c", p=128), writes=[B("sin2_sb")])
        with ExitStack() as sc:
            _emit_ln(ctx, T["x"], B("x_dram"), "ln1", scope=sc)
            mk.barrier()
        with ExitStack() as sc:
            _phase_kv_a(ctx, sc)
            _phase_stage0(ctx, sc)
            mk.barrier()
        with ExitStack() as sc:
            _phase_att_a(ctx, sc)
            mk.barrier()
        dst1 = T["out"] if n_sub == 1 else T["R1"]
        dbuf1 = B("out_dram") if n_sub == 1 else B("R1")
        with ExitStack() as sc:
            _phase_wo(ctx, sc, T["aWo"], T["x"], B("x_dram"), dst1, dbuf1,
                      q_store, tag="woa")
            mk.barrier()
    if n_sub == 1:
        mk.finalize(B("out_dram"))
        return

    with ExitStack() as s2:
        comb1 = alloc("comb_ln2", (128, NT * 32), F32, s2)
        with ExitStack() as sc:
            gat = _emit_ln(ctx, T["R1"], B("R1"), "ln2", with_gating=True, scope=sc)
            mk.barrier()
        dst2 = T["out"] if n_sub == 2 else T["R2"]
        dbuf2 = B("out_dram") if n_sub == 2 else B("R2")
        with ExitStack() as sc:
            _phase_ffn(ctx, sc, gat, T["R1"], B("R1"), dst2, dbuf2, q_store, "f1")
            mk.barrier()
    if n_sub == 2:
        mk.finalize(B("out_dram"))
        return

    with ExitStack() as sc:
        _emit_ln(ctx, T["R2"], B("R2"), "ln3", scope=sc)
        mk.barrier()
    with ExitStack() as sc:
        qb_all = alloc("qb_all", (128, 16 * 1024), BF16, sc)
        bo_b = alloc("bo_b", (128, 2048), F32, sc)
        ctx["c_att"] = alloc("c_att_b", (128, 16 * 1024), BF16, sc)
        with ExitStack() as sc2:
            _phase_qkv_b(ctx, sc2, qb_all, bo_b)
            mk.barrier()
        with ExitStack() as sc2:
            _phase_att_b(ctx, sc2, qb_all)
            mk.barrier()
        dst3 = T["out"] if n_sub == 3 else T["R3"]
        dbuf3 = B("out_dram") if n_sub == 3 else B("R3")
        with ExitStack() as sc2:
            _phase_wo(ctx, sc2, T["bWo"], T["R2"], B("R2"), dst3, dbuf3,
                      q_store, bias_sb=bo_b, bias_buf=B("bo_b"), tag="wob")
            mk.barrier()
    if n_sub == 3:
        mk.finalize(B("out_dram"))
        return

    with ExitStack() as s4:
        comb4 = alloc("comb_ln4", (128, NT * 32), F32, s4)
        with ExitStack() as sc:
            gat = _emit_ln(ctx, T["R3"], B("R3"), "ln4", with_gating=True, scope=sc)
            mk.barrier()
        with ExitStack() as sc:
            _phase_ffn(ctx, sc, gat, T["R3"], B("R3"), T["out"], B("out_dram"),
                       q_store, "f2")
            mk.barrier()
    mk.finalize(B("out_dram"))


def _prep_lamb(ctx, scope):
    nc, mk, B, T, alloc = ctx["nc"], ctx["mk"], ctx["B"], ctx["T"], ctx["alloc"]
    lt = alloc("lam_lt", (16, 128), F32, scope)
    ls = alloc("lam_ls", (16, 8), F32, scope)
    lrow = alloc("lam_row", (1, 16), F32, scope)
    q_l = mk.new_q("laml", "sync")
    p1 = scope.enter_context(nc.psum_tensor("lam_p", [128, 512], F32))
    bl = B("lam_lt"); bs = B("lam_ls"); bp = B("lam_p"); br = B("lam_row")
    for i, (a_, b_) in enumerate((("lq1", "lk1"), ("lq2", "lk2"))):
        t_a = alloc(f"lam_in{i}a", (16, 128), F32, scope)
        t_b = alloc(f"lam_in{i}b", (16, 128), F32, scope)
        mk.dma(q_l, t_a[:], T[a_], writes=[B(f"lam_in{i}a")])
        mk.dma(q_l, t_b[:], T[b_], writes=[B(f"lam_in{i}b")])
        mk.emit("vector", lambda E, o=lt[:], i0=t_a[:], i1=t_b[:]:
                E.tensor_mul(o, i0, i1),
                reads=[B(f"lam_in{i}a"), B(f"lam_in{i}b")], writes=[bl])
        mk.emit("vector", lambda E, o=ls[:, i:i + 1], i=lt[:]:
                E.reduce_sum(o, i, axis=AX.X), reads=[bl], writes=[bs])
        mk.emit("scalar", lambda E, o=ls[:, 2 + i:3 + i], i=ls[:, i:i + 1]:
                E.activation(o, i, AF.Exp, bias=0.0, scale=1.0),
                reads=[bs], writes=[bs])
    mk.emit("vector", lambda E, o=ls[:, 4:5], i0=ls[:, 2:3], i1=ls[:, 3:4]:
            E.tensor_sub(o, i0, i1), reads=[bs], writes=[bs])
    mk.emit("vector", lambda E, o=ls[:, 5:6], i=ls[:, 4:5]:
            E.tensor_scalar_add(o, i, LAM_INIT), reads=[bs], writes=[bs])
    mk.emit("tensor", lambda E, o=p1[0:1, 0:16], i=ls[0:16, 5:6],
            idn=ctx["ident_f32"][0:16, 0:16]:
            E.transpose(o, i, idn),
            reads=[bs, B("ident_f32")], writes=[bp])
    mk.emit("vector", lambda E, o=lrow[:], i=p1[0:1, 0:16]:
            E.tensor_copy(o, i), reads=[bp], writes=[br])
    mk.emit("tensor", lambda E, o=p1[:, 32:48], l=ctx["ones_row"][:], r=lrow[:]:
            E.matmul(o, l, r, start=True, stop=True),
            reads=[br, B("ones_row")], writes=[bp])
    mk.emit("vector", lambda E, o=ctx["lamb"][:], i=p1[:, 32:48]:
            E.tensor_copy(o, i), reads=[bp], writes=[B("lamb")])


def _phase_qkv_b(ctx, scope, qb_all, bo_b):
    nc, mk, B, T, alloc = ctx["nc"], ctx["mk"], ctx["B"], ctx["T"], ctx["alloc"]
    es = scope
    bia = alloc("b_bia", (16, 128), F32, es)
    biT = alloc("b_biT", (128, 3 * 16), F32, es)
    wtile = alloc("b_wt", (128, 2 * 384), BF16, es)
    kb_t = alloc("b_kt", (128, 1024), BF16, es)
    vb_t = alloc("b_vt", (128, 1024), BF16, es)
    vev = alloc("b_vev", (128, 256), BF16, es)
    bob_row = alloc("b_borow", (1, 2048), F32, es)
    pA = es.enter_context(nc.psum_tensor("b_pA", [128, 1024], F32))
    pB2 = es.enter_context(nc.psum_tensor("b_pB", [128, 1024], F32))
    pC = es.enter_context(nc.psum_tensor("b_pC", [128, 1024], F32))
    ptA = es.enter_context(nc.psum_tensor("b_ptA", [128, 1024], BF16))
    ptB = es.enter_context(nc.psum_tensor("b_ptB", [128, 1024], BF16))
    q_w = mk.new_q("bqkvw", "gpsimd")
    q_b = mk.new_q("bqkvb", "sync")
    q_st = mk.new_q("bqkvst", "scalar")
    psums = [pA, pB2, pC]
    bps = [B("b_pA"), B("b_pB"), B("b_pC")]
    bbiT = B("b_biT")
    for i, nm in enumerate(("bbq", "bbk", "bbv")):
        mk.dma(q_b, bia[:], T[nm], writes=[B("b_bia")])
        mk.emit("tensor", lambda E, o=pA[0:128, 0:16], i=bia[:], idn=ctx["ident_f32"][0:16, 0:16]:
                E.transpose(o, i, idn),
                reads=[B("b_bia"), B("ident_f32")], writes=[bps[0]])
        mk.emit("vector", lambda E, o=biT[:, i * 16:(i + 1) * 16], i=pA[0:128, 0:16]:
                E.tensor_copy(o, i), reads=[bps[0]], writes=[bbiT])
    mk.emit("vector", lambda E, o=biT[:, 0:16], i=biT[:, 0:16]:
            E.tensor_scalar_mul(o, i, SCL_B), reads=[bbiT], writes=[bbiT])
    mk.dma(q_b, bob_row[:], T["bbo"], writes=[B("b_borow")])
    for ns in range(4):
        mk.emit("tensor", lambda E, o=pC[:, 0:512], l=ctx["ones_row"][:],
                r=bob_row[:, ns * 512:(ns + 1) * 512]:
                E.matmul(o, l, r, start=True, stop=True),
                reads=[B("b_borow"), B("ones_row")], writes=[bps[2]])
        mk.emit("vector", lambda E, o=bo_b[:, ns * 512:(ns + 1) * 512], i=pC[:, 0:512]:
                E.tensor_copy(o, i), reads=[bps[2]], writes=[B("bo_b")])
    groups = [list(range(s, min(s + 3, 16))) for s in range(0, 16, 3)]
    for mat, w_nm, bi, out_kind in (("q", "bWq", 0, "q"), ("k", "bWk", 1, "k"),
                                    ("v", "bWv", 2, "v")):
        for grp in groups:
            for k in range(16):
                slot = k % 2
                bw = B(f"b_wt{slot}")
                cols = [m * 128 for m in grp]
                mk.dma(q_w, wtile[:, slot * 384: slot * 384 + 128 * len(grp)],
                       T[w_nm][k * 128:(k + 1) * 128, grp[0] * 128: grp[0] * 128 + 128 * len(grp)],
                       writes=[bw])
                for mi, m in enumerate(grp):
                    for hl in range(2):
                        mk.emit("tensor", lambda E, o=psums[mi][:, hl * 512:(hl + 1) * 512],
                                l=wtile[:, slot * 384 + mi * 128: slot * 384 + (mi + 1) * 128],
                                r=ctx["c_lnT"][:, k * 1024 + hl * 512: k * 1024 + (hl + 1) * 512],
                                s=(k == 0), st=(k == 15):
                                E.matmul(o, l, r, start=s, stop=st),
                                reads=[bw, B(f"lnT{k}")], writes=[bps[mi]],
                                inc=(mi == len(grp) - 1 and hl == 1) or (k == 15))
            for mi, m in enumerate(grp):
                if out_kind == "q":
                    mk.emit("vector", lambda E, o=qb_all[:, m * 1024:(m + 1) * 1024],
                            i=psums[mi][:, 0:1024], bb=biT[:, 0 * 16 + m: 0 * 16 + m + 1]:
                            E.tensor_scalar(o, i, SCL_B, bb, ALU.mult, ALU.add),
                            reads=[bps[mi], bbiT], writes=[B("qb_all")])
                elif out_kind == "k":
                    mk.emit("vector", lambda E, o=kb_t[:],
                            i=psums[mi][:, 0:1024], bb=biT[:, 16 + m: 16 + m + 1]:
                            E.tensor_scalar(o, i, bb, None, ALU.add),
                            reads=[bps[mi], bbiT], writes=[B("b_kt")])
                    mk.dma(q_st, T["kb_loc"][m * 128:(m + 1) * 128, :], kb_t[:],
                           reads=[B("b_kt")], writes=[B("kb_loc")])
                else:
                    mk.emit("vector", lambda E, o=vb_t[:],
                            i=psums[mi][:, 0:1024], bb=biT[:, 32 + m: 32 + m + 1]:
                            E.tensor_scalar(o, i, bb, None, ALU.add),
                            reads=[bps[mi], bbiT], writes=[B("b_vt")])
                    for c in range(8):
                        pt = ptA if c % 2 == 0 else ptB
                        bpt = B(f"b_pt{c % 2}")
                        mk.emit("tensor", lambda E, o=pt[:, 0:128],
                                i=vb_t[:, c * 128:(c + 1) * 128], idn=ctx["ident_bf"]:
                                E.transpose(o, i, idn),
                                reads=[B("b_vt"), B("ident_bf")], writes=[bpt])
                        bvev = B(f"b_vev{c % 2}")
                        mk.emit("vector", lambda E, o=vev[:, (c % 2) * 128:(c % 2) * 128 + 128],
                                i=pt[:, 0:128]:
                                E.tensor_copy(o, i), reads=[bpt], writes=[bvev])
                        mk.dma(q_st, T["vb_loc"][c * 128:(c + 1) * 128, m * 128:(m + 1) * 128],
                               vev[:, (c % 2) * 128:(c % 2) * 128 + 128],
                               reads=[bvev], writes=[B("vb_loc")])
    mk.collective(ctx["q_cc"], "AllGather", ALU.bypass,
                  [T["kb_loc"].opt(), T["vb_loc"].opt()],
                  [T["kb_all"].opt(), T["vb_all"].opt()],
                  reads=[B("kb_loc"), B("vb_loc")],
                  writes=[B("kb_all"), B("vb_all")])


def _phase_att_b(ctx, scope, qb_all):
    nc, mk, B, T, alloc = ctx["nc"], ctx["mk"], ctx["B"], ctx["T"], ctx["alloc"]
    es = scope
    cS_t = alloc("bS", (128, 8192), F32, es)
    cP_t = alloc("bP", (128, 2 * 8192), BF16, es)
    KB = alloc("bK", (128, 8192), BF16, es)
    VB = alloc("bV", (128, 8192), BF16, es)
    pts = alloc("bpts", (128, 1024), BF16, es)
    zst = alloc("bzst", (128, 40), F32, es)
    ae = alloc("bae", (128, 128), F32, es)
    ae2 = alloc("bae2", (128, 128), F32, es)
    aebf = alloc("baebf", (128, 128), BF16, es)
    psc = es.enter_context(nc.psum_tensor("b_psc", [128, 2048], F32))
    ppv = es.enter_context(nc.psum_tensor("b_ppv", [128, 512], F32))
    pep = es.enter_context(nc.psum_tensor("b_pep", [128, 512], F32))
    ptA = es.enter_context(nc.psum_tensor("b2_ptA", [128, 512], BF16))
    ptB = es.enter_context(nc.psum_tensor("b2_ptB", [128, 512], BF16))
    q_kv = mk.new_q("bkv", "sync")
    bsc = [B(f"b_psc{i}") for i in range(4)]
    bppv, bpep = B("b_ppv"), B("b_pep")
    bptA, bptB = B("b2_ptA"), B("b2_ptB")
    bS, bP, bKb, bVb = B("bS"), B("bP"), B("bK"), B("bV")
    bpts2, bz = B("bpts"), B("bzst")

    for pair in range(16):
        for r in range(NCORES):
            src = T["kb_all"][r * 2048 + pair * 128: r * 2048 + (pair + 1) * 128, :]
            src = src.rearrange("d (j t) -> d j t", t=128)
            dst = KB[:].rearrange("p (blk t) -> p blk t", t=128)[:, r::8, :]
            mk.dma(q_kv, dst, src, reads=[B("kb_all")], writes=[bKb])
            srcv = T["vb_all"][r * 1024:(r + 1) * 1024, pair * 128:(pair + 1) * 128]
            srcv = srcv.rearrange("(j t) d -> t j d", t=128)
            dstv = VB[:].rearrange("p (blk d) -> p blk d", d=128)[:, r::8, :]
            mk.dma(q_kv, dstv, srcv, reads=[B("vb_all")], writes=[bVb])
        for j in range(NT):
            for hh in range(2):
                for s in range(16):
                    bk = bsc[s % 4]
                    mk.emit("tensor", lambda E, o=psc[:, (s % 4) * 512:(s % 4) * 512 + 512],
                            l=qb_all[hh * 64:hh * 64 + 64,
                                     pair * 1024 + j * 128: pair * 1024 + (j + 1) * 128],
                            r=KB[hh * 64:hh * 64 + 64, s * 512:(s + 1) * 512]:
                            E.matmul(o, l, r, start=True, stop=True),
                            reads=[B("qb_all"), bKb], writes=[bk])
                    if s % 2 == 1:
                        eng = "vector" if (s // 2) % 2 == 0 else "scalar"
                        if eng == "vector":
                            mk.emit("vector", lambda E,
                                    o=cS_t[:, (s - 1) * 512:(s + 1) * 512],
                                    i=psc[:, ((s - 1) % 4) * 512:((s - 1) % 4) * 512 + 1024]:
                                    E.tensor_copy(o, i),
                                    reads=[bsc[(s - 1) % 4], bk], writes=[bS])
                        else:
                            mk.emit("scalar", lambda E,
                                    o=cS_t[:, (s - 1) * 512:(s + 1) * 512],
                                    i=psc[:, ((s - 1) % 4) * 512:((s - 1) % 4) * 512 + 1024]:
                                    E.activation(o, i, AF.Copy, bias=0.0, scale=1.0),
                                    reads=[bsc[(s - 1) % 4], bk], writes=[bS])
                        mk.emit("vector", lambda E, o=zst[:, 16 + 8 * hh + (s // 2):
                                                          17 + 8 * hh + (s // 2)],
                                i=cS_t[:, (s - 1) * 512:(s + 1) * 512]:
                                E.reduce_max(o, i, axis=AX.X),
                                reads=[bS], writes=[bz])
                mk.emit("vector", lambda E, o=zst[:, hh:hh + 1],
                        i=zst[:, 16 + 8 * hh:24 + 8 * hh]:
                        E.reduce_max(o, i, axis=AX.X, negate=True),
                        reads=[bz], writes=[bz])
                for ec in range(4):
                    mk.emit("scalar", lambda E,
                            o=cP_t[:, hh * 8192 + ec * 2048: hh * 8192 + (ec + 1) * 2048],
                            i=cS_t[:, ec * 2048:(ec + 1) * 2048], m=zst[:, hh:hh + 1],
                            acc=zst[:, 32 + 4 * hh + ec:33 + 4 * hh + ec]:
                            E.activation(o, i, AF.Exp, bias=m, scale=1.0, accum_out=acc),
                            reads=[bS, bz], writes=[bP, bz])
                mk.emit("vector", lambda E, o=zst[:, 2 + hh:3 + hh],
                        i=zst[:, 32 + 4 * hh:36 + 4 * hh]:
                        E.reduce_sum(o, i, axis=AX.X), reads=[bz], writes=[bz])
                mk.emit("vector", lambda E, o=zst[:, 4 + hh:5 + hh], i=zst[:, 2 + hh:3 + hh]:
                        E.reciprocal(o, i), reads=[bz], writes=[bz])
            def _b_ptevac_pv(hh, s):
                par = s % 2
                pt = ptA if par == 0 else ptB
                bpt = bptA if par == 0 else bptB
                peng = "vector" if par == 0 else "scalar"
                if peng == "vector":
                    mk.emit("vector", lambda E, o=pts[:, par * 512:par * 512 + 512],
                            i=pt[:, 0:512]:
                            E.tensor_copy(o, i), reads=[bpt], writes=[bpts2])
                else:
                    mk.emit("scalar", lambda E, o=pts[:, par * 512:par * 512 + 512],
                            i=pt[:, 0:512]:
                            E.activation(o, i, AF.Copy, bias=0.0, scale=1.0),
                            reads=[bpt], writes=[bpts2])
                for c4 in range(4):
                    gblk = s * 4 + c4
                    mk.emit("tensor", lambda E,
                            o=ppv[hh * 64:hh * 64 + 64, 0:128],
                            l=VB[:, gblk * 128 + hh * 64: gblk * 128 + hh * 64 + 64],
                            r=pts[:, par * 512 + c4 * 128: par * 512 + (c4 + 1) * 128],
                            st_=(s == 0 and c4 == 0),
                            sp=(s == 15 and c4 == 3),
                            tp=(0, hh * 64):
                            E.matmul(o, l, r, start=st_, stop=sp, tile_position=tp),
                            reads=[bVb, bpts2], writes=[bppv], inc=(c4 == 3))

            for hh in range(2):
                for s in range(16):
                    pt = ptA if s % 2 == 0 else ptB
                    bpt = bptA if s % 2 == 0 else bptB
                    for c4 in range(4):
                        mk.emit("tensor", lambda E, o=pt[:, c4 * 128:(c4 + 1) * 128],
                                i=cP_t[:, hh * 8192 + s * 512 + c4 * 128:
                                        hh * 8192 + s * 512 + (c4 + 1) * 128],
                                idn=ctx["ident_bf"]:
                                E.transpose(o, i, idn),
                                reads=[bP, B("ident_bf")], writes=[bpt], inc=(c4 == 3))
                    if s > 0:
                        _b_ptevac_pv(hh, s - 1)
                _b_ptevac_pv(hh, 15)
            mk.emit("vector", lambda E, o=ae[:], i=ppv[:, 0:128]:
                    E.tensor_copy(o, i), reads=[bppv], writes=[B("bae")])
            mk.emit("tensor", lambda E, o=pep[:, 0:128], i=ae[:], idn=ctx["ident_f32"][:]:
                    E.transpose(o, i, idn),
                    reads=[B("bae"), B("ident_f32")], writes=[bpep])
            for hh in range(2):
                mk.emit("vector", lambda E, o=ae2[:, hh * 64:(hh + 1) * 64],
                        i=pep[:, hh * 64:(hh + 1) * 64], r=zst[:, 4 + hh:5 + hh]:
                        E.tensor_scalar_mul(o, i, r),
                        reads=[bpep, bz], writes=[B("bae2")])
            mk.emit("vector", lambda E, o=aebf[:], i=ae2[:]:
                    E.tensor_copy(o, i), reads=[B("bae2")], writes=[B("baebf")])
            mk.emit("tensor", lambda E, o=ptA[:, 0:128], i=aebf[:], idn=ctx["ident_bf"]:
                    E.transpose(o, i, idn),
                    reads=[B("baebf"), B("ident_bf")], writes=[bptA])
            mk.emit("vector", lambda E,
                    o=ctx["c_att"][:, pair * 1024 + j * 128: pair * 1024 + (j + 1) * 128],
                    i=ptA[:, 0:128]:
                    E.tensor_copy(o, i), reads=[bptA], writes=[B(f"att{pair}")])



def kernel(**inputs):
    x = np.asarray(inputs["L0_attn_a_x"]).reshape(S, D)
    ident_bf_u16, ident_f32, ones_row = _host_shared_consts()
    rms_w_tile = np.ascontiguousarray(
        np.broadcast_to(np.asarray(inputs["L0_attn_a_rms_w"]), (128, 128)))
    bbq = np.ascontiguousarray(np.asarray(inputs["L0_attn_b_bq"]).reshape(16, 128))
    bbk = np.ascontiguousarray(np.asarray(inputs["L0_attn_b_bk"]).reshape(16, 128))
    bbv = np.ascontiguousarray(np.asarray(inputs["L0_attn_b_bv"]).reshape(16, 128))
    bbo = np.ascontiguousarray(np.asarray(inputs["L0_attn_b_bo"]).reshape(1, D))
    alpha = np.ascontiguousarray(np.asarray(inputs["L0_ffn_layer_alpha"]).reshape(1, 1))

    shared = {
        "aWq": np.ascontiguousarray(inputs["L0_attn_a_Wq"]),
        "aWk": np.ascontiguousarray(inputs["L0_attn_a_Wk"]),
        "aWv": np.ascontiguousarray(inputs["L0_attn_a_Wv"]),
        "aWo": np.ascontiguousarray(inputs["L0_attn_a_Wo"]),
        "lq1": np.ascontiguousarray(inputs["L0_attn_a_lq1"]),
        "lq2": np.ascontiguousarray(inputs["L0_attn_a_lq2"]),
        "lk1": np.ascontiguousarray(inputs["L0_attn_a_lk1"]),
        "lk2": np.ascontiguousarray(inputs["L0_attn_a_lk2"]),
        "rms_w_tile": rms_w_tile,
        "bWq": np.ascontiguousarray(inputs["L0_attn_b_Wq"]),
        "bWk": np.ascontiguousarray(inputs["L0_attn_b_Wk"]),
        "bWv": np.ascontiguousarray(inputs["L0_attn_b_Wv"]),
        "bWo": np.ascontiguousarray(inputs["L0_attn_b_Wo"]),
        "bbq": bbq, "bbk": bbk, "bbv": bbv, "bbo": bbo,
        "gw": np.ascontiguousarray(inputs["L0_ffn_gate_weight"]),
        "alpha": alpha,
        "ident_bf_u16": ident_bf_u16, "ident_f32": ident_f32, "ones_row": ones_row,
    }

    in_maps = []
    idxs = []
    for r in range(NCORES):
        cos2, sin2, tailmask = _host_consts(r)
        idx = _shard_idx(r)
        idxs.append(idx)
        m = dict(shared)
        m["x"] = np.ascontiguousarray(x[idx])
        m["gup_my"] = np.ascontiguousarray(
            np.asarray(inputs["L0_ffn_gate_up_proj"])[4 * r:4 * r + 4])
        m["dwn_my"] = np.ascontiguousarray(
            np.asarray(inputs["L0_ffn_down_proj"])[4 * r:4 * r + 4])
        m["cos2"] = cos2
        m["sin2"] = sin2
        m["tailmask"] = tailmask
        in_maps.append(m)

    nc = build_program(BUILD_SUBLAYERS)
    res = run_bass_kernel_spmd(nc, in_maps, core_ids=list(range(NCORES)))
    full = np.empty((S, D), dtype=np.float32)
    for r in range(NCORES):
        full[idxs[r]] = res.results[r]["out"]
    return full.reshape(1, S, D)


# version 08 (final)
# speedup vs baseline: 1.1879x; 1.0009x over previous
import numpy as np
from contextlib import ExitStack

import concourse.bass as bass
import concourse.mybir as mybir
from concourse.bass_utils import run_bass_kernel_spmd

F32 = mybir.dt.float32
BF16 = mybir.dt.bfloat16
AF = mybir.ActivationFunctionType
ALU = mybir.AluOpType
AX = mybir.AxisListType

NCORES = 8
S = 8192
D = 2048
TLOC = 1024
NT = 8
NEG = -1e30
SCL_A = 0.08838834764831845
SCL_B = 0.17677669529663687
EPS_LN = 1e-5
EPS_RMS = 1e-8
LAM_INIT = 0.2
TAU = 0.05
NEXP = 32
INTER = 768

BUILD_SUBLAYERS = 4

ENGS = ("sync", "vector", "scalar", "tensor", "gpsimd")



class _Prod:
    __slots__ = ("name", "sem", "cnt", "idx", "step", "engine", "is_cc")

    def __init__(self, name, sem, idx, step, engine=None):
        self.name = name
        self.sem = sem
        self.cnt = 0
        self.idx = idx
        self.step = step
        self.engine = engine
        self.is_cc = False


class Buf:
    __slots__ = ("name", "w", "readers", "dmap")

    def __init__(self, name=""):
        self.name = name
        self.w = None
        self.readers = {}
        self.dmap = None


class MK:
    def __init__(self, nc, es):
        self.nc = nc
        self.es = es
        self.seq = {e: [] for e in ENGS}
        self.prods = []
        self.eng = {}
        for e in ENGS:
            sem = es.enter_context(nc.semaphore(f"sem_{e}"))
            p = _Prod(e, sem, len(self.prods), 1, engine=e)
            self.prods.append(p)
            self.eng[e] = p
        self.waited = {e: {} for e in ENGS}
        self._nq = 0

    class _Queue:
        __slots__ = ("engine",)
        def __init__(self, engine):
            self.engine = engine

    def new_q(self, name, engine):
        return MK._Queue(engine)

    def _waits_for(self, ename, reads, writes):
        need = {}

        def add(p, c):
            if c > need.get(p, 0):
                need[p] = c

        for b in reads:
            if b.w is not None:
                add(*b.w)
        for b in writes:
            if b.w is not None:
                add(*b.w)
            for p, c in b.readers.items():
                add(p, c)
        w = self.waited[ename]
        out = []
        for p in sorted(need, key=lambda p: p.idx):
            c = need[p]
            assert c <= p.cnt, f"wait beyond emitted count on {p.name}: {c} > {p.cnt}"
            if w.get(p.idx, 0) < c:
                out.append((p.sem, c))
                w[p.idx] = c
        return out

    def _mark(self, prod, cnt, reads, writes):
        for b in reads:
            b.readers[prod] = cnt
        for b in writes:
            b.w = (prod, cnt)
            b.readers = {}

    def emit(self, ename, build, reads=(), writes=(), inc=True):
        for sem, c in self._waits_for(ename, reads, writes):
            self.seq[ename].append(("wait", sem, c))
        p = self.eng[ename]
        if not inc:
            self.seq[ename].append(("inst", build, None, 0))
            return p.cnt
        p.cnt += 1
        self.seq[ename].append(("inst", build, p.sem, 1))
        self._mark(p, p.cnt, reads, writes)
        return p.cnt

    def _buf_dma_prod(self, buf, is_cc=False):
        if buf.dmap is None:
            sem = self.es.enter_context(self.nc.semaphore(f"qb_{buf.name}"))
            buf.dmap = _Prod(f"qb_{buf.name}", sem, len(self.prods), 16)
            buf.dmap.is_cc = is_cc
            self.prods.append(buf.dmap)
        return buf.dmap

    def dma(self, q, out_ap, in_ap, reads=(), writes=(), **kw):
        assert len(writes) == 1, "dma needs exactly one target buf"
        p = self._buf_dma_prod(writes[0])
        for sem, c in self._waits_for(q.engine, reads, writes):
            self.seq[q.engine].append(("wait", sem, c))
        p.cnt += 16
        self.seq[q.engine].append(
            ("inst", lambda E, o=out_ap, i=in_ap, kw=kw: E.dma_start(out=o, in_=i, **kw),
             p.sem, 16))
        self._mark(p, p.cnt, reads, writes)
        return p.cnt

    def collective(self, q, kind, op, ins, outs, reads=(), writes=()):
        rg = [list(range(NCORES))]
        for i, (iap, oap) in enumerate(zip(ins, outs)):
            wr = [writes[i]]
            p = self._buf_dma_prod(writes[i], is_cc=True)
            p.step = 1
            for sem, c in self._waits_for(q.engine, reads, wr):
                self.seq[q.engine].append(("wait", sem, c))
            p.cnt += 1
            self.seq[q.engine].append(
                ("inst", lambda E, iap=iap, oap=oap: E.collective_compute(
                    kind, op, replica_groups=rg, ins=[iap], outs=[oap]),
                 p.sem, 1))
            self._mark(p, p.cnt, reads, wr)

    def barrier(self):
        snap = [(p, p.cnt) for p in self.prods if not p.is_cc]
        for e in ENGS:
            w = self.waited[e]
            for p, c in snap:
                if p is self.eng[e]:
                    continue
                if c > 0 and w.get(p.idx, 0) < c:
                    self.seq[e].append(("wait", p.sem, c))
                    w[p.idx] = c

    def finalize(self, out_buf):
        w = self.waited["sync"]
        p = out_buf.dmap
        assert p is not None and p.cnt > 0, "no output stores recorded"
        if w.get(p.idx, 0) < p.cnt:
            self.seq["sync"].append(("wait", p.sem, p.cnt))
            w[p.idx] = p.cnt

    def replay(self):
        with self.nc.Block() as blk:
            for ename in ENGS:
                items = self.seq[ename]

                def run(E, items=items):
                    for it in items:
                        if it[0] == "wait":
                            E.wait_ge(it[1], it[2])
                        else:
                            _, build, sem, inc = it
                            ins = build(E)
                            if sem is not None:
                                ins.then_inc(sem, inc)

                getattr(blk, ename)(run)



def _host_consts(core):
    import ml_dtypes
    pos = (np.arange(NT) * NCORES + core) * 128
    pos = (pos[:, None] + np.arange(128)[None, :]).reshape(-1).astype(np.float64)
    theta = 1.0 / (10000.0 ** (np.arange(0, 256, 2, dtype=np.float64) / 256.0))
    ang = pos[:, None] * theta[None, :]
    cos2 = np.repeat(np.cos(ang), 2, axis=1).astype(np.float32)
    sin2 = np.repeat(np.sin(ang), 2, axis=1).astype(np.float32)
    p = np.arange(128)[:, None]
    ck = np.arange(1024)[None, :]
    c = ck // 128
    kk = ck % 128
    keep = (c - core) * 128 + kk <= p
    tailmask = np.where(keep, 0.0, NEG).astype(np.float32)
    return cos2, sin2, tailmask


def _host_shared_consts():
    import ml_dtypes
    ident_bf_u16 = np.eye(128, dtype=ml_dtypes.bfloat16).view(np.uint16)
    ident_f32 = np.eye(128, dtype=np.float32)
    ones_row = np.ones((1, 128), dtype=np.float32)
    return ident_bf_u16, ident_f32, ones_row


def _shard_idx(core):
    return ((np.arange(NT)[:, None] * NCORES + core) * 128
            + np.arange(128)[None, :]).reshape(-1)



def build_program(n_sub=4):
    nc = bass.Bass("TRN2", target_bir_lowering=False, debug=False)
    es = ExitStack()
    mk = MK(nc, es)

    T = {}

    def din(name, shape, dt=F32):
        T[name] = nc.dram_tensor(name, list(shape), dt, kind="ExternalInput").ap()
        return T[name]

    def dint(name, shape, dt, addr_space="Local"):
        T[name] = nc.dram_tensor(name, list(shape), dt, kind="Internal",
                                 addr_space=addr_space).ap()
        return T[name]

    din("x", (TLOC, D))
    din("aWq", (D, 4096)); din("aWk", (D, 1024)); din("aWv", (D, 512)); din("aWo", (D, D))
    din("lq1", (16, 128)); din("lq2", (16, 128)); din("lk1", (16, 128)); din("lk2", (16, 128))
    din("rms_w_tile", (128, 128))
    din("bWq", (D, D)); din("bWk", (D, D)); din("bWv", (D, D)); din("bWo", (D, D))
    din("bbq", (16, 128)); din("bbk", (16, 128)); din("bbv", (16, 128)); din("bbo", (1, D))
    din("gw", (NEXP, D))
    din("gup_my", (4, 2 * INTER, D))
    din("dwn_my", (4, D, INTER))
    din("alpha", (1, 1))
    din("cos2", (TLOC, 256)); din("sin2", (TLOC, 256))
    din("tailmask", (128, 1024))
    din("ident_bf_u16", (128, 128), mybir.dt.uint16); din("ident_f32", (128, 128))
    din("ones_row", (1, 128))
    out_d = nc.dram_tensor("out", [TLOC, D], F32, kind="ExternalOutput").ap()
    T["out"] = out_d

    dint("R1", (TLOC, D), F32)
    dint("R2", (TLOC, D), F32)
    dint("R3", (TLOC, D), F32)
    dint("ka_loc", (1024, TLOC), BF16)
    dint("va_loc", (TLOC, 512), BF16)
    dint("ka_all", (1024 * NCORES, TLOC), BF16, addr_space="Shared")
    dint("va_all", (TLOC * NCORES, 512), BF16, addr_space="Shared")
    dint("kb_loc", (D, TLOC), BF16)
    dint("vb_loc", (TLOC, D), BF16)
    dint("kb_all", (D * NCORES, TLOC), BF16, addr_space="Shared")
    dint("vb_all", (TLOC * NCORES, D), BF16, addr_space="Shared")
    dint("gupT_loc", (4 * D, 2 * INTER), BF16)
    dint("dwnT_loc", (4 * INTER, D), BF16)
    dint("gupT_all", (4 * D * NCORES, 2 * INTER), BF16, addr_space="Shared")
    dint("dwnT_all", (4 * INTER * NCORES, D), BF16, addr_space="Shared")

    q_const = mk.new_q("const", "sync")
    q_cc = mk.new_q("cc", "gpsimd")

    sb = {}

    def alloc(name, shape, dt, scope=es):
        t = scope.enter_context(nc.sbuf_tensor(name, list(shape), dt))
        sb[name] = t
        return t

    c_lnT = alloc("c_lnT", (128, 16 * 1024), BF16)
    ident_bf_u16 = alloc("s_ident_bf", (128, 128), mybir.dt.uint16)
    ident_f32 = alloc("s_ident_f32", (128, 128), F32)
    ones_row = alloc("s_ones_row", (1, 128), F32)

    b = {}

    def B(name):
        if name not in b:
            b[name] = Buf(name)
        return b[name]

    mk.dma(q_const, ident_bf_u16[:], T["ident_bf_u16"], writes=[B("ident_bf")])
    mk.dma(q_const, ident_f32[:], T["ident_f32"], writes=[B("ident_f32")])
    mk.dma(q_const, ones_row[:], T["ones_row"], writes=[B("ones_row")])

    ident_bf_view = ident_bf_u16[:].bitcast(BF16)
    ctx = dict(nc=nc, mk=mk, es=es, T=T, sb=sb, B=B, alloc=alloc,
               q_const=q_const, q_cc=q_cc,
               c_lnT=c_lnT, ident_bf=ident_bf_view, ident_f32=ident_f32,
               ones_row=ones_row)

    for k in range(16):
        B(f"lnT{k}")

    _build_phases(ctx, n_sub)

    mk.replay()
    es.close()
    return nc


def _emit_ln(ctx, src_ap, src_buf, tag, with_gating=False, scope=None):
    nc, mk, B = ctx["nc"], ctx["mk"], ctx["B"]
    es = scope if scope is not None else ctx["es"]
    alloc = ctx["alloc"]
    c_lnT = ctx["c_lnT"]
    q_r = mk.new_q(f"lnr_{tag}", "sync")
    rt = [alloc(f"ln_rt{i}_{tag}", (128, D), F32, es) for i in range(2)]
    dump = alloc(f"ln_dump_{tag}", (128, D), F32, es)
    lnbf = alloc(f"ln_bf_{tag}", (128, D), BF16, es)
    stats = alloc(f"ln_stats_{tag}", (128, 8), F32, es)
    p2 = es.enter_context(nc.psum_tensor(f"ln_p2_{tag}", [128, 1024], F32))
    ptA = es.enter_context(nc.psum_tensor(f"ln_ptA_{tag}", [128, 1024], BF16))
    ptB = es.enter_context(nc.psum_tensor(f"ln_ptB_{tag}", [128, 1024], BF16))
    if with_gating:
        gat = _gating_prep(ctx, es, tag)

    for tt in range(NT):
        sl = tt % 2
        brt = B(f"ln_rt{sl}")
        mk.dma(q_r, rt[sl][:], src_ap[tt * 128:(tt + 1) * 128, :],
               reads=[src_buf], writes=[brt])
        bst = B("ln_stats")
        mk.emit("vector", lambda E, i=rt[sl][:], o=stats[:, 0:1]:
                E.reduce_sum(o, i, axis=AX.X), reads=[brt], writes=[bst])
        mk.emit("vector", lambda E, i=stats[:, 0:1], o=stats[:, 1:2]:
                E.tensor_scalar_mul(o, i, -1.0 / D), reads=[bst], writes=[bst])
        bdump = B("ln_dump")
        mk.emit("scalar", lambda E, i=rt[sl][:], o=dump[:], m=stats[:, 1:2], a=stats[:, 2:3]:
                E.activation(o, i, AF.Square, bias=m, scale=1.0, accum_out=a),
                reads=[brt, bst], writes=[bdump, bst])
        mk.emit("vector", lambda E, i=stats[:, 2:3], o=stats[:, 5:6]:
                E.tensor_scalar(o, i, 1.0 / D, EPS_LN, ALU.mult, ALU.add),
                reads=[bst], writes=[bst])
        mk.emit("scalar", lambda E, i=stats[:, 5:6], o=stats[:, 3:4]:
                E.activation(o, i, AF.Ln, bias=0.0, scale=1.0),
                reads=[bst], writes=[bst])
        mk.emit("scalar", lambda E, i=stats[:, 3:4], o=stats[:, 4:5]:
                E.activation(o, i, AF.Exp, bias=0.0, scale=-0.5),
                reads=[bst], writes=[bst])
        blnbf = B("ln_bf")
        mk.emit("vector", lambda E, i=rt[sl][:], o=lnbf[:], m=stats[:, 1:2], r=stats[:, 4:5]:
                E.tensor_scalar(o, i, m, r, ALU.add, ALU.mult),
                reads=[brt, bst], writes=[blnbf])
        for k in range(16):
            pt = ptA if k % 2 == 0 else ptB
            bpt = B(f"ln_pt{k % 2}")
            mk.emit("tensor", lambda E, o=pt[:, 0:128], i=lnbf[:, k * 128:(k + 1) * 128],
                    idn=ctx["ident_bf"]:
                    E.transpose(o, i, idn),
                    reads=[blnbf, B("ident_bf")], writes=[bpt])
            mk.emit("vector", lambda E, o=c_lnT[:, k * 1024 + tt * 128: k * 1024 + (tt + 1) * 128],
                    i=pt[:, 0:128]:
                    E.tensor_copy(o, i),
                    reads=[bpt], writes=[B(f"lnT{k}")])
        if with_gating:
            _gating_tile(ctx, gat, tt, rt[sl], brt, stats, bst, p2, tag)
    return gat if with_gating else None


def _gating_prep(ctx, es, tag):
    nc, mk, B, alloc = ctx["nc"], ctx["mk"], ctx["B"], ctx["alloc"]
    g = {}
    g["gwT"] = alloc(f"gwT_{tag}", (128, 16 * 32), F32, es)
    g["comb"] = ctx["sb"][f"comb_{tag}"][:] if f"comb_{tag}" in ctx["sb"] else None
    assert g["comb"] is not None, f"comb_{tag} must be preallocated by caller"
    g["ln32"] = alloc(f"ln32_{tag}", (128, D), F32, es)
    g["lnT32"] = alloc(f"lnT32_{tag}", (128, D), F32, es)
    g["gsc"] = alloc(f"gsc_{tag}", (128, 80), F32, es)
    g["alpha_b"] = alloc(f"alpha_b_{tag}", (128, 1), F32, es)
    g["gwsb"] = alloc(f"gwsb_{tag}", (32, D), F32, es)
    q_g = mk.new_q(f"gw_{tag}", "sync")
    g["q"] = q_g
    p4 = es.enter_context(nc.psum_tensor(f"g_p4_{tag}", [128, 512], F32))
    g["p4"] = p4
    mk.dma(q_g, g["gwsb"][:], ctx["T"]["gw"], writes=[B("gwsb")])
    for k in range(16):
        mk.emit("tensor", lambda E, o=p4[:, 0:32], i=g["gwsb"][:, k * 128:(k + 1) * 128],
                idn=ctx["ident_f32"][0:32, 0:32]:
                E.transpose(o, i, idn),
                reads=[B("gwsb"), B("ident_f32")], writes=[B("g_p4")])
        mk.emit("vector", lambda E, o=g["gwT"][:, k * 32:(k + 1) * 32], i=p4[:, 0:32]:
                E.tensor_copy(o, i),
                reads=[B("g_p4")], writes=[B("gwT")])
    al = alloc(f"alpha_sb_{tag}", (1, 1), F32, es)
    mk.dma(q_g, al[:], ctx["T"]["alpha"], writes=[B("alpha_sb")])
    mk.emit("tensor", lambda E, o=p4[:, 0:1], l=ctx["ones_row"][:], r=al[:]:
            E.matmul(o, l, r, start=True, stop=True),
            reads=[B("alpha_sb"), B("ones_row")], writes=[B("g_p4")])
    mk.emit("vector", lambda E, o=g["alpha_b"][:], i=p4[:, 0:1]:
            E.tensor_copy(o, i),
            reads=[B("g_p4")], writes=[B("alpha_b")])
    g["tag"] = tag
    return g


def _gating_tile(ctx, g, tt, rt, brt, stats, bst, p2, tag):
    nc, mk, B = ctx["nc"], ctx["mk"], ctx["B"]
    b32 = B("ln32")
    mk.emit("vector", lambda E, i=rt[:], o=g["ln32"][:], m=stats[:, 1:2], r=stats[:, 4:5]:
            E.tensor_scalar(o, i, m, r, ALU.add, ALU.mult),
            reads=[brt, bst], writes=[b32])
    bT32 = B("lnT32")
    for k in range(16):
        half = (k % 2) * 512
        bp = B(f"g_p2{k % 2}")
        mk.emit("tensor", lambda E, o=p2[:, half:half + 128], i=g["ln32"][:, k * 128:(k + 1) * 128],
                idn=ctx["ident_f32"][:]:
                E.transpose(o, i, idn),
                reads=[b32, B("ident_f32")], writes=[bp])
        mk.emit("vector", lambda E, o=g["lnT32"][:, k * 128:(k + 1) * 128], i=p2[:, half:half + 128]:
                E.tensor_copy(o, i),
                reads=[bp], writes=[bT32])
    p4 = g["p4"]
    bpl = B("g_logit")
    for k in range(16):
        mk.emit("tensor", lambda E, o=p4[:, 64:96], l=g["lnT32"][:, k * 128:(k + 1) * 128],
                r=g["gwT"][:, k * 32:(k + 1) * 32], s=(k == 0), st=(k == 15):
                E.matmul(o, l, r, start=s, stop=st),
                reads=[bT32, B("gwT")], writes=[bpl], inc=(k == 15))
    gs = ctx["sb"][f"gsc_{g['tag']}"]
    bgs = B("gsc")
    probs = gs[:, 0:32]
    top8 = gs[:, 32:40]
    mk.emit("vector", lambda E, o=gs[:, 40:41], i=p4[:, 64:96]:
            E.reduce_max(o, i, axis=AX.X, negate=True), reads=[bpl], writes=[bgs])
    mk.emit("scalar", lambda E, o=probs, i=p4[:, 64:96], m=gs[:, 40:41], a=gs[:, 41:42]:
            E.activation(o, i, AF.Exp, bias=m, scale=1.0, accum_out=a),
            reads=[bpl, bgs], writes=[bgs])
    mk.emit("vector", lambda E, o=gs[:, 42:43], i=gs[:, 41:42]:
            E.reciprocal(o, i), reads=[bgs], writes=[bgs])
    mk.emit("vector", lambda E, o=probs, i=probs, r=gs[:, 42:43]:
            E.tensor_scalar_mul(o, i, r), reads=[bgs], writes=[bgs])
    mk.emit("vector", lambda E, o=top8, i=probs:
            E.max(o, i), reads=[bgs], writes=[bgs])
    mk.emit("vector", lambda E, o=gs[:, 43:44], i=top8:
            E.reduce_sum(o, i, axis=AX.X), reads=[bgs], writes=[bgs])
    pa = gs[:, 48:80]
    mk.emit("vector", lambda E, o=gs[:, 44:45], i=gs[:, 43:44]:
            E.tensor_scalar_mul(o, i, TAU), reads=[bgs], writes=[bgs])
    mk.emit("vector", lambda E, o=pa, i=probs, r=ctx["sb"][f"alpha_b_{g['tag']}"][:]:
            E.tensor_scalar_mul(o, i, r), reads=[bgs, B("alpha_b")], writes=[bgs])
    mk.emit("vector", lambda E, o=pa, i=pa, r=gs[:, 44:45]:
            E.tensor_single_scalar(o, i, r, ALU.is_ge), reads=[bgs], writes=[bgs])
    mk.emit("vector", lambda E, o=gs[:, 45:46], i=top8[:, 0:1], r=ctx["sb"][f"alpha_b_{g['tag']}"][:]:
            E.tensor_scalar_mul(o, i, r), reads=[bgs, B("alpha_b")], writes=[bgs])
    mk.emit("vector", lambda E, o=gs[:, 45:46], i=gs[:, 45:46], r=gs[:, 44:45]:
            E.tensor_single_scalar(o, i, r, ALU.is_lt), reads=[bgs], writes=[bgs])
    fb = gs[:, 0:32]
    comb_t = g["comb"][:, tt * 32:(tt + 1) * 32]
    bcomb = B("comb")
    mk.emit("vector", lambda E, o=comb_t, i=probs, r=top8[:, 0:1]:
            E.tensor_single_scalar(o, i, r, ALU.is_equal), reads=[bgs], writes=[bcomb])
    mk.emit("vector", lambda E, o=comb_t, i=comb_t, r=gs[:, 45:46]:
            E.tensor_scalar_mul(o, i, r), reads=[bcomb, bgs], writes=[bcomb])
    mk.emit("vector", lambda E, o=pa, i0=pa, i1=comb_t:
            E.tensor_add(o, i0, i1), reads=[bgs, bcomb], writes=[bgs])
    mk.emit("vector", lambda E, o=comb_t, i=probs, r=top8[:, 7:8]:
            E.tensor_single_scalar(o, i, r, ALU.is_ge), reads=[bgs], writes=[bcomb])
    mk.emit("vector", lambda E, o=pa, i0=pa, i1=comb_t:
            E.tensor_mul(o, i0, i1), reads=[bgs, bcomb], writes=[bgs])
    mk.emit("vector", lambda E, o=comb_t, i0=probs, i1=pa:
            E.tensor_mul(o, i0, i1), reads=[bgs], writes=[bcomb])
    mk.emit("vector", lambda E, o=gs[:, 46:47], i=comb_t:
            E.reduce_sum(o, i, axis=AX.X), reads=[bcomb], writes=[bgs])
    mk.emit("vector", lambda E, o=gs[:, 46:47], i=gs[:, 46:47]:
            E.tensor_single_scalar(o, i, 1e-12, ALU.max), reads=[bgs], writes=[bgs])
    mk.emit("vector", lambda E, o=gs[:, 47:48], i=gs[:, 46:47]:
            E.reciprocal(o, i), reads=[bgs], writes=[bgs])
    mk.emit("vector", lambda E, o=comb_t, i=comb_t, r=gs[:, 47:48]:
            E.tensor_scalar_mul(o, i, r), reads=[bcomb, bgs], writes=[bcomb])


def _phase_kv_a(ctx, scope):
    nc, mk, B, T, alloc = ctx["nc"], ctx["mk"], ctx["B"], ctx["T"], ctx["alloc"]
    c_lnT = ctx["c_lnT"]
    es = scope
    wk = alloc("kv_wk", (128, 16 * 1024), BF16, es)
    wv = alloc("kv_wv", (128, 16 * 512), BF16, es)
    krt = alloc("kv_krt", (128, 1024), BF16, es)
    tco = alloc("kv_tcos", (128, 256), F32, es)
    tsi = alloc("kv_tsin", (128, 256), F32, es)
    vbf = alloc("kv_vbf", (128, 512), BF16, es)
    kev = alloc("kv_kev", (128, 256), BF16, es)
    p2a = es.enter_context(nc.psum_tensor("kv_p2a", [128, 1024], F32))
    p2b = es.enter_context(nc.psum_tensor("kv_p2b", [128, 1024], F32))
    p1 = es.enter_context(nc.psum_tensor("kv_p1", [128, 512], F32))
    ptA = es.enter_context(nc.psum_tensor("kv_ptA", [128, 1024], BF16))
    ptB = es.enter_context(nc.psum_tensor("kv_ptB", [128, 1024], BF16))
    ctx["kv_ptA"], ctx["kv_ptB"] = ptA, ptB
    q_w = mk.new_q("kvw", "gpsimd")
    q_st = mk.new_q("kvst", "scalar")

    for k in range(16):
        mk.dma(q_w, wk[:, k * 1024:(k + 1) * 1024], T["aWk"][k * 128:(k + 1) * 128, :],
               writes=[B("kv_wk")])
        mk.dma(q_w, wv[:, k * 512:(k + 1) * 512], T["aWv"][k * 128:(k + 1) * 128, :],
               writes=[B("kv_wv")])
    for tt in range(NT):
        p2 = p2a if tt % 2 == 0 else p2b
        bp2 = B(f"kv_p2{tt % 2}")
        for k in range(16):
            for hl in range(2):
                mk.emit("tensor", lambda E, o=p2[:, hl * 512:(hl + 1) * 512],
                        l=c_lnT[:, k * 1024 + tt * 128: k * 1024 + (tt + 1) * 128],
                        r=wk[:, k * 1024 + hl * 512: k * 1024 + (hl + 1) * 512],
                        s=(k == 0), st=(k == 15):
                        E.matmul(o, l, r, start=s, stop=st),
                        reads=[B(f"lnT{k}"), B("kv_wk")], writes=[bp2], inc=(k == 15))
        bkrt = B("kv_krt")
        for kh in range(4):
            c0 = kh * 256
            mk.emit("vector", lambda E, o=tco[:], i0=p2[:, c0:c0 + 256],
                    i1=ctx["cos2_sb"][:, tt * 256:(tt + 1) * 256]:
                    E.tensor_mul(o, i0, i1),
                    reads=[bp2, B("cos2_sb")], writes=[B("kv_tcos")])
            mk.emit("vector", lambda E, o=tsi[:], i0=p2[:, c0:c0 + 256],
                    i1=ctx["sin2_sb"][:, tt * 256:(tt + 1) * 256]:
                    E.tensor_mul(o, i0, i1),
                    reads=[bp2, B("sin2_sb")], writes=[B("kv_tsin")])
            mk.emit("vector", lambda E, o=krt[:, c0:c0 + 256:2], i0=tco[:, 0:256:2],
                    i1=tsi[:, 1:256:2]:
                    E.tensor_sub(o, i0, i1),
                    reads=[B("kv_tcos"), B("kv_tsin")], writes=[bkrt])
            mk.emit("vector", lambda E, o=krt[:, c0 + 1:c0 + 256:2], i0=tsi[:, 0:256:2],
                    i1=tco[:, 1:256:2]:
                    E.tensor_add(o, i0, i1),
                    reads=[B("kv_tcos"), B("kv_tsin")], writes=[bkrt])
        for c in range(8):
            pt = ptA if c % 2 == 0 else ptB
            bpt = B(f"kv_pt{c % 2}")
            mk.emit("tensor", lambda E, o=pt[:, 0:128], i=krt[:, c * 128:(c + 1) * 128],
                    idn=ctx["ident_bf"]:
                    E.transpose(o, i, idn),
                    reads=[bkrt, B("ident_bf")], writes=[bpt])
            bke = B(f"kv_kev{c % 2}")
            mk.emit("vector", lambda E, o=kev[:, (c % 2) * 128:(c % 2) * 128 + 128], i=pt[:, 0:128]:
                    E.tensor_copy(o, i),
                    reads=[bpt], writes=[bke])
            mk.dma(q_st, T["ka_loc"][c * 128:(c + 1) * 128, tt * 128:(tt + 1) * 128],
                   kev[:, (c % 2) * 128:(c % 2) * 128 + 128],
                   reads=[bke], writes=[B("ka_loc")])
        bp1 = B("kv_p1")
        for k in range(16):
            mk.emit("tensor", lambda E, o=p1[:, 0:512],
                    l=c_lnT[:, k * 1024 + tt * 128: k * 1024 + (tt + 1) * 128],
                    r=wv[:, k * 512:(k + 1) * 512], s=(k == 0), st=(k == 15):
                    E.matmul(o, l, r, start=s, stop=st),
                    reads=[B(f"lnT{k}"), B("kv_wv")], writes=[bp1], inc=(k == 15))
        mk.emit("vector", lambda E, o=vbf[:], i=p1[:, 0:512]:
                E.tensor_copy(o, i), reads=[bp1], writes=[B("kv_vbf")])
        mk.dma(q_st, T["va_loc"][tt * 128:(tt + 1) * 128, :], vbf[:],
               reads=[B("kv_vbf")], writes=[B("va_loc")])
    mk.collective(ctx["q_cc"], "AllGather", ALU.bypass,
                  [T["ka_loc"].opt(), T["va_loc"].opt()],
                  [T["ka_all"].opt(), T["va_all"].opt()],
                  reads=[B("ka_loc"), B("va_loc")],
                  writes=[B("ka_all"), B("va_all")])


def _phase_stage0(ctx, scope, core_experts=4):
    nc, mk, B, T, alloc = ctx["nc"], ctx["mk"], ctx["B"], ctx["T"], ctx["alloc"]
    es = scope
    wt = alloc("s0_wt", (128, 2 * 2048), BF16, es)
    wev = alloc("s0_wev", (128, 2 * 128), BF16, es)
    ptA, ptB = ctx["kv_ptA"], ctx["kv_ptB"]
    q_l = mk.new_q("s0l", "gpsimd")
    q_s = mk.new_q("s0s", "scalar")

    def transpose_store(src_view, n_f, dst_rowbase, dst_col, dst_ap, slot, tag):
        for f in range(n_f):
            pt = ptA if f % 2 == 0 else ptB
            bpt = B(f"kv_pt{f % 2}")
            mk.emit("tensor", lambda E, o=pt[:, 0:128], i=src_view[:, f * 128:(f + 1) * 128],
                    idn=ctx["ident_bf"]:
                    E.transpose(o, i, idn),
                    reads=[B(f"s0_wt{slot}"), B("ident_bf")], writes=[bpt])
            bev = B(f"s0_wev{f % 2}")
            mk.emit("vector", lambda E, o=wev[:, (f % 2) * 128:(f % 2) * 128 + 128], i=pt[:, 0:128]:
                    E.tensor_copy(o, i), reads=[bpt], writes=[bev])
            mk.dma(q_s, dst_ap[dst_rowbase + f * 128: dst_rowbase + (f + 1) * 128,
                              dst_col:dst_col + 128],
                   wev[:, (f % 2) * 128:(f % 2) * 128 + 128],
                   reads=[bev], writes=[B(tag)])

    for e4 in range(core_experts):
        for ib in range(12):
            slot = ib % 2
            bwt = B(f"s0_wt{slot}")
            mk.dma(q_l, wt[:, slot * 2048:(slot + 1) * 2048],
                   T["gup_my"][e4, ib * 128:(ib + 1) * 128, :],
                   writes=[bwt])
            transpose_store(wt[:, slot * 2048:(slot + 1) * 2048], 16,
                            e4 * 2048, ib * 128, T["gupT_loc"], slot, "gupT_loc")
        for ob in range(16):
            slot = ob % 2
            bwt = B(f"s0_wt{slot}")
            mk.dma(q_l, wt[:, slot * 2048: slot * 2048 + 768],
                   T["dwn_my"][e4, ob * 128:(ob + 1) * 128, :],
                   writes=[bwt])
            transpose_store(wt[:, slot * 2048: slot * 2048 + 768], 6,
                            e4 * 768, ob * 128, T["dwnT_loc"], slot, "dwnT_loc")
    mk.collective(ctx["q_cc"], "AllGather", ALU.bypass,
                  [T["gupT_loc"].opt(), T["dwnT_loc"].opt()],
                  [T["gupT_all"].opt(), T["dwnT_all"].opt()],
                  reads=[B("gupT_loc"), B("dwnT_loc")],
                  writes=[B("gupT_all"), B("dwnT_all")])


def _phase_att_a(ctx, outer, n_heads=16):
    nc, mk, B, T, alloc = ctx["nc"], ctx["mk"], ctx["B"], ctx["T"], ctx["alloc"]
    es = outer
    cS_t = alloc("aS", (128, 8192), F32, es)
    cS = cS_t[:]
    cE0 = cS_t[:].bitcast(BF16)[:, 0:8192]
    cE_t = alloc("aE", (128, 8192), BF16, es)
    cK_t = alloc("aK", (128, 2 * 8192), BF16, es)
    cV_t = alloc("aV", (128, 8192), BF16, es)
    qfm = alloc("aqfm", (128, 2 * 1024), BF16, es)
    wq = alloc("awq", (128, 2 * 256), BF16, es)
    tmask = alloc("atm", (128, 1024), F32, es)
    zst = alloc("azst", (128, 96), F32, es)
    d0 = alloc("ad0", (128, 512), F32, es)
    t0 = alloc("at0", (128, 512), F32, es)
    pps = alloc("apps", (128, 1024), BF16, es)
    pts = alloc("apts", (128, 1024), BF16, es)
    tco = alloc("atcos", (128, 256), F32, es)
    tsi = alloc("atsin", (128, 256), F32, es)
    qrt = alloc("aqrt", (128, 256), BF16, es)
    ae = alloc("aae", (128, 128), F32, es)
    ae2 = alloc("aae2", (128, 128), F32, es)
    dump = alloc("adump", (128, 128), F32, es)
    aebf = alloc("aaebf", (128, 128), BF16, es)
    pBig = es.enter_context(nc.psum_tensor("a_pbig", [128, 2048], F32))
    ptA = es.enter_context(nc.psum_tensor("a_ptA", [128, 1024], BF16))
    ptB = es.enter_context(nc.psum_tensor("a_ptB", [128, 1024], BF16))
    q_k = mk.new_q("akv", "sync")
    q_k2 = mk.new_q("akv2", "scalar")
    q_wq = mk.new_q("awq", "gpsimd")

    mk.dma(mk.new_q("atm", "sync"), tmask[:], T["tailmask"], writes=[B("atm")])
    bq = [B(f"a_pb{i}") for i in range(4)]
    bptA, bptB = B("a_ptA"), B("a_ptB")
    c_att = ctx["c_att"]

    for g in range(4):
        for a in range(2):
            bK = B(f"aK{a}")
            for r in range(NCORES):
                src = T["ka_all"][r * 1024 + (g * 2 + a) * 128:
                                  r * 1024 + (g * 2 + a) * 128 + 128, :]
                src = src.rearrange("d (j t) -> d j t", t=128)
                dst = cK_t[:, a * 8192:(a + 1) * 8192].rearrange(
                    "p (blk t) -> p blk t", t=128)[:, r::8, :]
                mk.dma(q_k, dst, src, reads=[B("ka_all")], writes=[bK])
        bV = B("aV")
        for r in range(NCORES):
            src = T["va_all"][r * 1024:(r + 1) * 1024, g * 128:(g + 1) * 128]
            src = src.rearrange("(j t) d -> t j d", t=128)
            dst = cV_t[:].rearrange("p (blk d) -> p blk d", d=128)[:, r::8, :]
            mk.dma(q_k2, dst, src, reads=[B("va_all")], writes=[bV])

        for h4 in range(4):
            h = g * 4 + h4
            if h >= n_heads:
                continue
            bqfm = B("aqfm")
            for k in range(16):
                slot = k % 2
                bwq = B(f"awq{slot}")
                mk.dma(q_wq, wq[:, slot * 256:(slot + 1) * 256],
                       T["aWq"][k * 128:(k + 1) * 128, h * 256:(h + 1) * 256],
                       writes=[bwq])
                for tt in range(NT):
                    mk.emit("tensor", lambda E, o=pBig[:, tt * 256:(tt + 1) * 256],
                            l=c_lnT_slice(ctx, k, tt),
                            r=wq[:, slot * 256:(slot + 1) * 256],
                            s=(k == 0 and tt % 2 == 0), st=(k == 15 and tt % 2 == 1):
                            E.matmul(o, l, r, start=s, stop=st),
                            reads=[B(f"lnT{k}"), bwq], writes=[bq[tt // 2]],
                            inc=(tt == NT - 1) or (k == 15))
            for tt in range(NT):
                bnk = bq[tt // 2]
                mk.emit("vector", lambda E, o=tco[:], i0=pBig[:, tt * 256:(tt + 1) * 256],
                        i1=ctx["cos2_sb"][:, tt * 256:(tt + 1) * 256]:
                        E.tensor_mul(o, i0, i1),
                        reads=[bnk, B("cos2_sb")], writes=[B("atcos")])
                mk.emit("vector", lambda E, o=tsi[:], i0=pBig[:, tt * 256:(tt + 1) * 256],
                        i1=ctx["sin2_sb"][:, tt * 256:(tt + 1) * 256]:
                        E.tensor_mul(o, i0, i1),
                        reads=[bnk, B("sin2_sb")], writes=[B("atsin")])
                mk.emit("vector", lambda E, o=qrt[:, 0:256:2], i0=tco[:, 0:256:2],
                        i1=tsi[:, 1:256:2]:
                        E.tensor_sub(o, i0, i1),
                        reads=[B("atcos"), B("atsin")], writes=[B("aqrt")])
                mk.emit("vector", lambda E, o=qrt[:, 1:256:2], i0=tsi[:, 0:256:2],
                        i1=tco[:, 1:256:2]:
                        E.tensor_add(o, i0, i1),
                        reads=[B("atcos"), B("atsin")], writes=[B("aqrt")])
                for a in range(2):
                    pt = ptA if a == 0 else ptB
                    bpt = bptA if a == 0 else bptB
                    mk.emit("tensor", lambda E, o=pt[:, 0:128],
                            i=qrt[:, a * 128:(a + 1) * 128], idn=ctx["ident_bf"]:
                            E.transpose(o, i, idn),
                            reads=[B("aqrt"), B("ident_bf")], writes=[bpt])
                    mk.emit("vector", lambda E,
                            o=qfm[:, a * 1024 + tt * 128: a * 1024 + (tt + 1) * 128],
                            i=pt[:, 0:128]:
                            E.tensor_copy(o, i), reads=[bpt], writes=[bqfm])
            for j in range(NT):
                _att_a_unit(ctx, h, j, cS, cE0, cE_t, cK_t, cV_t, qfm, tmask,
                            zst, d0, t0, pps, pts, ae, ae2, dump, aebf,
                            pBig, ptA, ptB, bq, bptA, bptB)


def c_lnT_slice(ctx, k, tt):
    return ctx["c_lnT"][:, k * 1024 + tt * 128: k * 1024 + (tt + 1) * 128]


def _att_a_unit(ctx, h, j, cS, cE0, cE_t, cK_t, cV_t, qfm, tmask, zst, d0, t0,
                pps, pts, ae, ae2, dump, aebf, pBig, ptA, ptB, bq, bptA, bptB):
    nc, mk, B = ctx["nc"], ctx["mk"], ctx["B"]
    nslab = 2 * (j + 1)
    ext = nslab * 512
    bS, bE, bz = B("aS"), B("aE"), B("azst")
    for a in (1, 0):
        for s in range(nslab):
            bank = bq[s % 2]
            mk.emit("tensor", lambda E, o=pBig[:, (s % 2) * 512:(s % 2) * 512 + 512],
                    l=qfm[:, a * 1024 + j * 128: a * 1024 + (j + 1) * 128],
                    r=cK_t[:, a * 8192 + s * 512: a * 8192 + (s + 1) * 512]:
                    E.matmul(o, l, r, start=True, stop=True),
                    reads=[B("aqfm"), B(f"aK{a}")], writes=[bank])
            if s >= nslab - 2:
                tm0 = (s - (nslab - 2)) * 512
                mk.emit("vector", lambda E, o=cS[:, s * 512:(s + 1) * 512],
                        i0=pBig[:, (s % 2) * 512:(s % 2) * 512 + 512],
                        i1=tmask[:, tm0:tm0 + 512]:
                        E.scalar_tensor_tensor(o, i0, SCL_A, i1, ALU.mult, ALU.add),
                        reads=[bank, B("atm")], writes=[bS])
            elif a == 1:
                mk.emit("scalar", lambda E, o=cS[:, s * 512:(s + 1) * 512],
                        i=pBig[:, (s % 2) * 512:(s % 2) * 512 + 512]:
                        E.activation(o, i, AF.Copy, bias=0.0, scale=SCL_A),
                        reads=[bank], writes=[bS])
            else:
                mk.emit("vector", lambda E, o=cS[:, s * 512:(s + 1) * 512],
                        i=pBig[:, (s % 2) * 512:(s % 2) * 512 + 512]:
                        E.tensor_scalar_mul(o, i, SCL_A),
                        reads=[bank], writes=[bS])
            mk.emit("vector", lambda E, o=zst[:, 64 + s:65 + s],
                    i=cS[:, s * 512:(s + 1) * 512]:
                    E.reduce_max(o, i, axis=AX.X),
                    reads=[bS], writes=[bz])
        if a == 1:
            mk.emit("vector", lambda E, o=zst[:, 0:1], i=zst[:, 64:64 + nslab]:
                    E.reduce_max(o, i, axis=AX.X, negate=True),
                    reads=[bz], writes=[bz])
            mk.emit("scalar", lambda E, o=cE_t[:, 0:ext], i=cS[:, 0:ext],
                    m=zst[:, 0:1], acc=zst[:, 1:2]:
                    E.activation(o, i, AF.Exp, bias=m, scale=1.0, accum_out=acc),
                    reads=[bS, bz], writes=[bE, bz])
            mk.emit("vector", lambda E, o=zst[:, 2:3], i=zst[:, 1:2]:
                    E.reciprocal(o, i), reads=[bz], writes=[bz])
            mk.emit("vector", lambda E, o=zst[:, 3:4], i=zst[:, 2:3],
                    r=ctx["lamb"][:, h:h + 1]:
                    E.tensor_scalar_mul(o, i, r),
                    reads=[bz, B("lamb")], writes=[bz])
        else:
            mk.emit("vector", lambda E, o=zst[:, 4:5], i=zst[:, 64:64 + nslab]:
                    E.reduce_max(o, i, axis=AX.X, negate=True),
                    reads=[bz], writes=[bz])
            for s in range(nslab):
                mk.emit("scalar", lambda E, o=cE0[:, s * 512:(s + 1) * 512],
                        i=cS[:, s * 512:(s + 1) * 512],
                        m=zst[:, 4:5], acc=zst[:, 16 + s:17 + s]:
                        E.activation(o, i, AF.Exp, bias=m, scale=1.0, accum_out=acc),
                        reads=[bS, bz], writes=[bS, bz])
            mk.emit("vector", lambda E, o=zst[:, 5:6], i=zst[:, 16:16 + nslab]:
                    E.reduce_sum(o, i, axis=AX.X), reads=[bz], writes=[bz])
            mk.emit("vector", lambda E, o=zst[:, 6:7], i=zst[:, 5:6]:
                    E.reciprocal(o, i), reads=[bz], writes=[bz])
    bd0, bt0, bpps, bpts = B("ad0"), B("at0"), B("apps"), B("apts")
    bpv = bq[2]

    def _ptevac_pv(s):
        par = s % 2
        pt = ptA if par == 0 else ptB
        bpt = bptA if par == 0 else bptB
        mk.emit("vector", lambda E, o=pts[:, par * 512:(par + 1) * 512], i=pt[:, 0:512]:
                E.tensor_copy(o, i), reads=[bpt], writes=[bpts])
        for c4 in range(4):
            gblk = s * 4 + c4
            mk.emit("tensor", lambda E, o=pBig[:, 1024:1152],
                    l=cV_t[:, gblk * 128:(gblk + 1) * 128],
                    r=pts[:, par * 512 + c4 * 128: par * 512 + (c4 + 1) * 128],
                    st_=(s == 0 and c4 == 0), sp=(s == nslab - 1 and c4 == 3):
                    E.matmul(o, l, r, start=st_, stop=sp),
                    reads=[B("aV"), bpts], writes=[bpv], inc=(c4 == 3))

    for s in range(nslab):
        par = s % 2
        mk.emit("vector", lambda E, o=t0[:], i=cE_t[:, s * 512:(s + 1) * 512],
                r=zst[:, 3:4]:
                E.tensor_scalar_mul(o, i, r),
                reads=[bE, bz], writes=[bt0])
        mk.emit("vector", lambda E, o=d0[:], i0=cE0[:, s * 512:(s + 1) * 512],
                r=zst[:, 6:7], i1=t0[:]:
                E.scalar_tensor_tensor(o, i0, r, i1, ALU.mult, ALU.subtract),
                reads=[bS, bz, bt0], writes=[bd0])
        if s >= nslab - 2:
            tm0 = (s - (nslab - 2)) * 512
            mk.emit("vector", lambda E, o=d0[:], i0=d0[:], i1=tmask[:, tm0:tm0 + 512]:
                    E.tensor_add(o, i0, i1),
                    reads=[bd0, B("atm")], writes=[bd0])
        mk.emit("scalar", lambda E, o=pps[:, par * 512:(par + 1) * 512], i=d0[:],
                acc=zst[:, 32 + s:33 + s]:
                E.activation(o, i, AF.Exp, bias=0.0, scale=1.0, accum_out=acc),
                reads=[bd0], writes=[bpps, bz])
        pt = ptA if par == 0 else ptB
        bpt = bptA if par == 0 else bptB
        for c4 in range(4):
            mk.emit("tensor", lambda E, o=pt[:, c4 * 128:(c4 + 1) * 128],
                    i=pps[:, par * 512 + c4 * 128: par * 512 + (c4 + 1) * 128],
                    idn=ctx["ident_bf"]:
                    E.transpose(o, i, idn),
                    reads=[bpps, B("ident_bf")], writes=[bpt], inc=(c4 == 3))
        if s > 0:
            _ptevac_pv(s - 1)
    _ptevac_pv(nslab - 1)
    mk.emit("vector", lambda E, o=zst[:, 7:8], i=zst[:, 32:32 + nslab]:
            E.reduce_sum(o, i, axis=AX.X), reads=[bz], writes=[bz])
    mk.emit("vector", lambda E, o=zst[:, 8:9], i=zst[:, 7:8]:
            E.reciprocal(o, i), reads=[bz], writes=[bz])
    bae, bae2, bdump, baebf = B("aae"), B("aae2"), B("adump"), B("aaebf")
    mk.emit("vector", lambda E, o=ae[:], i=pBig[:, 1024:1152]:
            E.tensor_copy(o, i), reads=[bpv], writes=[bae])
    mk.emit("tensor", lambda E, o=pBig[:, 1536:1664], i=ae[:], idn=ctx["ident_f32"][:]:
            E.transpose(o, i, idn),
            reads=[bae, B("ident_f32")], writes=[bq[3]])
    mk.emit("vector", lambda E, o=ae2[:], i=pBig[:, 1536:1664], r=zst[:, 8:9]:
            E.tensor_scalar_mul(o, i, r),
            reads=[bq[3], bz], writes=[bae2])
    mk.emit("scalar", lambda E, o=dump[:], i=ae2[:], acc=zst[:, 48:49]:
            E.activation(o, i, AF.Square, bias=0.0, scale=1.0, accum_out=acc),
            reads=[bae2], writes=[bdump, bz])
    mk.emit("vector", lambda E, o=zst[:, 51:52], i=zst[:, 48:49]:
            E.tensor_scalar(o, i, 1.0 / 128.0, EPS_RMS, ALU.mult, ALU.add),
            reads=[bz], writes=[bz])
    mk.emit("scalar", lambda E, o=zst[:, 49:50], i=zst[:, 51:52]:
            E.activation(o, i, AF.Ln, bias=0.0, scale=1.0),
            reads=[bz], writes=[bz])
    mk.emit("scalar", lambda E, o=zst[:, 50:51], i=zst[:, 49:50]:
            E.activation(o, i, AF.Exp, bias=0.0, scale=-0.5),
            reads=[bz], writes=[bz])
    mk.emit("vector", lambda E, o=ae2[:], i=ae2[:], r=zst[:, 50:51]:
            E.tensor_scalar(o, i, r, 0.8, ALU.mult, ALU.mult),
            reads=[bae2, bz], writes=[bae2])
    mk.emit("vector", lambda E, o=aebf[:], i0=ae2[:], i1=ctx["rmsw_sb"][:]:
            E.tensor_mul(o, i0, i1),
            reads=[bae2, B("rmsw_sb")], writes=[baebf])
    mk.emit("tensor", lambda E, o=ptA[:, 512:640], i=aebf[:], idn=ctx["ident_bf"]:
            E.transpose(o, i, idn),
            reads=[baebf, B("ident_bf")], writes=[bptA])
    mk.emit("vector", lambda E,
            o=ctx["c_att"][:, h * 1024 + j * 128: h * 1024 + (j + 1) * 128],
            i=ptA[:, 512:640]:
            E.tensor_copy(o, i), reads=[bptA], writes=[B(f"att{h}")])


def _phase_wo(ctx, scope, w_ap, res_ap, res_buf, dst_ap, dst_buf, q_store,
              bias_sb=None, bias_buf=None, tag="wo"):
    nc, mk, B, alloc = ctx["nc"], ctx["mk"], ctx["B"], ctx["alloc"]
    es = scope
    wob = alloc(f"{tag}_w", (128, 2 * 512), BF16, es)
    xrb = alloc(f"{tag}_x", (128, 2 * 512), F32, es)
    rob = alloc(f"{tag}_o", (128, 2 * 512), F32, es)
    pw = [es.enter_context(nc.psum_tensor(f"{tag}_p{i}", [128, 512], F32))
          for i in range(8)]
    q_w = mk.new_q(f"{tag}w", "gpsimd")
    q_x = mk.new_q(f"{tag}x", "sync")
    c_att = ctx["c_att"]
    batt = [B(f"att{h}") for h in range(16)]
    for ns in range(4):
        for k in range(16):
            slot = k % 2
            bw = B(f"wo_w{slot}")
            mk.dma(q_w, wob[:, slot * 512:(slot + 1) * 512],
                   w_ap[k * 128:(k + 1) * 128, ns * 512:(ns + 1) * 512],
                   writes=[bw])
            for tt in range(NT):
                mk.emit("tensor", lambda E, o=pw[tt][:, 0:512],
                        l=c_att[:, k * 1024 + tt * 128: k * 1024 + (tt + 1) * 128],
                        r=wob[:, slot * 512:(slot + 1) * 512],
                        s=(k == 0), st=(k == 15):
                        E.matmul(o, l, r, start=s, stop=st),
                        reads=[batt[k], bw], writes=[B(f"wo_p{tt}")],
                        inc=(tt == NT - 1) or (k == 15))
        for tt in range(NT):
            slot = tt % 2
            bx = B(f"wo_x{slot}")
            mk.dma(q_x, xrb[:, slot * 512:(slot + 1) * 512],
                   res_ap[tt * 128:(tt + 1) * 128, ns * 512:(ns + 1) * 512],
                   reads=[res_buf], writes=[bx])
            bo = B(f"wo_o{slot}")
            if bias_sb is not None:
                mk.emit("vector", lambda E, o=rob[:, slot * 512:(slot + 1) * 512],
                        i0=pw[tt][:, 0:512], i1=bias_sb[:, ns * 512:(ns + 1) * 512]:
                        E.tensor_add(o, i0, i1),
                        reads=[B(f"wo_p{tt}"), bias_buf], writes=[bo])
                mk.emit("vector", lambda E, o=rob[:, slot * 512:(slot + 1) * 512],
                        i0=rob[:, slot * 512:(slot + 1) * 512],
                        i1=xrb[:, slot * 512:(slot + 1) * 512]:
                        E.tensor_add(o, i0, i1),
                        reads=[bo, bx], writes=[bo])
            else:
                mk.emit("vector", lambda E, o=rob[:, slot * 512:(slot + 1) * 512],
                        i0=pw[tt][:, 0:512], i1=xrb[:, slot * 512:(slot + 1) * 512]:
                        E.tensor_add(o, i0, i1),
                        reads=[B(f"wo_p{tt}"), bx], writes=[bo])
            mk.dma(q_store, dst_ap[tt * 128:(tt + 1) * 128, ns * 512:(ns + 1) * 512],
                   rob[:, slot * 512:(slot + 1) * 512],
                   reads=[bo], writes=[dst_buf])


def _phase_ffn(ctx, scope, gat, src_ap, src_buf, dst_ap, dst_buf, q_store, tag):
    nc, mk, B, T, alloc = ctx["nc"], ctx["mk"], ctx["B"], ctx["T"], ctx["alloc"]
    es = scope
    acc = alloc(f"{tag}_acc", (128, NT * 2048), F32, es)
    hb = alloc(f"{tag}_h", (128, NT * 768), BF16, es)
    wg = alloc(f"{tag}_wg", (128, 2 * 8192), BF16, es)
    wd = alloc(f"{tag}_wd", (128, 2 * 3072), BF16, es)
    st_ = alloc(f"{tag}_silu", (128, 256), F32, es)
    pg = es.enter_context(nc.psum_tensor(f"{tag}_pg", [128, 1024], F32))
    pd = es.enter_context(nc.psum_tensor(f"{tag}_pd", [128, 1024], F32))
    ptA = es.enter_context(nc.psum_tensor(f"{tag}_ptA", [128, 1024], BF16))
    ptB = es.enter_context(nc.psum_tensor(f"{tag}_ptB", [128, 1024], BF16))
    q_a = mk.new_q(f"{tag}a", "sync")
    q_wg = mk.new_q(f"{tag}wg", "sync")
    q_wd = mk.new_q(f"{tag}wd", "scalar")
    bacc = B("ffn_acc")
    bh = B("ffn_h")
    bcomb = B("comb")
    mk.dma(q_a, acc[:].rearrange("p (t d) -> p t d", d=2048),
           src_ap.rearrange("(t p) d -> p t d", p=128),
           reads=[src_buf], writes=[bacc])
    for e in range(NEXP):
        row0 = (e // 4) * 8192 + (e % 4) * 2048
        row0d = (e // 4) * 3072 + (e % 4) * 768
        for jb in range(3):
            slot = (e * 3 + jb) % 2
            bwg = B(f"ffn_wg{slot}")
            for half in range(2):
                src = T["gupT_all"][row0:row0 + 2048,
                                    half * 768 + jb * 256: half * 768 + (jb + 1) * 256]
                dst = wg[:, slot * 8192: (slot + 1) * 8192].rearrange(
                    "p (k c) -> p k c", c=512)[:, :, half * 256:(half + 1) * 256]
                mk.dma(q_wg, dst, src.rearrange("(k p) c -> p k c", p=128),
                       reads=[B("gupT_all")], writes=[bwg])
            for tt in range(NT):
                par = tt % 2
                bpg = B(f"ffn_pg{par}")
                for k in range(16):
                    mk.emit("tensor", lambda E, o=pg[:, par * 512:(par + 1) * 512],
                            l=c_lnT_slice(ctx, k, tt),
                            r=wg[:, slot * 8192 + k * 512: slot * 8192 + (k + 1) * 512],
                            s=(k == 0), st=(k == 15):
                            E.matmul(o, l, r, start=s, stop=st),
                            reads=[B(f"lnT{k}"), bwg], writes=[bpg], inc=(k == 15))
                mk.emit("scalar", lambda E, o=st_[:], i=pg[:, par * 512: par * 512 + 256]:
                        E.activation(o, i, AF.Silu, bias=0.0, scale=1.0),
                        reads=[bpg], writes=[B("ffn_silu")])
                mk.emit("vector", lambda E, o=hb[:, tt * 768 + jb * 256: tt * 768 + (jb + 1) * 256],
                        i0=st_[:], r=gat["comb"][:, tt * 32 + e: tt * 32 + e + 1],
                        i1=pg[:, par * 512 + 256: par * 512 + 512]:
                        E.scalar_tensor_tensor(o, i0, r, i1, ALU.mult, ALU.mult),
                        reads=[B("ffn_silu"), bcomb, bpg], writes=[bh])
        for tt in range(NT):
            for jc in range(6):
                pt = ptA if jc % 2 == 0 else ptB
                bpt = B(f"ffn_pt{jc % 2}")
                mk.emit("tensor", lambda E, o=pt[:, 0:128],
                        i=hb[:, tt * 768 + jc * 128: tt * 768 + (jc + 1) * 128],
                        idn=ctx["ident_bf"]:
                        E.transpose(o, i, idn),
                        reads=[bh, B("ident_bf")], writes=[bpt])
                mk.emit("vector", lambda E,
                        o=hb[:, tt * 768 + jc * 128: tt * 768 + (jc + 1) * 128],
                        i=pt[:, 0:128]:
                        E.tensor_copy(o, i), reads=[bpt], writes=[bh])
        for os in range(4):
            slot = (e * 4 + os) % 2
            bwd = B(f"ffn_wd{slot}")
            src = T["dwnT_all"][row0d:row0d + 768, os * 512:(os + 1) * 512]
            mk.dma(q_wd, wd[:, slot * 3072:(slot + 1) * 3072].rearrange(
                   "p (k c) -> p k c", c=512),
                   src.rearrange("(k p) c -> p k c", p=128),
                   reads=[B("dwnT_all")], writes=[bwd])
            for tt in range(NT):
                par = tt % 2
                bpd = B(f"ffn_pd{par}")
                for jc in range(6):
                    mk.emit("tensor", lambda E, o=pd[:, par * 512:(par + 1) * 512],
                            l=hb[:, tt * 768 + jc * 128: tt * 768 + (jc + 1) * 128],
                            r=wd[:, slot * 3072 + jc * 512: slot * 3072 + (jc + 1) * 512],
                            s=(jc == 0), st=(jc == 5):
                            E.matmul(o, l, r, start=s, stop=st),
                            reads=[bh, bwd], writes=[bpd], inc=(jc == 5))
                mk.emit("vector", lambda E,
                        o=acc[:, tt * 2048 + os * 512: tt * 2048 + (os + 1) * 512],
                        i0=acc[:, tt * 2048 + os * 512: tt * 2048 + (os + 1) * 512],
                        i1=pd[:, par * 512:(par + 1) * 512]:
                        E.tensor_add(o, i0, i1),
                        reads=[bacc, bpd], writes=[bacc])
    for tt in range(NT):
        mk.dma(q_store, dst_ap[tt * 128:(tt + 1) * 128, :],
               acc[:, tt * 2048:(tt + 1) * 2048],
               reads=[bacc], writes=[dst_buf])


def _build_phases(ctx, n_sub):
    nc, mk, B, T, alloc, es = (ctx["nc"], ctx["mk"], ctx["B"], ctx["T"],
                               ctx["alloc"], ctx["es"])
    q_store = mk.new_q("store", "scalar")

    ctx["rmsw_sb"] = alloc("rmsw_sb", (128, 128), F32)
    ctx["lamb"] = alloc("lamb", (128, 16), F32)
    mk.dma(ctx["q_const"], ctx["rmsw_sb"][:], T["rms_w_tile"], writes=[B("rmsw_sb")])
    with ExitStack() as sc:
        _prep_lamb(ctx, sc)
        mk.barrier()

    with ExitStack() as s1:
        ctx["cos2_sb"] = alloc("cos2_sb", (128, 8 * 256), F32, s1)
        ctx["sin2_sb"] = alloc("sin2_sb", (128, 8 * 256), F32, s1)
        ctx["c_att"] = alloc("c_att_a", (128, 16 * 1024), BF16, s1)
        mk.dma(ctx["q_const"], ctx["cos2_sb"][:].rearrange("p (t c) -> p t c", c=256),
               T["cos2"].rearrange("(t p) c -> p t c", p=128), writes=[B("cos2_sb")])
        mk.dma(ctx["q_const"], ctx["sin2_sb"][:].rearrange("p (t c) -> p t c", c=256),
               T["sin2"].rearrange("(t p) c -> p t c", p=128), writes=[B("sin2_sb")])
        with ExitStack() as sc:
            _emit_ln(ctx, T["x"], B("x_dram"), "ln1", scope=sc)
            mk.barrier()
        with ExitStack() as sc:
            _phase_kv_a(ctx, sc)
            _phase_stage0(ctx, sc)
            mk.barrier()
        with ExitStack() as sc:
            _phase_att_a(ctx, sc)
            mk.barrier()
        dst1 = T["out"] if n_sub == 1 else T["R1"]
        dbuf1 = B("out_dram") if n_sub == 1 else B("R1")
        with ExitStack() as sc:
            _phase_wo(ctx, sc, T["aWo"], T["x"], B("x_dram"), dst1, dbuf1,
                      q_store, tag="woa")
            mk.barrier()
    if n_sub == 1:
        mk.finalize(B("out_dram"))
        return

    with ExitStack() as s2:
        comb1 = alloc("comb_ln2", (128, NT * 32), F32, s2)
        with ExitStack() as sc:
            gat = _emit_ln(ctx, T["R1"], B("R1"), "ln2", with_gating=True, scope=sc)
            mk.barrier()
        dst2 = T["out"] if n_sub == 2 else T["R2"]
        dbuf2 = B("out_dram") if n_sub == 2 else B("R2")
        with ExitStack() as sc:
            _phase_ffn(ctx, sc, gat, T["R1"], B("R1"), dst2, dbuf2, q_store, "f1")
            mk.barrier()
    if n_sub == 2:
        mk.finalize(B("out_dram"))
        return

    with ExitStack() as sc:
        _emit_ln(ctx, T["R2"], B("R2"), "ln3", scope=sc)
        mk.barrier()
    with ExitStack() as sc:
        qb_all = alloc("qb_all", (128, 16 * 1024), BF16, sc)
        bo_b = alloc("bo_b", (128, 2048), F32, sc)
        ctx["c_att"] = alloc("c_att_b", (128, 16 * 1024), BF16, sc)
        with ExitStack() as sc2:
            _phase_qkv_b(ctx, sc2, qb_all, bo_b)
            mk.barrier()
        with ExitStack() as sc2:
            _phase_att_b(ctx, sc2, qb_all)
            mk.barrier()
        dst3 = T["out"] if n_sub == 3 else T["R3"]
        dbuf3 = B("out_dram") if n_sub == 3 else B("R3")
        with ExitStack() as sc2:
            _phase_wo(ctx, sc2, T["bWo"], T["R2"], B("R2"), dst3, dbuf3,
                      q_store, bias_sb=bo_b, bias_buf=B("bo_b"), tag="wob")
            mk.barrier()
    if n_sub == 3:
        mk.finalize(B("out_dram"))
        return

    with ExitStack() as s4:
        comb4 = alloc("comb_ln4", (128, NT * 32), F32, s4)
        with ExitStack() as sc:
            gat = _emit_ln(ctx, T["R3"], B("R3"), "ln4", with_gating=True, scope=sc)
            mk.barrier()
        with ExitStack() as sc:
            _phase_ffn(ctx, sc, gat, T["R3"], B("R3"), T["out"], B("out_dram"),
                       q_store, "f2")
            mk.barrier()
    mk.finalize(B("out_dram"))


def _prep_lamb(ctx, scope):
    nc, mk, B, T, alloc = ctx["nc"], ctx["mk"], ctx["B"], ctx["T"], ctx["alloc"]
    lt = alloc("lam_lt", (16, 128), F32, scope)
    ls = alloc("lam_ls", (16, 8), F32, scope)
    lrow = alloc("lam_row", (1, 16), F32, scope)
    q_l = mk.new_q("laml", "sync")
    p1 = scope.enter_context(nc.psum_tensor("lam_p", [128, 512], F32))
    bl = B("lam_lt"); bs = B("lam_ls"); bp = B("lam_p"); br = B("lam_row")
    for i, (a_, b_) in enumerate((("lq1", "lk1"), ("lq2", "lk2"))):
        t_a = alloc(f"lam_in{i}a", (16, 128), F32, scope)
        t_b = alloc(f"lam_in{i}b", (16, 128), F32, scope)
        mk.dma(q_l, t_a[:], T[a_], writes=[B(f"lam_in{i}a")])
        mk.dma(q_l, t_b[:], T[b_], writes=[B(f"lam_in{i}b")])
        mk.emit("vector", lambda E, o=lt[:], i0=t_a[:], i1=t_b[:]:
                E.tensor_mul(o, i0, i1),
                reads=[B(f"lam_in{i}a"), B(f"lam_in{i}b")], writes=[bl])
        mk.emit("vector", lambda E, o=ls[:, i:i + 1], i=lt[:]:
                E.reduce_sum(o, i, axis=AX.X), reads=[bl], writes=[bs])
        mk.emit("scalar", lambda E, o=ls[:, 2 + i:3 + i], i=ls[:, i:i + 1]:
                E.activation(o, i, AF.Exp, bias=0.0, scale=1.0),
                reads=[bs], writes=[bs])
    mk.emit("vector", lambda E, o=ls[:, 4:5], i0=ls[:, 2:3], i1=ls[:, 3:4]:
            E.tensor_sub(o, i0, i1), reads=[bs], writes=[bs])
    mk.emit("vector", lambda E, o=ls[:, 5:6], i=ls[:, 4:5]:
            E.tensor_scalar_add(o, i, LAM_INIT), reads=[bs], writes=[bs])
    mk.emit("tensor", lambda E, o=p1[0:1, 0:16], i=ls[0:16, 5:6],
            idn=ctx["ident_f32"][0:16, 0:16]:
            E.transpose(o, i, idn),
            reads=[bs, B("ident_f32")], writes=[bp])
    mk.emit("vector", lambda E, o=lrow[:], i=p1[0:1, 0:16]:
            E.tensor_copy(o, i), reads=[bp], writes=[br])
    mk.emit("tensor", lambda E, o=p1[:, 32:48], l=ctx["ones_row"][:], r=lrow[:]:
            E.matmul(o, l, r, start=True, stop=True),
            reads=[br, B("ones_row")], writes=[bp])
    mk.emit("vector", lambda E, o=ctx["lamb"][:], i=p1[:, 32:48]:
            E.tensor_copy(o, i), reads=[bp], writes=[B("lamb")])


def _phase_qkv_b(ctx, scope, qb_all, bo_b):
    nc, mk, B, T, alloc = ctx["nc"], ctx["mk"], ctx["B"], ctx["T"], ctx["alloc"]
    es = scope
    bia = alloc("b_bia", (16, 128), F32, es)
    biT = alloc("b_biT", (128, 3 * 16), F32, es)
    wtile = alloc("b_wt", (128, 2 * 384), BF16, es)
    kb_t = alloc("b_kt", (128, 1024), BF16, es)
    vb_t = alloc("b_vt", (128, 1024), BF16, es)
    vev = alloc("b_vev", (128, 256), BF16, es)
    bob_row = alloc("b_borow", (1, 2048), F32, es)
    pA = es.enter_context(nc.psum_tensor("b_pA", [128, 1024], F32))
    pB2 = es.enter_context(nc.psum_tensor("b_pB", [128, 1024], F32))
    pC = es.enter_context(nc.psum_tensor("b_pC", [128, 1024], F32))
    ptA = es.enter_context(nc.psum_tensor("b_ptA", [128, 1024], BF16))
    ptB = es.enter_context(nc.psum_tensor("b_ptB", [128, 1024], BF16))
    q_w = mk.new_q("bqkvw", "gpsimd")
    q_b = mk.new_q("bqkvb", "sync")
    q_st = mk.new_q("bqkvst", "scalar")
    psums = [pA, pB2, pC]
    bps = [B("b_pA"), B("b_pB"), B("b_pC")]
    bbiT = B("b_biT")
    for i, nm in enumerate(("bbq", "bbk", "bbv")):
        mk.dma(q_b, bia[:], T[nm], writes=[B("b_bia")])
        mk.emit("tensor", lambda E, o=pA[0:128, 0:16], i=bia[:], idn=ctx["ident_f32"][0:16, 0:16]:
                E.transpose(o, i, idn),
                reads=[B("b_bia"), B("ident_f32")], writes=[bps[0]])
        mk.emit("vector", lambda E, o=biT[:, i * 16:(i + 1) * 16], i=pA[0:128, 0:16]:
                E.tensor_copy(o, i), reads=[bps[0]], writes=[bbiT])
    mk.emit("vector", lambda E, o=biT[:, 0:16], i=biT[:, 0:16]:
            E.tensor_scalar_mul(o, i, SCL_B), reads=[bbiT], writes=[bbiT])
    mk.dma(q_b, bob_row[:], T["bbo"], writes=[B("b_borow")])
    for ns in range(4):
        mk.emit("tensor", lambda E, o=pC[:, 0:512], l=ctx["ones_row"][:],
                r=bob_row[:, ns * 512:(ns + 1) * 512]:
                E.matmul(o, l, r, start=True, stop=True),
                reads=[B("b_borow"), B("ones_row")], writes=[bps[2]])
        mk.emit("vector", lambda E, o=bo_b[:, ns * 512:(ns + 1) * 512], i=pC[:, 0:512]:
                E.tensor_copy(o, i), reads=[bps[2]], writes=[B("bo_b")])
    groups = [list(range(s, min(s + 3, 16))) for s in range(0, 16, 3)]
    for mat, w_nm, bi, out_kind in (("q", "bWq", 0, "q"), ("k", "bWk", 1, "k"),
                                    ("v", "bWv", 2, "v")):
        for grp in groups:
            for k in range(16):
                slot = k % 2
                bw = B(f"b_wt{slot}")
                cols = [m * 128 for m in grp]
                mk.dma(q_w, wtile[:, slot * 384: slot * 384 + 128 * len(grp)],
                       T[w_nm][k * 128:(k + 1) * 128, grp[0] * 128: grp[0] * 128 + 128 * len(grp)],
                       writes=[bw])
                for mi, m in enumerate(grp):
                    for hl in range(2):
                        mk.emit("tensor", lambda E, o=psums[mi][:, hl * 512:(hl + 1) * 512],
                                l=wtile[:, slot * 384 + mi * 128: slot * 384 + (mi + 1) * 128],
                                r=ctx["c_lnT"][:, k * 1024 + hl * 512: k * 1024 + (hl + 1) * 512],
                                s=(k == 0), st=(k == 15):
                                E.matmul(o, l, r, start=s, stop=st),
                                reads=[bw, B(f"lnT{k}")], writes=[bps[mi]],
                                inc=(mi == len(grp) - 1 and hl == 1) or (k == 15))
            for mi, m in enumerate(grp):
                if out_kind == "q":
                    mk.emit("vector", lambda E, o=qb_all[:, m * 1024:(m + 1) * 1024],
                            i=psums[mi][:, 0:1024], bb=biT[:, 0 * 16 + m: 0 * 16 + m + 1]:
                            E.tensor_scalar(o, i, SCL_B, bb, ALU.mult, ALU.add),
                            reads=[bps[mi], bbiT], writes=[B("qb_all")])
                elif out_kind == "k":
                    mk.emit("vector", lambda E, o=kb_t[:],
                            i=psums[mi][:, 0:1024], bb=biT[:, 16 + m: 16 + m + 1]:
                            E.tensor_scalar(o, i, bb, None, ALU.add),
                            reads=[bps[mi], bbiT], writes=[B("b_kt")])
                    mk.dma(q_st, T["kb_loc"][m * 128:(m + 1) * 128, :], kb_t[:],
                           reads=[B("b_kt")], writes=[B("kb_loc")])
                else:
                    mk.emit("vector", lambda E, o=vb_t[:],
                            i=psums[mi][:, 0:1024], bb=biT[:, 32 + m: 32 + m + 1]:
                            E.tensor_scalar(o, i, bb, None, ALU.add),
                            reads=[bps[mi], bbiT], writes=[B("b_vt")])
                    for c in range(8):
                        pt = ptA if c % 2 == 0 else ptB
                        bpt = B(f"b_pt{c % 2}")
                        mk.emit("tensor", lambda E, o=pt[:, 0:128],
                                i=vb_t[:, c * 128:(c + 1) * 128], idn=ctx["ident_bf"]:
                                E.transpose(o, i, idn),
                                reads=[B("b_vt"), B("ident_bf")], writes=[bpt])
                        bvev = B(f"b_vev{c % 2}")
                        mk.emit("vector", lambda E, o=vev[:, (c % 2) * 128:(c % 2) * 128 + 128],
                                i=pt[:, 0:128]:
                                E.tensor_copy(o, i), reads=[bpt], writes=[bvev])
                        mk.dma(q_st, T["vb_loc"][c * 128:(c + 1) * 128, m * 128:(m + 1) * 128],
                               vev[:, (c % 2) * 128:(c % 2) * 128 + 128],
                               reads=[bvev], writes=[B("vb_loc")])
    mk.collective(ctx["q_cc"], "AllGather", ALU.bypass,
                  [T["kb_loc"].opt(), T["vb_loc"].opt()],
                  [T["kb_all"].opt(), T["vb_all"].opt()],
                  reads=[B("kb_loc"), B("vb_loc")],
                  writes=[B("kb_all"), B("vb_all")])


def _phase_att_b(ctx, scope, qb_all):
    nc, mk, B, T, alloc = ctx["nc"], ctx["mk"], ctx["B"], ctx["T"], ctx["alloc"]
    es = scope
    cS_t = alloc("bS", (128, 8192), F32, es)
    cP_t = alloc("bP", (128, 2 * 8192), BF16, es)
    KB = alloc("bK", (128, 8192), BF16, es)
    VB = alloc("bV", (128, 8192), BF16, es)
    pts = alloc("bpts", (128, 1024), BF16, es)
    zst = alloc("bzst", (128, 40), F32, es)
    ae = alloc("bae", (128, 128), F32, es)
    ae2 = alloc("bae2", (128, 128), F32, es)
    aebf = alloc("baebf", (128, 128), BF16, es)
    psc = es.enter_context(nc.psum_tensor("b_psc", [128, 2048], F32))
    ppv = es.enter_context(nc.psum_tensor("b_ppv", [128, 512], F32))
    pep = es.enter_context(nc.psum_tensor("b_pep", [128, 512], F32))
    ptA = es.enter_context(nc.psum_tensor("b2_ptA", [128, 512], BF16))
    ptB = es.enter_context(nc.psum_tensor("b2_ptB", [128, 512], BF16))
    q_kv = mk.new_q("bkv", "sync")
    q_kv2 = mk.new_q("bkv2", "scalar")
    bsc = [B(f"b_psc{i}") for i in range(4)]
    bppv, bpep = B("b_ppv"), B("b_pep")
    bptA, bptB = B("b2_ptA"), B("b2_ptB")
    bS, bP, bKb, bVb = B("bS"), B("bP"), B("bK"), B("bV")
    bpts2, bz = B("bpts"), B("bzst")

    for pair in range(16):
        for r in range(NCORES):
            src = T["kb_all"][r * 2048 + pair * 128: r * 2048 + (pair + 1) * 128, :]
            src = src.rearrange("d (j t) -> d j t", t=128)
            dst = KB[:].rearrange("p (blk t) -> p blk t", t=128)[:, r::8, :]
            mk.dma(q_kv, dst, src, reads=[B("kb_all")], writes=[bKb])
            srcv = T["vb_all"][r * 1024:(r + 1) * 1024, pair * 128:(pair + 1) * 128]
            srcv = srcv.rearrange("(j t) d -> t j d", t=128)
            dstv = VB[:].rearrange("p (blk d) -> p blk d", d=128)[:, r::8, :]
            mk.dma(q_kv2, dstv, srcv, reads=[B("vb_all")], writes=[bVb])
        for j in range(NT):
            for hh in range(2):
                for s in range(16):
                    bk = bsc[s % 4]
                    mk.emit("tensor", lambda E, o=psc[:, (s % 4) * 512:(s % 4) * 512 + 512],
                            l=qb_all[hh * 64:hh * 64 + 64,
                                     pair * 1024 + j * 128: pair * 1024 + (j + 1) * 128],
                            r=KB[hh * 64:hh * 64 + 64, s * 512:(s + 1) * 512]:
                            E.matmul(o, l, r, start=True, stop=True),
                            reads=[B("qb_all"), bKb], writes=[bk])
                    if s % 2 == 1:
                        eng = "vector" if (s // 2) % 2 == 0 else "scalar"
                        if eng == "vector":
                            mk.emit("vector", lambda E,
                                    o=cS_t[:, (s - 1) * 512:(s + 1) * 512],
                                    i=psc[:, ((s - 1) % 4) * 512:((s - 1) % 4) * 512 + 1024]:
                                    E.tensor_copy(o, i),
                                    reads=[bsc[(s - 1) % 4], bk], writes=[bS])
                        else:
                            mk.emit("scalar", lambda E,
                                    o=cS_t[:, (s - 1) * 512:(s + 1) * 512],
                                    i=psc[:, ((s - 1) % 4) * 512:((s - 1) % 4) * 512 + 1024]:
                                    E.activation(o, i, AF.Copy, bias=0.0, scale=1.0),
                                    reads=[bsc[(s - 1) % 4], bk], writes=[bS])
                        mk.emit("vector", lambda E, o=zst[:, 16 + 8 * hh + (s // 2):
                                                          17 + 8 * hh + (s // 2)],
                                i=cS_t[:, (s - 1) * 512:(s + 1) * 512]:
                                E.reduce_max(o, i, axis=AX.X),
                                reads=[bS], writes=[bz])
                mk.emit("vector", lambda E, o=zst[:, hh:hh + 1],
                        i=zst[:, 16 + 8 * hh:24 + 8 * hh]:
                        E.reduce_max(o, i, axis=AX.X, negate=True),
                        reads=[bz], writes=[bz])
                for ec in range(4):
                    mk.emit("scalar", lambda E,
                            o=cP_t[:, hh * 8192 + ec * 2048: hh * 8192 + (ec + 1) * 2048],
                            i=cS_t[:, ec * 2048:(ec + 1) * 2048], m=zst[:, hh:hh + 1],
                            acc=zst[:, 32 + 4 * hh + ec:33 + 4 * hh + ec]:
                            E.activation(o, i, AF.Exp, bias=m, scale=1.0, accum_out=acc),
                            reads=[bS, bz], writes=[bP, bz])
                mk.emit("vector", lambda E, o=zst[:, 2 + hh:3 + hh],
                        i=zst[:, 32 + 4 * hh:36 + 4 * hh]:
                        E.reduce_sum(o, i, axis=AX.X), reads=[bz], writes=[bz])
                mk.emit("vector", lambda E, o=zst[:, 4 + hh:5 + hh], i=zst[:, 2 + hh:3 + hh]:
                        E.reciprocal(o, i), reads=[bz], writes=[bz])
            def _b_ptevac_pv(hh, s):
                par = s % 2
                pt = ptA if par == 0 else ptB
                bpt = bptA if par == 0 else bptB
                peng = "vector" if par == 0 else "scalar"
                if peng == "vector":
                    mk.emit("vector", lambda E, o=pts[:, par * 512:par * 512 + 512],
                            i=pt[:, 0:512]:
                            E.tensor_copy(o, i), reads=[bpt], writes=[bpts2])
                else:
                    mk.emit("scalar", lambda E, o=pts[:, par * 512:par * 512 + 512],
                            i=pt[:, 0:512]:
                            E.activation(o, i, AF.Copy, bias=0.0, scale=1.0),
                            reads=[bpt], writes=[bpts2])
                for c4 in range(4):
                    gblk = s * 4 + c4
                    mk.emit("tensor", lambda E,
                            o=ppv[hh * 64:hh * 64 + 64, 0:128],
                            l=VB[:, gblk * 128 + hh * 64: gblk * 128 + hh * 64 + 64],
                            r=pts[:, par * 512 + c4 * 128: par * 512 + (c4 + 1) * 128],
                            st_=(s == 0 and c4 == 0),
                            sp=(s == 15 and c4 == 3),
                            tp=(0, hh * 64):
                            E.matmul(o, l, r, start=st_, stop=sp, tile_position=tp),
                            reads=[bVb, bpts2], writes=[bppv], inc=(c4 == 3))

            for hh in range(2):
                for s in range(16):
                    pt = ptA if s % 2 == 0 else ptB
                    bpt = bptA if s % 2 == 0 else bptB
                    for c4 in range(4):
                        mk.emit("tensor", lambda E, o=pt[:, c4 * 128:(c4 + 1) * 128],
                                i=cP_t[:, hh * 8192 + s * 512 + c4 * 128:
                                        hh * 8192 + s * 512 + (c4 + 1) * 128],
                                idn=ctx["ident_bf"]:
                                E.transpose(o, i, idn),
                                reads=[bP, B("ident_bf")], writes=[bpt], inc=(c4 == 3))
                    if s > 0:
                        _b_ptevac_pv(hh, s - 1)
                _b_ptevac_pv(hh, 15)
            mk.emit("vector", lambda E, o=ae[:], i=ppv[:, 0:128]:
                    E.tensor_copy(o, i), reads=[bppv], writes=[B("bae")])
            mk.emit("tensor", lambda E, o=pep[:, 0:128], i=ae[:], idn=ctx["ident_f32"][:]:
                    E.transpose(o, i, idn),
                    reads=[B("bae"), B("ident_f32")], writes=[bpep])
            for hh in range(2):
                mk.emit("vector", lambda E, o=ae2[:, hh * 64:(hh + 1) * 64],
                        i=pep[:, hh * 64:(hh + 1) * 64], r=zst[:, 4 + hh:5 + hh]:
                        E.tensor_scalar_mul(o, i, r),
                        reads=[bpep, bz], writes=[B("bae2")])
            mk.emit("vector", lambda E, o=aebf[:], i=ae2[:]:
                    E.tensor_copy(o, i), reads=[B("bae2")], writes=[B("baebf")])
            mk.emit("tensor", lambda E, o=ptA[:, 0:128], i=aebf[:], idn=ctx["ident_bf"]:
                    E.transpose(o, i, idn),
                    reads=[B("baebf"), B("ident_bf")], writes=[bptA])
            mk.emit("vector", lambda E,
                    o=ctx["c_att"][:, pair * 1024 + j * 128: pair * 1024 + (j + 1) * 128],
                    i=ptA[:, 0:128]:
                    E.tensor_copy(o, i), reads=[bptA], writes=[B(f"att{pair}")])



def kernel(**inputs):
    x = np.asarray(inputs["L0_attn_a_x"]).reshape(S, D)
    ident_bf_u16, ident_f32, ones_row = _host_shared_consts()
    rms_w_tile = np.ascontiguousarray(
        np.broadcast_to(np.asarray(inputs["L0_attn_a_rms_w"]), (128, 128)))
    bbq = np.ascontiguousarray(np.asarray(inputs["L0_attn_b_bq"]).reshape(16, 128))
    bbk = np.ascontiguousarray(np.asarray(inputs["L0_attn_b_bk"]).reshape(16, 128))
    bbv = np.ascontiguousarray(np.asarray(inputs["L0_attn_b_bv"]).reshape(16, 128))
    bbo = np.ascontiguousarray(np.asarray(inputs["L0_attn_b_bo"]).reshape(1, D))
    alpha = np.ascontiguousarray(np.asarray(inputs["L0_ffn_layer_alpha"]).reshape(1, 1))

    shared = {
        "aWq": np.ascontiguousarray(inputs["L0_attn_a_Wq"]),
        "aWk": np.ascontiguousarray(inputs["L0_attn_a_Wk"]),
        "aWv": np.ascontiguousarray(inputs["L0_attn_a_Wv"]),
        "aWo": np.ascontiguousarray(inputs["L0_attn_a_Wo"]),
        "lq1": np.ascontiguousarray(inputs["L0_attn_a_lq1"]),
        "lq2": np.ascontiguousarray(inputs["L0_attn_a_lq2"]),
        "lk1": np.ascontiguousarray(inputs["L0_attn_a_lk1"]),
        "lk2": np.ascontiguousarray(inputs["L0_attn_a_lk2"]),
        "rms_w_tile": rms_w_tile,
        "bWq": np.ascontiguousarray(inputs["L0_attn_b_Wq"]),
        "bWk": np.ascontiguousarray(inputs["L0_attn_b_Wk"]),
        "bWv": np.ascontiguousarray(inputs["L0_attn_b_Wv"]),
        "bWo": np.ascontiguousarray(inputs["L0_attn_b_Wo"]),
        "bbq": bbq, "bbk": bbk, "bbv": bbv, "bbo": bbo,
        "gw": np.ascontiguousarray(inputs["L0_ffn_gate_weight"]),
        "alpha": alpha,
        "ident_bf_u16": ident_bf_u16, "ident_f32": ident_f32, "ones_row": ones_row,
    }

    in_maps = []
    idxs = []
    for r in range(NCORES):
        cos2, sin2, tailmask = _host_consts(r)
        idx = _shard_idx(r)
        idxs.append(idx)
        m = dict(shared)
        m["x"] = np.ascontiguousarray(x[idx])
        m["gup_my"] = np.ascontiguousarray(
            np.asarray(inputs["L0_ffn_gate_up_proj"])[4 * r:4 * r + 4])
        m["dwn_my"] = np.ascontiguousarray(
            np.asarray(inputs["L0_ffn_down_proj"])[4 * r:4 * r + 4])
        m["cos2"] = cos2
        m["sin2"] = sin2
        m["tailmask"] = tailmask
        in_maps.append(m)

    nc = build_program(BUILD_SUBLAYERS)
    res = run_bass_kernel_spmd(nc, in_maps, core_ids=list(range(NCORES)))
    full = np.empty((S, D), dtype=np.float32)
    for r in range(NCORES):
        full[idxs[r]] = res.results[r]["out"]
    return full.reshape(1, S, D)
